# version 12
# baseline (speedup 1.0000x reference)
import contextlib
import numpy as np
import ml_dtypes
import concourse.bass as bass
import concourse.mybir as mybir
from concourse.bass_utils import run_bass_kernel_spmd
from concourse.alu_op_type import AluOpType as OP

F32 = mybir.dt.float32
BF16 = mybir.dt.bfloat16
AF = mybir.ActivationFunctionType
AX = mybir.AxisListType
BFD = ml_dtypes.bfloat16

L, DM, DI, NS, DTR = 2048, 1024, 2048, 16, 64
TC = 512
NCH = L // TC
NDT = DI // 128
NKM = DM // 128
H, DH, WIN = 8, 64, 512
HID, HIDP = 2730, 2816
NPAIR = HIDP // 128
NEGBIG = -1.0e30

ENGS = ("sync", "gpsimd", "scalar", "vector", "tensor")

DUMPS = ()
DUMP_LAYER = 0
INTERLEAVE = 4
DBGPOOL = False
LNSER = True
SERIAL = 0


class KB:
    def __init__(self, nc):
        self.nc = nc
        self.q = {e: [] for e in ENGS}
        self.st = contextlib.ExitStack()
        self.semh, self.semc = {}, {}

    def sem(self, *names):
        for n in names:
            self.semh[n] = self.st.enter_context(self.nc.semaphore(n))
            self.semc[n] = 0

    def sb(self, name, shape, dt=F32):
        return self.st.enter_context(self.nc.sbuf_tensor(name, list(shape), dt))

    def pb(self, name, shape=(128, 512)):
        return self.st.enter_context(self.nc.psum_tensor(name, list(shape), F32))

    def wait(self, eng, sem, val):
        if val is None or val <= 0:
            return
        h = self.semh[sem]
        self.q[eng].append(lambda e, h=h, v=val: e.wait_ge(h, v))

    def wait_tok(self, eng, tok):
        if tok is None or tok == 0:
            return
        self.wait(eng, tok[0], tok[1])

    def dma_pool_init(self, pools):
        self._pool = {}
        self._pidx = {}
        for eng, n in pools.items():
            names = [f"DQ_{eng}_{i}" for i in range(n)]
            self.sem(*names)
            self._pool[eng] = names
            self._pidx[eng] = 0

    def ldg(self, eng, pairs):
        names = self._pool[eng]
        sem = names[self._pidx[eng] % len(names)]
        self._pidx[eng] += 1
        self.wait(eng, sem, self.semc[sem])
        v = None
        for dst, src in pairs:
            v = self.emit(eng, lambda e, d=dst, s=src: e.dma_start(
                out=d, in_=s), inc=sem, amt=16)
        return (sem, v)

    def emit(self, eng, fn, inc=None, amt=1):
        if inc is None:
            self.q[eng].append(fn)
            return None
        h = self.semh[inc]
        self.semc[inc] += amt
        v = self.semc[inc]
        self.q[eng].append(lambda e, fn=fn, h=h, a=amt: fn(e).then_inc(h, a))
        return v

    def v(self, sem):
        return self.semc[sem]

    def arena_init(self, nbytes):
        self._ar = self.st.enter_context(
            self.nc.sbuf_tensor("ARENA", [128, nbytes // 4], F32))
        self._arb = self._ar.bitcast(BF16)
        self._free = [(0, nbytes)]
        self._live = {}

    def alloc(self, name, cols, dt=F32, rows=None):
        esz = 4 if dt == F32 else 2
        nb = (cols * esz + 31) & ~31
        self._hw = max(getattr(self, "_hw", 0),
                       sum(v[1] for v in self._live.values()) + nb)
        for i, (off, sz) in enumerate(self._free):
            if sz >= nb:
                self._free[i] = (off + nb, sz - nb)
                if self._free[i][1] == 0:
                    self._free.pop(i)
                self._live[name] = (off, nb)
                base = self._ar if dt == F32 else self._arb
                ap = base[:, off // esz: off // esz + cols]
                if rows is not None:
                    ap = ap[0:rows, :]
                return ap
        raise MemoryError(f"arena OOM for {name}: need {nb}, free={self._free}")

    def bfview(self, name, cols):
        off, nb = self._live[name]
        assert cols * 2 <= nb
        return self._arb[:, off // 2: off // 2 + cols]

    def free(self, *names):
        for name in names:
            off, nb = self._live.pop(name)
            self._free.append((off, nb))
        self._free.sort()
        out = []
        for off, sz in self._free:
            if out and out[-1][0] + out[-1][1] == off:
                out[-1] = (out[-1][0], out[-1][1] + sz)
            else:
                out.append((off, sz))
        self._free = [tuple(x) for x in out]

    def _cvals_init(self):
        if not hasattr(self, "_cvals"):
            self._cvals = self.sb("cvals_sb", [128, 8], F32)
            self.cmap = {}
        return self._cvals[:]

    def const_col(self, val):
        if not hasattr(self, "_cvals"):
            self._cvals = self.sb("cvals_sb", [128, 8], F32)
            self.cmap = {}
        if val not in self.cmap:
            i = len(self.cmap)
            assert i < 8
            self.nc.const_aps.register(F32, float(val), self._cvals[:, i:i + 1])
            self.cmap[val] = i
        return float(val)

    def finish(self):
        with self.st:
            with self.nc.Block() as blk:
                for eng in ENGS:
                    def mk(eng=eng):
                        def body(e):
                            for fn in self.q[eng]:
                                fn(e)
                        return body
                    getattr(blk, eng)(mk())


def _pack_tiles_rows(w):
    return np.ascontiguousarray(w, dtype=np.float32)


def host_prep(inputs):
    g = {}
    W_in = np.asarray(inputs["L0_attn_W_in"], np.float32)
    win_s = np.stack(
        [np.concatenate([W_in[:, dt * 128:(dt + 1) * 128],
                         W_in[:, DI + dt * 128:DI + (dt + 1) * 128]], axis=1)
         for dt in range(NDT)])
    g["win_s"] = np.ascontiguousarray(win_s.reshape(NDT * DM, 256))
    W_out = np.asarray(inputs["L0_attn_W_out"], np.float32)
    wout_s = np.stack([W_out[:, dm * 128:(dm + 1) * 128] for dm in range(NKM)])
    g["wout_s"] = np.ascontiguousarray(wout_s.reshape(NKM * DI, 128))
    W_x = np.asarray(inputs["L0_attn_W_x"], np.float32)
    g["wx_s"] = np.ascontiguousarray(
        W_x.reshape(NDT, 128, 96).transpose(1, 0, 2).reshape(128, NDT * 96))
    g["wdt_s"] = _pack_tiles_rows(inputs["L0_attn_W_dt"])
    g["bdt_c"] = np.ascontiguousarray(
        np.asarray(inputs["L0_attn_b_dt"], np.float32).reshape(NDT, 128).T)
    g["d_c"] = np.ascontiguousarray(
        np.asarray(inputs["L0_attn_D_param"], np.float32).reshape(NDT, 128).T)
    g["wconv_c"] = np.ascontiguousarray(
        np.asarray(inputs["L0_attn_W_conv"], np.float32)
        .reshape(NDT, 128, 3).transpose(1, 0, 2).reshape(128, NDT * 3))
    g["alog_c"] = np.ascontiguousarray(
        np.asarray(inputs["L0_attn_A_log"], np.float32)
        .reshape(NDT, 128, NS).transpose(1, 0, 2).reshape(128, NDT * NS))

    wqkv = np.asarray(inputs["L0_ffn_wqkv"], np.float32)
    for l in range(2):
        g[f"wq_{l}"] = np.ascontiguousarray(wqkv[l][:, 0:512])
        g[f"wk_{l}"] = np.ascontiguousarray(wqkv[l][:, 512:1024])
        g[f"wv_{l}"] = np.ascontiguousarray(wqkv[l][:, 1024:1536])
        wo = np.asarray(inputs["L0_ffn_wo"], np.float32)[l]
        g[f"wo_s_{l}"] = np.ascontiguousarray(
            wo.reshape(H, DH, DM).transpose(1, 0, 2).reshape(DH, H * DM))
        w1 = np.asarray(inputs["L0_ffn_w1"], np.float32)[l]
        w1p = np.zeros((DM, 2 * HIDP), np.float32)
        w1p[:, :HID] = w1[:, :HID]
        w1p[:, HIDP:HIDP + HID] = w1[:, HID:]
        w1_s = np.stack(
            [np.concatenate([w1p[:, j * 128:(j + 1) * 128],
                             w1p[:, HIDP + j * 128:HIDP + (j + 1) * 128]], 1)
             for j in range(NPAIR)])
        g[f"w1_s_{l}"] = np.ascontiguousarray(w1_s.reshape(NPAIR * DM, 256))
        b1 = np.asarray(inputs["L0_ffn_b1"], np.float32)[l]
        b1p = np.zeros(2 * HIDP, np.float32)
        b1p[:HID] = b1[:HID]
        b1p[HIDP:HIDP + HID] = b1[HID:]
        b1c = np.concatenate([b1p[:HIDP].reshape(NPAIR, 128).T,
                              b1p[HIDP:].reshape(NPAIR, 128).T], axis=1)
        g[f"b1c_{l}"] = np.ascontiguousarray(b1c)
        w2 = np.asarray(inputs["L0_ffn_w2"], np.float32)[l]
        w2p = np.zeros((HIDP, DM), np.float32)
        w2p[:HID] = w2
        g[f"w2_s_{l}"] = np.ascontiguousarray(w2p)
        g[f"b2c_{l}"] = np.ascontiguousarray(
            np.asarray(inputs["L0_ffn_b2"], np.float32)[l].reshape(NKM, 128).T)
        g[f"lng_c_{l}"] = np.ascontiguousarray(
            np.asarray(inputs["L0_ffn_ln_g"], np.float32)[l].reshape(NKM, 128).T)
        g[f"lnb_c_{l}"] = np.ascontiguousarray(
            np.asarray(inputs["L0_ffn_ln_b"], np.float32)[l].reshape(NKM, 128).T)
        qs = np.asarray(inputs["L0_ffn_q_scale"], np.float32)[l]
        ks = np.asarray(inputs["L0_ffn_k_scale"], np.float32)[l]
        g[f"qs2_{l}"] = np.ascontiguousarray(np.tile(qs, 2)[:, None])
        g[f"qs2r_{l}"] = np.ascontiguousarray(
            np.tile(np.concatenate([qs[32:], qs[:32]]), 2)[:, None])
        g[f"ks2_{l}"] = np.ascontiguousarray(np.tile(ks, 2)[:, None])
        g[f"ks2r_{l}"] = np.ascontiguousarray(
            np.tile(np.concatenate([ks[32:], ks[:32]]), 2)[:, None])

    half = np.arange(0, DH, 2, dtype=np.float64)
    inv_freq = 1.0 / (10000.0 ** (half / DH))
    t = np.arange(L, dtype=np.float64)
    freqs = t[:, None] * inv_freq[None, :]
    freqs = np.concatenate([freqs, freqs], 1)
    base = (half + 0.4 * DH) / (1.4 * DH)
    power = (t - L // 2) / float(WIN // 2)
    scale = base[None, :] ** power[:, None]
    scale = np.concatenate([scale, scale], 1)
    cos, sin = np.cos(freqs), np.sin(freqs)
    QK = 8.0
    csq = (QK * cos * scale).T.astype(np.float32)
    ssq = (QK * sin * scale).T.astype(np.float32)
    csk = (cos / scale).T.astype(np.float32)
    ssk = (sin / scale).T.astype(np.float32)
    g["tab_csq"] = np.ascontiguousarray(np.tile(csq, (2, 1)))
    g["tab_ssq"] = np.ascontiguousarray(np.tile(ssq, (2, 1)))
    g["tab_csk"] = np.ascontiguousarray(np.tile(csk, (2, 1)))
    g["tab_ssk"] = np.ascontiguousarray(np.tile(ssk, (2, 1)))
    R = np.zeros((DH, DH), np.float32)
    for f in range(32):
        R[f + 32, f] = -1.0
        R[f, f + 32] = 1.0
    g["r2f"] = np.ascontiguousarray(
        np.kron(np.eye(2, dtype=np.float32), R))
    g["i128"] = np.eye(128, dtype=np.float32)
    g["i128bf"] = np.eye(128, dtype=np.float32).astype(BFD)
    ii, jj = np.meshgrid(np.arange(128), np.arange(128), indexing="ij")
    m_near = np.where(jj <= ii, 0.0, NEGBIG).astype(np.float32)
    m_far = np.where(jj >= ii, 0.0, NEGBIG).astype(np.float32)
    g["mask_near_ij"] = m_near
    g["mask_far_ij"] = m_far
    g["maskT_near"] = np.ascontiguousarray(m_near.T)
    g["maskT_far"] = np.ascontiguousarray(m_far.T)
    g["ones_row"] = np.ones((1, 128), np.float32)
    g["ones_row_bf"] = np.ones((1, 128), np.float32).astype(BFD)
    g["ones_col"] = np.ones((128, 1), np.float32)
    g["ones_col_bf"] = np.ones((128, 1), np.float32).astype(BFD)
    bl2 = np.zeros((128, 2), np.float32)
    bl2[0:64, 0] = 1.0
    bl2[64:128, 1] = 1.0
    g["bl2_bf"] = bl2.astype(BFD)
    hs = np.zeros((2, 128), np.float32)
    hs[0, 0:64] = 1.0
    hs[1, 64:128] = 1.0
    g["headsel"] = hs

    x = np.asarray(inputs["L0_attn_x"], np.float32)
    xts = [np.ascontiguousarray(x[c].T) for c in range(x.shape[0])]
    return g, xts



def build_program(dumps=()):
    nc = bass.Bass("TRN2", target_bir_lowering=False)
    b = KB(nc)
    di = {}

    def inp(name, shape, dt=F32):
        di[name] = nc.dram_tensor(name, list(shape), dt, kind="ExternalInput").ap()
        return di[name]

    inp("xT", [DM, L])
    inp("win_s", [NDT * DM, 256]); inp("wout_s", [NKM * DI, 128])
    inp("wx_s", [128, NDT * 96]); inp("wdt_s", [64, DI])
    inp("bdt_c", [128, NDT]); inp("d_c", [128, NDT])
    inp("wconv_c", [128, NDT * 3]); inp("alog_c", [128, NDT * NS])
    for l in range(2):
        inp(f"wq_{l}", [DM, 512]); inp(f"wk_{l}", [DM, 512]); inp(f"wv_{l}", [DM, 512])
        inp(f"wo_s_{l}", [DH, H * DM]); inp(f"w1_s_{l}", [NPAIR * DM, 256])
        inp(f"b1c_{l}", [128, 2 * NPAIR]); inp(f"w2_s_{l}", [HIDP, DM])
        inp(f"b2c_{l}", [128, NKM]); inp(f"lng_c_{l}", [128, NKM]); inp(f"lnb_c_{l}", [128, NKM])
        for nm in ("qs2", "qs2r", "ks2", "ks2r"):
            inp(f"{nm}_{l}", [128, 1])
    for nm in ("tab_csq", "tab_ssq", "tab_csk", "tab_ssk"):
        inp(nm, [128, L])
    inp("r2f", [128, 128]); inp("i128", [128, 128]); inp("i128bf", [128, 128], BF16)
    inp("cvals", [128, 8])
    inp("mask_near_ij", [128, 128]); inp("mask_far_ij", [128, 128])
    inp("maskT_near", [128, 128]); inp("maskT_far", [128, 128])
    inp("ones_row", [1, 128]); inp("ones_row_bf", [1, 128], BF16)
    inp("ones_col", [128, 1]); inp("ones_col_bf", [128, 1], BF16)
    inp("bl2_bf", [128, 2], BF16); inp("headsel", [2, 128])

    out = nc.dram_tensor("out", [DM, L], F32, kind="ExternalOutput").ap()
    dbg = {}
    for name in dumps:
        shape = {"x1": [DM, L], "du": [DI, L], "delta": [DI, L], "u0": [DI, L],
                 "yraw": [DI, L], "y3": [DI, L], "xdbl": [96, L],
                 "su": [DI, L], "qt": [512, L], "kt": [512, L],
                 "ot": [512, L], "xwo": [DM, L], "hln": [DM, L],
                 "hmid": [HIDP, L], "x2l": [DM, L],
                 "qraw": [512, L], "wops": [DM, L], "y2": [DI, L],
                 "ch00": [128, L], "s0": [DM, L], "sx": [DM, L]}[name]
        dbg[name] = nc.dram_tensor("dbg_" + name, shape, F32,
                                   kind="ExternalOutput").ap()

    bc_scr = nc.dram_tensor("bc_scr", [NCH * 32, TC], F32, kind="Internal").ap()
    x1_scr = nc.dram_tensor("x1_scr", [DM, L], F32, kind="Internal").ap()
    x1bf_scr = nc.dram_tensor("x1bf_scr", [DM, 1 + L], BF16, kind="Internal").ap()
    x2_scr = nc.dram_tensor("x2_scr", [DM, 1 + L], F32, kind="Internal").ap()
    x2bf_scr = nc.dram_tensor("x2bf_scr", [DM, 1 + L], BF16, kind="Internal").ap()

    b.sem("DW", "DG", "DR", "DS", "DO", "DO2", "SV", "SS", "ST", "SG", "MKS", "MKG")
    b.dma_pool_init({"sync": 12, "gpsimd": 8})

    bdt = b.sb("bdt", [128, NDT])
    dpar = b.sb("dpar", [128, NDT])
    wconv = b.sb("wconv", [128, NDT * 3])
    hstate = b.sb("hstate", [128, NDT * NS])
    ucarry = b.sb("ucarry", [128, NDT * 2])
    i128 = b.sb("i128s", [128, 128])
    i128bf = b.sb("i128bfs", [128, 128], BF16)
    onesr = b.sb("onesr", [1, 128]); onesrb = b.sb("onesrb", [1, 128], BF16)
    onesc = b.sb("onesc", [128, 1]); onescb = b.sb("onescb", [128, 1], BF16)
    bl2 = b.sb("bl2", [128, 2], BF16)
    headsel = b.sb("headsel_s", [2, 128])
    r2f = b.sb("r2fs", [128, 128])
    mnij = b.sb("mnij", [128, 128]); mfij = b.sb("mfij", [128, 128])
    mtn = b.sb("mtn", [128, 128]); mtf = b.sb("mtf", [128, 128])

    b.arena_init(190 * 1024)
    A = b.alloc
    aneg = A("aneg", NDT * NS)
    wx = A("wx", NDT * 96, BF16)
    wdt = A("wdt", DI, BF16, rows=64)
    xbf = [A(f"xbf{k}", TC, BF16) for k in range(NKM)]
    winb = [A(f"winb{p}", 8 * 256, BF16) for p in range(2)]
    woutb = [A(f"woutb{p}", 16 * 128, BF16) for p in range(2)]
    ubuf = A("ubuf", 2 + TC)
    convA = A("convA", TC); convB = A("convB", TC)
    sudu = [A(f"sudu{i}", TC) for i in range(NDT)]
    delta = [A(f"delta{i}", TC) for i in range(NDT)]
    subf = [A(f"subf{i}", TC, BF16) for i in range(NDT)]
    sres = [A(f"sres{i}", TC, BF16) for i in range(NDT)]
    y2buf = A("y2buf", TC)
    dtbf = A("dtbf", TC, BF16, rows=64)
    bcsb = A("bcsb", TC)
    dA2 = [A(f"dA{p}", TC) for p in range(2)]
    dbu2 = [A(f"dbu{p}", TC) for p in range(2)]
    hsl = A("hsl", TC)
    chsl = [A(f"chsl{p}", TC, BF16) for p in range(2)]
    Bsl = [A(f"Bsl{p}", TC) for p in range(2)]
    Csl = [A(f"Csl{p}", TC) for p in range(2)]
    sqs = b.sb("sqs", [128, TC], BF16)
    rowm = b.sb("rowm", [1, TC]); rowv = b.sb("rowv", [1, TC])
    rowr = b.sb("rowr", [1, TC])
    row2 = b.sb("row2", [2, TC])
    x1bf = b.sb("x1bf", [128, TC], BF16)

    xch = sudu[:NKM]
    pbank = [b.pb(f"pbank{i}") for i in range(8)]

    ld = b.ldg

    prelude_tok = b.ldg("sync", [
        (bdt[:], di["bdt_c"]), (dpar[:], di["d_c"]), (wconv[:], di["wconv_c"]),
        (aneg[:], di["alog_c"]), (i128[:], di["i128"]),
        (onesr[:], di["ones_row"]), (onesrb[:], di["ones_row_bf"]),
        (onesc[:], di["ones_col"]), (onescb[:], di["ones_col_bf"]),
        (bl2[:], di["bl2_bf"]), (headsel[:], di["headsel"]),
        (r2f[:], di["r2f"]), (i128bf[:], di["i128bf"]),
        (mnij[:], di["mask_near_ij"]),
        (mfij[:], di["mask_far_ij"]), (mtn[:], di["maskT_near"]),
        (mtf[:], di["maskT_far"]), (b._cvals_init(), di["cvals"])])
    wxdt_tok = b.ldg("gpsimd", [(wx[:], di["wx_s"]), (wdt[:], di["wdt_s"])])

    b.wait_tok("scalar", prelude_tok)
    an1 = b.emit("scalar", lambda e: e.activation(aneg[:], aneg[:], AF.Exp),
                 inc="SS")
    b.wait("scalar", "SS", an1)
    b.emit("scalar", lambda e: e.activation(aneg[:], aneg[:], AF.Copy,
                                            scale=-1.0), inc="SS")
    b.emit("vector", lambda e: e.memset(hstate[:], 0.0))
    b.emit("vector", lambda e: e.memset(ucarry[:], 0.0))
    b.emit("vector", lambda e: e.memset(ubuf[:], 0.0), inc="SV")

    bar = {"SV": b.v("SV"), "SS": b.v("SS"), "ST": b.v("ST"), "SG": b.v("SG")}
    for eng in ENGS:
        for s in ("SV", "SS", "ST", "SG"):
            b.wait(eng, s, bar[s])

    xch = [dA2[0], dA2[1], dbu2[0], dbu2[1], Bsl[0], Bsl[1], Csl[0], Csl[1]]
    y3b = [b.bfview(f"delta{i}", TC) for i in range(NDT)]

    def new_m1_ctx(c, ydp):
        return {"c": c, "tsl": slice(c * TC, (c + 1) * TC), "ydp": ydp,
                "win_dg": [None] * NDT, "umm": [None] * NDT,
                "resmm": [None] * NDT, "evu": [None] * NDT,
                "res_s": [None] * NDT, "conv": [None] * NDT,
                "xdbl_done": None, "xbf_tok": None, "su_s": [None] * NDT,
                "defer_xdbl": False,
                "du_done": [None] * NDT, "sp": [None] * NDT,
                "bc_tok": None, "bc_s": None, "dtbf_s": None}

    def emit_xbf_load(ctx):
        ctx["xbf_tok"] = b.ldg("gpsimd", [
            (xbf[k][:], di["xT"][k * 128:(k + 1) * 128, ctx["tsl"]])
            for k in range(NKM)])

    def load_win(ctx, dt):
        if dt >= NDT:
            return
        if dt >= 2:
            b.wait("gpsimd", "ST", ctx["resmm"][dt - 2])
        ctx["win_dg"][dt] = b.ldg("gpsimd", [(
            winb[dt % 2][:].rearrange("p (ki cc) -> p ki cc", ki=NKM),
            di["win_s"][dt * DM:(dt + 1) * DM, :]
            .rearrange("(ki p) cc -> p ki cc", p=128))])

    def emit_m1_dt(ctx, dt):
        c = ctx["c"]; tsl = ctx["tsl"]; ydp = ctx["ydp"]
        if SERIAL & 8:
            for _e in ENGS:
                for _s in ("SV", "SS", "ST"):
                    b.wait(_e, _s, b.v(_s))
        wslab = winb[dt % 2]
        b.wait_tok("tensor", ctx["win_dg"][dt])
        if dt == 0:
            b.wait_tok("tensor", ctx["xbf_tok"])
            b.wait_tok("tensor", wxdt_tok)
        if dt >= 1:
            b.wait("tensor", "SS", ctx["evu"][dt - 1])
            b.wait("tensor", "SS", ctx["res_s"][dt - 1])
            b.wait("tensor", "SV", ctx["conv"][dt - 1])
        elif ydp is not None:
            b.wait("tensor", "SV", ydp[8])
            b.wait("tensor", "SV", ydp[9])
        for ki in range(NKM):
            wv = wslab[:, ki * 256:ki * 256 + 128]
            last = ki == NKM - 1
            if last:
                ctx["umm"][dt] = b.emit("tensor", lambda e, wv=wv, ki=ki:
                                        e.matmul(
                    pbank[0][:], wv, xbf[ki][:], start=(ki == 0), stop=True),
                    inc="ST")
            else:
                b.emit("tensor", lambda e, wv=wv, ki=ki: e.matmul(
                    pbank[0][:], wv, xbf[ki][:], start=(ki == 0), stop=False))
        for ki in range(NKM):
            wv = wslab[:, ki * 256 + 128:ki * 256 + 256]
            last = ki == NKM - 1
            if last:
                ctx["resmm"][dt] = b.emit("tensor", lambda e, wv=wv, ki=ki:
                                          e.matmul(
                    pbank[1][:], wv, xbf[ki][:], start=(ki == 0), stop=True),
                    inc="ST")
            else:
                b.emit("tensor", lambda e, wv=wv, ki=ki: e.matmul(
                    pbank[1][:], wv, xbf[ki][:], start=(ki == 0), stop=False))
        load_win(ctx, dt + 2)
        b.wait("scalar", "ST", ctx["umm"][dt])
        if dt == 0 and c == 0:
            b.wait("scalar", "SV", b.v("SV"))
        ctx["evu"][dt] = b.emit("scalar", lambda e: e.activation(
            ubuf[:, 2:2 + TC], pbank[0][:], AF.Copy), inc="SS")
        if "u0" in dbg:
            b.wait("sync", "SS", ctx["evu"][dt])
            dver = b.emit("sync", lambda e, dt=dt, tsl=tsl: e.dma_start(
                out=dbg["u0"][dt * 128:(dt + 1) * 128, tsl],
                in_=ubuf[:, 2:2 + TC]), inc="DO", amt=16)
            b.wait("scalar", "DO", dver)
        b.wait("vector", "SS", ctx["evu"][dt])
        cpin = ucarry[:, dt * 2:dt * 2 + 2]
        cp1 = b.emit("vector", lambda e, cpin=cpin: e.tensor_copy(
            ubuf[:, 0:2], cpin), inc="SV")
        b.wait("vector", "SV", cp1)
        b.emit("vector", lambda e, cpin=cpin: e.tensor_copy(
            cpin, pbank[0][:, TC - 2:TC]))
        w0 = wconv[:, dt * 3:dt * 3 + 1]
        w1c = wconv[:, dt * 3 + 1:dt * 3 + 2]
        w2c = wconv[:, dt * 3 + 2:dt * 3 + 3]
        b.emit("vector", lambda e, w0=w0: e.tensor_scalar(
            convA[:], ubuf[:, 0:TC], w0, None, OP.mult))
        b.emit("vector", lambda e, w1c=w1c: e.scalar_tensor_tensor(
            convB[:], ubuf[:, 1:1 + TC], w1c, convA[:], OP.mult, OP.add))
        conv_v = b.emit("vector", lambda e, w2c=w2c: e.scalar_tensor_tensor(
            convA[:], ubuf[:, 2:2 + TC], w2c, convB[:], OP.mult, OP.add),
            inc="SV")
        ctx["conv"][dt] = conv_v
        b.wait("scalar", "SV", conv_v)
        sud = sudu[dt]; sub = subf[dt]; srb = sres[dt]
        b.emit("scalar", lambda e, sud=sud: e.activation(
            sud[:], convA[:], AF.Silu))
        su_s = b.emit("scalar", lambda e, sub=sub: e.activation(
            sub[:], convA[:], AF.Silu), inc="SS")
        b.wait("scalar", "ST", ctx["resmm"][dt])
        ctx["res_s"][dt] = b.emit("scalar", lambda e, srb=srb: e.activation(
            srb[:], pbank[1][:], AF.Silu), inc="SS")
        ctx["su_s"][dt] = su_s
        if not ctx.get("defer_xdbl"):
            emit_xdbl(ctx, dt)

    def emit_xdbl(ctx, dt):
        ydp = ctx["ydp"]
        b.wait("tensor", "SS", ctx["su_s"][dt])
        if dt == 0 and ydp is not None:
            b.wait("tensor", "SV", ydp[10])
        wxs = wx[:, dt * 96:(dt + 1) * 96]
        sub = subf[dt]
        ctx["xdbl_done"] = b.emit("tensor", lambda e, wxs=wxs, sub=sub, dt=dt:
                                  e.matmul(pbank[2][0:96, :], wxs, sub[:],
                                           start=(dt == 0),
                                           stop=(dt == NDT - 1)),
                                  inc="ST")

    def emit_m2(ctx, wout7):
        c = ctx["c"]; tsl = ctx["tsl"]; ydp = ctx["ydp"]
        b.wait("scalar", "ST", ctx["xdbl_done"])
        ctx["dtbf_s"] = b.emit("scalar", lambda e: e.activation(
            dtbf[:], pbank[2][0:64, :], AF.Copy), inc="SS")
        ctx["bc_s"] = b.emit("scalar", lambda e: e.activation(
            bcsb[64:96, :], pbank[2][64:96, :], AF.Copy), inc="SS")
        b.wait("sync", "SS", ctx["bc_s"])
        ctx["bc_tok"] = b.ldg("sync", [(bc_scr[c * 32:(c + 1) * 32, :],
                                        bcsb[64:96, :])])
        if wout7 is not None:
            b.wait("vector", "ST", wout7)
        for dt in range(NDT):
            b.wait("tensor", "SS", ctx["dtbf_s"])
            if dt >= 1:
                b.wait("tensor", "SS", ctx["sp"][dt - 1])
            elif ydp is not None:
                b.wait("tensor", "SV", ydp[11])
            wds = wdt[:, dt * 128:(dt + 1) * 128]
            dmm = b.emit("tensor", lambda e, wds=wds: e.matmul(
                pbank[3][:], wds, dtbf[:], start=True, stop=True), inc="ST")
            b.wait("scalar", "ST", dmm)
            if dt >= 1:
                b.wait("scalar", "SV", ctx["du_done"][dt - 1])
            bcol = bdt[:, dt:dt + 1]
            b.emit("scalar", lambda e, bcol=bcol: e.activation(
                convA[:], pbank[3][:], AF.Abs, bias=bcol))
            b.emit("scalar", lambda e: e.activation(
                convA[:], convA[:], AF.Exp, scale=-1.0))
            b.emit("scalar", lambda e: e.activation(
                convA[:], convA[:], AF.Ln, bias=1.0))
            ctx["sp"][dt] = b.emit("scalar", lambda e, bcol=bcol: e.activation(
                convB[:], pbank[3][:], AF.Relu, bias=bcol), inc="SS")
            b.wait("vector", "SS", ctx["sp"][dt])
            dl_ = delta[dt]; sud = sudu[dt]
            b.emit("vector", lambda e, dl_=dl_: e.tensor_tensor(
                dl_[:], convA[:], convB[:], OP.add))
            ctx["du_done"][dt] = b.emit("vector", lambda e, dl_=dl_, sud=sud:
                                        e.tensor_tensor(sud[:], dl_[:], sud[:],
                                                        OP.mult), inc="SV")
        if "su" in dbg:
            for dt in range(NDT):
                b.wait("gpsimd", "SS", ctx["res_s"][dt])
                b.emit("gpsimd", lambda e, dt=dt, tsl=tsl: e.dma_start(
                    out=dbg["su"][dt * 128:(dt + 1) * 128, tsl],
                    in_=subf[dt][:]), inc="DO2", amt=16)
        if "xdbl" in dbg:
            b.wait("gpsimd", "SS", ctx["bc_s"])
            b.emit("gpsimd", lambda e, tsl=tsl: e.dma_start(
                out=dbg["xdbl"][0:64, tsl], in_=dtbf[:]), inc="DO2", amt=16)
            b.wait("sync", "SS", ctx["bc_s"])
            b.emit("sync", lambda e, tsl=tsl: e.dma_start(
                out=dbg["xdbl"][64:96, tsl], in_=bcsb[64:96, :]), inc="DO", amt=16)
        if "delta" in dbg:
            for dt in range(NDT):
                b.wait("sync", "SV", ctx["du_done"][dt])
                b.emit("sync", lambda e, dt=dt, tsl=tsl: e.dma_start(
                    out=dbg["delta"][dt * 128:(dt + 1) * 128, tsl],
                    in_=delta[dt][:]), inc="DO", amt=16)
        if "du" in dbg:
            for dt in range(NDT):
                b.wait("sync", "SV", ctx["du_done"][dt])
                b.emit("sync", lambda e, dt=dt, tsl=tsl: e.dma_start(
                    out=dbg["du"][dt * 128:(dt + 1) * 128, tsl],
                    in_=sudu[dt][:]), inc="DO", amt=16)

    def emit_scan(ctx, prev_sc):
        c = ctx["c"]; tsl = ctx["tsl"]
        wout_dg = [None] * NKM
        wout_nop = [None] * NKM

        def load_wout(dm):
            if dm >= NKM:
                return
            if dm >= 2:
                b.wait("gpsimd", "ST", wout_nop[dm - 2])
            elif prev_sc is not None:
                b.wait("gpsimd", "ST", prev_sc["wout_nop"][6 + dm])
            wout_dg[dm] = b.ldg("gpsimd", [(
                woutb[dm % 2][:].rearrange("p (ki cc) -> p ki cc", ki=NDT),
                di["wout_s"][dm * DI:(dm + 1) * DI, :]
                .rearrange("(ki p) cc -> p ki cc", p=128))])
        load_wout(0)
        load_wout(1)
        flush = ctx.get("flush")
        bankfree = ctx.get("bankfree")
        if flush is None:
            b.wait("tensor", "SS", b.v("SS"))
            b.wait("tensor", "SV", b.v("SV"))
            b.wait("scalar", "SV", b.v("SV"))
            b.wait("gpsimd", "SV", b.v("SV"))
        scan_ms = {}
        scan_pe = {}
        repl_ms = {}
        nlast_ms = {}
        y3_done = [None] * NDT
        for half in range(2):
            for n in range(NS):
                hn = half * NS + n
                par = hn % 2
                b.wait_tok("sync", ctx["bc_tok"])
                if hn >= 2:
                    b.wait("sync", "SV", nlast_ms[hn - 2])
                elif flush is not None:
                    b.wait_tok("sync", flush[4 + par])
                    b.wait_tok("sync", flush[6 + par])
                repl_ms[hn] = b.ldg("sync", [
                    (Bsl[par][:],
                     bass.AP(bc_scr.tensor, (c * 32 + n) * TC, [[0, 128], [1, TC]])),
                    (Csl[par][:],
                     bass.AP(bc_scr.tensor, (c * 32 + 16 + n) * TC,
                             [[0, 128], [1, TC]]))])
                for k8 in range(8):
                    dt = half * 8 + k8
                    it = hn * 8 + k8
                    apar = it % 2
                    ipar = it % 2
                    b.wait("scalar", "SV", ctx["du_done"][dt])
                    if it >= 2:
                        b.wait("scalar", "SV", scan_ms.get(it - 2, 0))
                    elif flush is not None:
                        b.wait_tok("scalar", flush[it])
                    acol = aneg[:, dt * NS + n:dt * NS + n + 1]
                    dl_ = delta[dt]
                    da_s = b.emit("scalar", lambda e, apar=apar, acol=acol,
                                  dl_=dl_: e.activation(
                        dA2[apar][:], dl_[:], AF.Exp, scale=acol), inc="SS")
                    sud = sudu[dt]
                    b.wait_tok("vector", repl_ms[hn])
                    if it < 2 and flush is not None:
                        b.wait_tok("vector", flush[2 + it])
                    b.emit("vector", lambda e, sud=sud, par=par, ipar=ipar:
                           e.tensor_tensor(dbu2[ipar][:], sud[:],
                                           Bsl[par][:], OP.mult))
                    b.wait("vector", "SS", da_s)
                    if it >= 2:
                        b.wait("vector", "ST", scan_pe.get(it - 2, 0))
                    hinit = hstate[:, dt * NS + n:dt * NS + n + 1]
                    b.emit("vector", lambda e, apar=apar, hinit=hinit, ipar=ipar:
                           e.tensor_tensor_scan(hsl[:], dA2[apar][:],
                                                dbu2[ipar][:], hinit,
                                                OP.mult, OP.add))
                    chv = b.emit("vector", lambda e, par=par, ipar=ipar:
                                 e.tensor_tensor(chsl[ipar][:], hsl[:],
                                                 Csl[par][:], OP.mult), inc="SV")
                    sv1 = b.emit("vector", lambda e, hinit=hinit: e.tensor_copy(
                        hinit, hsl[:, TC - 1:TC]), inc="SV")
                    scan_ms[it] = sv1
                    if k8 == 7:
                        nlast_ms[hn] = sv1
                    b.wait("tensor", "SV", chv)
                    if n == 0 and half == 0 and bankfree is not None:
                        for sem_, val_ in bankfree[k8]:
                            b.wait("tensor", sem_, val_)
                    pe_ = b.emit("tensor", lambda e, k8=k8, ipar=ipar, n=n:
                                 e.matmul(pbank[k8][:], i128bf[:],
                                          chsl[ipar][:], start=(n == 0),
                                          stop=(n == NS - 1)), inc="ST")
                    scan_pe[it] = pe_
            for k8 in range(8):
                dt = half * 8 + k8
                dcol = dpar[:, dt:dt + 1]
                sub = subf[dt]; srb = sres[dt]; dl3 = y3b[dt]
                b.wait("vector", "ST", scan_pe[(half * NS + NS - 1) * 8 + k8])
                b.emit("vector", lambda e, sub=sub, dcol=dcol, k8=k8:
                       e.scalar_tensor_tensor(y2buf[:], sub[:], dcol,
                                              pbank[k8][:], OP.mult, OP.add))
                if "y2" in dbg:
                    yq = b.emit("vector", lambda e: e.memset(
                        onesr[0:1, 0:1], 1.0), inc="SV")
                    b.wait("sync", "SV", yq)
                    dq = b.ldg("sync", [(dbg["y2"][dt * 128:(dt + 1) * 128, tsl],
                                         y2buf[:])])
                    b.wait_tok("vector", dq)
                yev = b.emit("vector", lambda e, dl3=dl3, srb=srb:
                             e.tensor_tensor(dl3, y2buf[:], srb[:],
                                             OP.mult), inc="SV")
                y3_done[dt] = yev
        if "y3" in dbg:
            for dt in range(NDT):
                b.wait("gpsimd", "SV", y3_done[dt])
                b.emit("gpsimd", lambda e, dt=dt, tsl=tsl: e.dma_start(
                    out=dbg["y3"][dt * 128:(dt + 1) * 128, tsl],
                    in_=y3b[dt]), inc="DO2", amt=16)
        return dict(y3_done=y3_done, nlast=nlast_ms[2 * NS - 1],
                    wout_dg=wout_dg, wout_nop=wout_nop, load_wout=load_wout)

    def emit_m5(ctx, sc, nctx):
        c = ctx["c"]; tsl = ctx["tsl"]
        y3_done = sc["y3_done"]
        wout_dg = sc["wout_dg"]; wout_nop = sc["wout_nop"]
        b.wait("sync", "SV", sc["nlast"])
        xch_tok = b.ldg("sync", [
            (xch[k][:], di["xT"][k * 128:(k + 1) * 128, tsl])
            for k in range(NKM)])
        s_done = [None] * NKM
        b.wait("tensor", "SV", y3_done[NDT - 1])
        for dm in range(NKM):
            if nctx is not None and INTERLEAVE in (1, 2, 3):
                emit_m1_dt(nctx, 2 * dm)
                emit_m1_dt(nctx, 2 * dm + 1)
            b.wait_tok("tensor", wout_dg[dm])
            if DBGPOOL:
                for s_ in b._pool["gpsimd"]:
                    b.wait("tensor", s_, b.v(s_))
            if dm >= 2:
                b.wait("tensor", "SV", s_done[dm - 2])
            bank = pbank[4 + dm % 2]
            wslab = woutb[dm % 2]
            b.wait("tensor", "SV", y3_done[8 + 4 + dm % 2])
            for ki in range(NDT):
                wv = wslab[:, ki * 128:(ki + 1) * 128]
                dl3 = y3b[ki]
                last = ki == NDT - 1
                mm = b.emit("tensor", lambda e, wv=wv, dl3=dl3, ki=ki,
                            bank=bank: e.matmul(bank[:], wv, dl3,
                                                start=(ki == 0),
                                                stop=(ki == NDT - 1)),
                            inc="ST" if last else None)
                if last:
                    wout_nop[dm] = mm
            sc["load_wout"](dm + 2)
            b.wait("vector", "ST", wout_nop[dm])
            b.wait_tok("vector", xch_tok)
            if "sx" in dbg:
                b.wait_tok("sync", xch_tok)
                dsx = b.emit("sync", lambda e, dm=dm, tsl=tsl: e.dma_start(
                    out=dbg["sx"][dm * 128:(dm + 1) * 128, tsl],
                    in_=xch[dm][:]), inc="DO", amt=16)
                b.wait("vector", "DO", dsx)
            xc = xch[dm]
            s_done[dm] = b.emit("vector", lambda e, xc=xc, bank=bank:
                                e.tensor_tensor(xc[:], xc[:], bank[:], OP.add),
                                inc="SV")
        if "s0" in dbg:
            for dm in range(NKM):
                b.wait("sync", "SV", s_done[dm])
                b.emit("sync", lambda e, dm=dm, tsl=tsl: e.dma_start(
                    out=dbg["s0"][dm * 128:(dm + 1) * 128, tsl],
                    in_=xch[dm][:]), inc="DO", amt=16)
        mu_ms = None
        for dm in range(NKM):
            b.wait("tensor", "SV", s_done[dm])
            if dm == 0:
                b.wait("tensor", "SV", y3_done[14])
            xc = xch[dm]
            mu_ms = b.emit("tensor", lambda e, xc=xc, dm=dm: e.matmul(
                pbank[6][0:1, :], onesc[:], xc[:],
                start=(dm == 0), stop=(dm == NKM - 1)),
                inc="ST" if dm == NKM - 1 else None)
        sq_mm = [None] * NKM
        for dm in range(NKM):
            b.wait("scalar", "SV", s_done[dm])
            if dm >= 1:
                b.wait("scalar", "ST", sq_mm[dm - 1])
            xc = xch[dm]
            v = b.emit("scalar", lambda e, xc=xc: e.activation(
                sqs[:], xc[:], AF.Square), inc="SS")
            b.wait("tensor", "SS", v)
            if dm == 0:
                b.wait("tensor", "SV", y3_done[15])
            sq_mm[dm] = b.emit("tensor", lambda e, dm=dm: e.matmul(
                pbank[7][0:1, :], onescb[:], sqs[:],
                start=(dm == 0), stop=(dm == NKM - 1)), inc="ST")
        b.wait("scalar", "ST", mu_ms)
        b.emit("scalar", lambda e: e.activation(
            rowm[:], pbank[6][0:1, :], AF.Copy, scale=1.0 / DM))
        b.emit("scalar", lambda e: e.activation(rowv[:], rowm[:], AF.Square))
        b.wait("scalar", "ST", sq_mm[NKM - 1])
        var_s = b.emit("scalar", lambda e: e.activation(
            rowr[:], pbank[7][0:1, :], AF.Copy, scale=1.0 / DM), inc="SS")
        b.wait("vector", "SS", var_s)
        var_v = b.emit("vector", lambda e: e.tensor_tensor(
            rowv[:], rowr[:], rowv[:], OP.subtract), inc="SV")
        b.wait("scalar", "SV", var_v)
        b.emit("scalar", lambda e, bv=b.const_col(1e-5): e.activation(
            rowv[:], rowv[:], AF.Ln, bias=bv))
        rst_s = b.emit("scalar", lambda e: e.activation(
            rowv[:], rowv[:], AF.Exp, scale=-0.5), inc="SS")
        b.wait("tensor", "SS", rst_s)
        b.emit("tensor", lambda e: e.matmul(
            pbank[4][:], onesr[:], rowm[:], start=True, stop=True))
        bc_mm2 = b.emit("tensor", lambda e: e.matmul(
            pbank[5][:], onesr[:], rowv[:], start=True, stop=True), inc="ST")
        flush = [None] * NKM
        nv_last = None
        for dm in range(NKM):
            b.wait("vector", "ST", bc_mm2)
            xc = xch[dm]
            b.emit("vector", lambda e, xc=xc: e.tensor_tensor(
                xc[:], xc[:], pbank[4][:], OP.subtract))
            nv = b.emit("vector", lambda e, xc=xc: e.tensor_tensor(
                xc[:], xc[:], pbank[5][:], OP.mult), inc="SV")
            nv_last = nv
            cv = b.emit("vector", lambda e, xc=xc: e.tensor_copy(
                x1bf[:], xc[:]), inc="SV")
            b.wait("sync", "SV", cv)
            flush[dm] = b.ldg("sync", [
                (x1_scr[dm * 128:(dm + 1) * 128, tsl], xch[dm][:]),
                (x2_scr[dm * 128:(dm + 1) * 128,
                        1 + c * TC:1 + (c + 1) * TC], xch[dm][:]),
                (x1bf_scr[dm * 128:(dm + 1) * 128, 1 + c * TC:1 + (c + 1) * TC],
                 x1bf[:])])
            if "x1" in dbg:
                b.emit("sync", lambda e, dm=dm, tsl=tsl: e.dma_start(
                    out=dbg["x1"][dm * 128:(dm + 1) * 128, tsl],
                    in_=xch[dm][:]), inc="DO", amt=16)
            b.wait_tok("vector", flush[dm])
        if nctx is not None and INTERLEAVE and nctx["defer_xdbl"]:
            for dt2 in range(NDT):
                emit_xdbl(nctx, dt2)
        if nctx is not None and INTERLEAVE == 4:
            for dt2 in range(NDT):
                emit_m1_dt(nctx, dt2)
        if nctx is not None and INTERLEAVE >= 2:
            emit_m2(nctx, wout_nop[NKM - 1])
        if nctx is not None:
            nctx["flush"] = flush
            nctx["bankfree"] = {
                0: [("SV", nctx["conv"][NDT - 1]), ("SS", nctx["evu"][NDT - 1])],
                1: [("SS", nctx["res_s"][NDT - 1])],
                2: [("SS", nctx["bc_s"])],
                3: [("SS", nctx["sp"][NDT - 1])],
                4: [("SV", nv_last)], 5: [("SV", nv_last)],
                6: [("SS", var_s)], 7: [("SS", var_s)]}
        return flush[NKM - 1]

    ctx = new_m1_ctx(0, None)
    emit_xbf_load(ctx)
    load_win(ctx, 0)
    load_win(ctx, 1)
    for dt in range(NDT):
        emit_m1_dt(ctx, dt)
    emit_m2(ctx, None)
    prev_sc = None
    for c in range(NCH):
        sc = emit_scan(ctx, prev_sc)
        prev_sc = sc
        if c + 1 < NCH:
            nctx = new_m1_ctx(c + 1, sc["y3_done"])
            nctx["defer_xdbl"] = (INTERLEAVE == 3)
            b.wait("gpsimd", "ST", ctx["resmm"][NDT - 1])
            emit_xbf_load(nctx)
            load_win(nctx, 0)
            load_win(nctx, 1)
        else:
            nctx = None
        emit_m5(ctx, sc, nctx)
        if nctx is not None:
            if INTERLEAVE == 0:
                for dt in range(NDT):
                    emit_m1_dt(nctx, dt)
            if INTERLEAVE <= 1:
                emit_m2(nctx, sc["wout_nop"][NKM - 1])
            ctx = nctx

    mamba_ds = b.v("DS")

    def bar_all(dma=True):
        cur = {s: b.v(s) for s in ("SV", "SS", "ST", "SG")}
        if dma:
            b.wait("sync", "DS", b.v("DS"))
            b.wait("sync", "DO", b.v("DO"))
            for s in b._pool["sync"]:
                b.wait("sync", s, b.v(s))
            mks = b.emit("sync", lambda e: e.nop(), inc="MKS")
            for s in b._pool["gpsimd"]:
                b.wait("gpsimd", s, b.v(s))
            b.wait("gpsimd", "DO2", b.v("DO2"))
            mkg = b.emit("gpsimd", lambda e: e.nop(), inc="MKG")
            for eng in ENGS:
                b.wait(eng, "MKS", mks)
                b.wait(eng, "MKG", mkg)
        for eng in ENGS:
            for s, v in cur.items():
                b.wait(eng, s, v)

    b.free("aneg", "wx", "wdt")
    b.free(*[f"xbf{k}" for k in range(NKM)], "winb0", "winb1", "woutb0",
           "woutb1", "ubuf", "convA", "convB",
           *[f"sudu{i}" for i in range(NDT)], *[f"delta{i}" for i in range(NDT)],
           *[f"subf{i}" for i in range(NDT)], *[f"sres{i}" for i in range(NDT)],
           "y2buf", "dtbf", "bcsb", "dA0", "dA1",
           "dbu0", "dbu1", "hsl", "chsl0", "chsl1", "Bsl0", "Bsl1", "Csl0", "Csl1")
    bar_all()

    x2c = [A(f"x2c{dm}", 1 + TC) for dm in range(NKM)]
    onesP = b.sb("onesP", [128, 128])
    zcol = b.sb("zcol", [128, 1])

    b.emit("vector", lambda e: e.memset(onesP[:], 1.0), inc="SV")
    b.emit("vector", lambda e: e.memset(zcol[:], 0.0))
    mz = b.emit("vector", lambda e: e.memset(x1bf[:, 0:1], 0.0), inc="SV")
    b.wait("sync", "SV", mz)
    for dm in range(NKM):
        b.emit("sync", lambda e, dm=dm: e.dma_start(
            out=x1bf_scr[dm * 128:(dm + 1) * 128, 0:1], in_=x1bf[:, 0:1],
            allow_slow_non_contiguous=True), inc="DS", amt=16)
        b.emit("sync", lambda e, dm=dm: e.dma_start(
            out=x2bf_scr[dm * 128:(dm + 1) * 128, 0:1], in_=x1bf[:, 0:1],
            allow_slow_non_contiguous=True), inc="DS", amt=16)
        b.emit("sync", lambda e, dm=dm: e.dma_start(
            out=x2_scr[dm * 128:(dm + 1) * 128, 0:1], in_=zcol[:],
            allow_slow_non_contiguous=True), inc="DS", amt=16)
    b.wait("sync", "DS", mamba_ds)

    lnst = {"ss": 0, "sv": 0}

    def ln_stats_norm(s_views, out_emit, tag):
        b.wait("tensor", "SS", lnst["ss"])
        b.wait("tensor", "SV", lnst["sv"])
        mu_ms = None
        for dm in range(NKM):
            sv_ = s_views[dm]
            mu_ms = b.emit("tensor", lambda e, sv_=sv_, dm=dm: e.matmul(
                pbank[5][0:1, :], onesc[:], sv_,
                start=(dm == 0), stop=(dm == NKM - 1)),
                inc="ST" if dm == NKM - 1 else None)
        sq_mm = None
        for dm in range(NKM):
            sv_ = s_views[dm]
            if dm >= 1:
                b.wait("scalar", "ST", sq_mm)
            v = b.emit("scalar", lambda e, sv_=sv_: e.activation(
                sqs[:], sv_, AF.Square), inc="SS")
            b.wait("tensor", "SS", v)
            sq_mm = b.emit("tensor", lambda e, dm=dm: e.matmul(
                pbank[6][0:1, :], onescb[:], sqs[:],
                start=(dm == 0), stop=(dm == NKM - 1)), inc="ST")
        b.wait("scalar", "ST", mu_ms)
        b.emit("scalar", lambda e: e.activation(
            rowm[:], pbank[5][0:1, :], AF.Copy, scale=1.0 / DM))
        b.emit("scalar", lambda e: e.activation(rowv[:], rowm[:], AF.Square))
        b.wait("scalar", "ST", sq_mm)
        var_s = b.emit("scalar", lambda e: e.activation(
            rowr[:], pbank[6][0:1, :], AF.Copy, scale=1.0 / DM), inc="SS")
        b.wait("vector", "SS", var_s)
        var_v = b.emit("vector", lambda e: e.tensor_tensor(
            rowv[:], rowr[:], rowv[:], OP.subtract), inc="SV")
        b.wait("scalar", "SV", var_v)
        b.emit("scalar", lambda e, bv=b.const_col(1e-5): e.activation(
            rowv[:], rowv[:], AF.Ln, bias=bv))
        rst_s = b.emit("scalar", lambda e: e.activation(
            rowv[:], rowv[:], AF.Exp, scale=-0.5), inc="SS")
        b.wait("tensor", "SS", rst_s)
        b.emit("tensor", lambda e: e.matmul(
            pbank[7][:], onesr[:], rowm[:], start=True, stop=True))
        bc_mm2 = b.emit("tensor", lambda e: e.matmul(
            pbank[2][:], onesr[:], rowv[:], start=True, stop=True), inc="ST")
        for dm in range(NKM):
            b.wait("vector", "ST", bc_mm2)
            out_emit(dm, pbank[7][:], pbank[2][:])
        lnst["ss"] = b.v("SS")
        lnst["sv"] = b.v("SV")

    def ln_pipeline(groups, serial=False):
        n = len(groups)
        mu_l = [None] * n
        sq_l = [None] * n
        var_l = [None] * n
        rst_l = [None] * n
        bc2_l = [None] * n
        nv_l = [None] * n

        def stats(i):
            svs, _, prew = groups[i]
            if callable(prew):
                prew()
            else:
                for eng, sem, val in prew:
                    b.wait(eng, sem, val)
            if i >= 1:
                b.wait("tensor", "SS", var_l[i - 1])
            else:
                b.wait("tensor", "SS", lnst["ss"])
                b.wait("tensor", "SV", lnst["sv"])
            mu = None
            for dm in range(NKM):
                sv_ = svs[dm]
                mu = b.emit("tensor", lambda e, sv_=sv_, dm=dm: e.matmul(
                    pbank[5][0:1, :], onesc[:], sv_,
                    start=(dm == 0), stop=(dm == NKM - 1)),
                    inc="ST" if dm == NKM - 1 else None)
            mu_l[i] = mu
            sq = None
            for dm in range(NKM):
                sv_ = svs[dm]
                if dm >= 1:
                    b.wait("scalar", "ST", sq)
                elif i >= 1:
                    b.wait("scalar", "ST", sq_l[i - 1])
                v = b.emit("scalar", lambda e, sv_=sv_: e.activation(
                    sqs[:], sv_, AF.Square), inc="SS")
                b.wait("tensor", "SS", v)
                sq = b.emit("tensor", lambda e, dm=dm: e.matmul(
                    pbank[6][0:1, :], onescb[:], sqs[:],
                    start=(dm == 0), stop=(dm == NKM - 1)), inc="ST")
            sq_l[i] = sq

        def chain(i):
            b.wait("scalar", "ST", mu_l[i])
            if i >= 1:
                b.wait("scalar", "ST", bc2_l[i - 1])
            b.emit("scalar", lambda e: e.activation(
                rowm[:], pbank[5][0:1, :], AF.Copy, scale=1.0 / DM))
            b.emit("scalar", lambda e: e.activation(rowv[:], rowm[:], AF.Square))
            b.wait("scalar", "ST", sq_l[i])
            var_l[i] = b.emit("scalar", lambda e: e.activation(
                rowr[:], pbank[6][0:1, :], AF.Copy, scale=1.0 / DM), inc="SS")
            b.wait("vector", "SS", var_l[i])
            vv = b.emit("vector", lambda e: e.tensor_tensor(
                rowv[:], rowr[:], rowv[:], OP.subtract), inc="SV")
            b.wait("scalar", "SV", vv)
            b.emit("scalar", lambda e, bv=b.const_col(1e-5): e.activation(
                rowv[:], rowv[:], AF.Ln, bias=bv))
            rst_l[i] = b.emit("scalar", lambda e: e.activation(
                rowv[:], rowv[:], AF.Exp, scale=-0.5), inc="SS")

        def bcast_norm(i):
            _, out_emit, _ = groups[i]
            b.wait("tensor", "SS", rst_l[i])
            if i >= 1:
                b.wait("tensor", "SV", nv_l[i - 1])
            b.emit("tensor", lambda e: e.matmul(
                pbank[7][:], onesr[:], rowm[:], start=True, stop=True))
            bc2_l[i] = b.emit("tensor", lambda e: e.matmul(
                pbank[2][:], onesr[:], rowv[:], start=True, stop=True), inc="ST")
            for dm in range(NKM):
                b.wait("vector", "ST", bc2_l[i])
                out_emit(dm, pbank[7][:], pbank[2][:])
            nv_l[i] = b.v("SV")

        if serial:
            for i in range(n):
                stats(i)
                chain(i)
                bcast_norm(i)
        else:
            for i in range(n):
                stats(i)
                if i >= 1:
                    bcast_norm(i - 1)
                chain(i)
            bcast_norm(n - 1)
        lnst["ss"] = b.v("SS")
        lnst["sv"] = b.v("SV")

    def emit_ffn_layer(l):
        lw = b.v("DW")
        wqb = A("wqb", 8 * 512, BF16)
        wkb = A("wkb", 8 * 512, BF16)
        wvb = A("wvb", 8 * 512, BF16)
        tbA = A("tbA", L); tbB = A("tbB", L)
        qt = [A(f"qt{i}", L, BF16) for i in range(4)]
        kt = [A(f"kt{i}", L, BF16) for i in range(4)]
        vt = [A(f"vt{i}", 8 * 65, BF16) for i in range(16)]
        xin = [A(f"xin{k}", 1 + TC, BF16) for k in range(NKM)]
        qtmp = A("qtmp", TC); rq1 = A("rq1", TC); rq2 = A("rq2", TC)
        bar_all()
        qkv_tok = b.ldg("gpsimd", [
            (dst.rearrange("p (ki cc) -> p ki cc", ki=NKM),
             di[src_].rearrange("(ki p) cc -> p ki cc", p=128))
            for dst, src_ in ((wqb, f"wq_{l}"), (wkb, f"wk_{l}"), (wvb, f"wv_{l}"))])
        for dm in range(16):
            b.emit("vector", lambda e, dm=dm: e.memset(
                vt[dm][:].rearrange("p (h c) -> p h c", h=H)[:, :, 64:65], 1.0))
        vones_v = b.emit("vector", lambda e: e.memset(onesP[0:1, 0:1], 1.0), inc="SV")

        xsrc = x1bf_scr if l == 0 else x2bf_scr

        def qk_build(which, wslab, tgt, tb_src_a, tb_src_b, sc2, sc2r):
            b.wait("sync", "SV", b.v("SV"))
            b.wait("sync", "SS", b.v("SS"))
            tb_tok = b.ldg("sync", [(tbA[:], di[tb_src_a]),
                                    (tbB[:], di[tb_src_b])])
            b.wait_tok("scalar", tb_tok)
            b.emit("scalar", lambda e: e.activation(
                tbA[:], tbA[:], AF.Copy, scale=di_sc[sc2]))
            fold_s = b.emit("scalar", lambda e: e.activation(
                tbB[:], tbB[:], AF.Copy, scale=di_sc[sc2r]), inc="SS")
            for fc in range(NCH):
                t0, t1 = fc * TC, (fc + 1) * TC
                b.wait("sync", "ST", b.v("ST"))
                b.wait("sync", "DS", b.v("DS"))
                xin_tok = b.ldg("sync", [
                    (xin[k][:],
                     xsrc[k * 128:(k + 1) * 128, fc * TC:fc * TC + 1 + TC])
                    for k in range(NKM)])
                for fh in range(4):
                    bank = pbank[fh % 2]
                    b.wait_tok("tensor", qkv_tok)
                    b.wait_tok("tensor", xin_tok)
                    g = (qk_ev.get((which, fc, fh - 2))
                         or qk_ev.get((which, fc - 1, fh + 2))
                         or (qk_ev.get(("q", NCH - 1, fh + 2)) if which == "k"
                             else None) or 0)
                    b.wait("tensor", "SV", g)
                    if SERIAL & 1:
                        for _e in ENGS:
                            for _s in ("SV", "SS", "ST"):
                                b.wait(_e, _s, b.v(_s))
                    mlast = None
                    for ki in range(NKM):
                        wv_ = wslab[:, ki * 512 + fh * 128:ki * 512 + fh * 128 + 128]
                        rhs = xin[ki][:, 1:1 + TC] if ki < 4 else xin[ki][:, 0:TC]
                        mlast = b.emit("tensor", lambda e, wv_=wv_, rhs=rhs, ki=ki,
                                       bank=bank: e.matmul(
                            bank[:], wv_, rhs, start=(ki == 0), stop=(ki == NKM - 1)),
                            inc="ST" if ki == NKM - 1 else None)
                    b.wait("scalar", "ST", mlast)
                    ev = b.emit("scalar", lambda e, bank=bank: e.activation(
                        qtmp[:], bank[:], AF.Copy), inc="SS")
                    if l == DUMP_LAYER and which == "q" and "qraw" in dbg:
                        b.wait("sync", "SS", ev)
                        dq = b.emit("sync", lambda e, fh=fh, fc=fc: e.dma_start(
                            out=dbg["qraw"][fh * 128:(fh + 1) * 128,
                                            fc * TC:(fc + 1) * TC],
                            in_=qtmp[:]), inc="DO", amt=16)
                        b.wait("scalar", "DO", dq)
                        b.wait("vector", "DO", dq)
                    b.wait("tensor", "SS", ev)
                    rot_mm = b.emit("tensor", lambda e: e.matmul(
                        pbank[2][:], r2f[:], qtmp[:], start=True, stop=True),
                        inc="ST")
                    sq_s = b.emit("scalar", lambda e: e.activation(
                        sqs[:], qtmp[:], AF.Square), inc="SS")
                    b.wait("tensor", "SS", sq_s)
                    ss_mm = b.emit("tensor", lambda e: e.matmul(
                        pbank[3][0:2, :], bl2[:], sqs[:], start=True, stop=True),
                        inc="ST")
                    b.wait("scalar", "ST", ss_mm)
                    b.emit("scalar", lambda e, bv=b.const_col(1e-24): e.activation(
                        row2[:], pbank[3][0:2, :], AF.Ln, bias=bv))
                    rb_s = b.emit("scalar", lambda e: e.activation(
                        row2[:], row2[:], AF.Exp, scale=-0.5), inc="SS")
                    b.wait("tensor", "SS", rb_s)
                    rb_mm = b.emit("tensor", lambda e: e.matmul(
                        pbank[4][:], headsel[:], row2[:], start=True, stop=True),
                        inc="ST")
                    b.wait("vector", "SS", fold_s)
                    b.wait("vector", "ST", rb_mm)
                    tga = tgt[fh]
                    b.emit("vector", lambda e, t0=t0, t1=t1: e.tensor_tensor(
                        rq1[:], qtmp[:], tbA[:, t0:t1], OP.mult))
                    b.emit("vector", lambda e, t0=t0, t1=t1: e.tensor_tensor(
                        rq2[:], pbank[2][:], tbB[:, t0:t1], OP.mult))
                    b.emit("vector", lambda e: e.tensor_tensor(
                        rq1[:], rq1[:], rq2[:], OP.add))
                    qk_ev[(which, fc, fh)] = b.emit(
                        "vector", lambda e, tga=tga, t0=t0, t1=t1: e.tensor_tensor(
                            tga[:, t0:t1], rq1[:], pbank[4][:], OP.mult), inc="SV")
                    b.wait("tensor", "SV", qk_ev[(which, fc, fh)])
                    b.wait("scalar", "SV", qk_ev[(which, fc, fh)])

        qk_ev = {}
        qscol = b.sb(f"qscol{l}", [128, 4])
        sc_tok = b.ldg("sync", [
            (qscol[:, 0:1], di[f"qs2_{l}"]), (qscol[:, 1:2], di[f"qs2r_{l}"]),
            (qscol[:, 2:3], di[f"ks2_{l}"]), (qscol[:, 3:4], di[f"ks2r_{l}"])])
        di_sc = {f"qs2_{l}": qscol[:, 0:1], f"qs2r_{l}": qscol[:, 1:2],
                 f"ks2_{l}": qscol[:, 2:3], f"ks2r_{l}": qscol[:, 3:4]}
        b.wait_tok("scalar", sc_tok)
        qk_build("q", wqb, qt, "tab_csq", "tab_ssq", f"qs2_{l}", f"qs2r_{l}")
        if l == DUMP_LAYER and "qt" in dbg:
            for i4 in range(4):
                b.wait("gpsimd", "SV", b.v("SV"))
                b.emit("gpsimd", lambda e, i4=i4: e.dma_start(
                    out=dbg["qt"][i4 * 128:(i4 + 1) * 128, :], in_=qt[i4][:]),
                    inc="DO2", amt=16)
        qk_build("k", wkb, kt, "tab_csk", "tab_ssk", f"ks2_{l}", f"ks2r_{l}")
        if l == DUMP_LAYER and "kt" in dbg:
            for i4 in range(4):
                b.wait("gpsimd", "SV", b.v("SV"))
                b.emit("gpsimd", lambda e, i4=i4: e.dma_start(
                    out=dbg["kt"][i4 * 128:(i4 + 1) * 128, :], in_=kt[i4][:]),
                    inc="DO2", amt=16)
        v_ev = {}
        for fc in range(NCH):
            b.wait("sync", "ST", b.v("ST"))
            b.wait("sync", "DS", b.v("DS"))
            xin_tok = b.ldg("sync", [
                (xin[k][:],
                 xsrc[k * 128:(k + 1) * 128, fc * TC:fc * TC + 1 + TC])
                for k in range(NKM)])
            for jt in range(4):
                tix = fc * 4 + jt
                bank = pbank[tix % 2]
                b.wait_tok("tensor", xin_tok)
                b.wait("tensor", "SS", v_ev.get(tix - 2, 0))
                if tix < 2:
                    b.wait("tensor", "SV", qk_ev.get(("k", NCH - 1, 2 + tix), 0))
                mlast = None
                for ki in range(NKM):
                    lhs = (xin[ki][:, 1 + jt * 128:1 + (jt + 1) * 128] if ki < 4
                           else xin[ki][:, jt * 128:(jt + 1) * 128])
                    rhs = wvb[:, ki * 512:(ki + 1) * 512]
                    mlast = b.emit("tensor", lambda e, lhs=lhs, rhs=rhs, ki=ki,
                                   bank=bank: e.matmul(
                        bank[:], lhs, rhs, start=(ki == 0), stop=(ki == NKM - 1)),
                        inc="ST" if ki == NKM - 1 else None)
                b.wait("scalar", "ST", mlast)
                b.wait("scalar", "SV", vones_v)
                vtile = vt[tix]
                v_ev[tix] = b.emit("scalar", lambda e, vtile=vtile, bank=bank:
                                   e.activation(
                    vtile[:].rearrange("p (h c) -> p h c", h=H)[:, :, 0:64],
                    bank[:].rearrange("p (h c) -> p h c", h=H), AF.Copy),
                    inc="SS")
        bar_all(dma=False)
        b.free("wqb", "wkb", "wvb", "tbA", "tbB",
               *[f"xin{k}" for k in range(NKM)], "qtmp", "rq1", "rq2")

        ptsl = [A(f"ptsl{j}", 1536, BF16) for j in range(16)]
        for j in range(16):
            b.emit("vector", lambda e, j=j: e.memset(ptsl[j][:], 0.0))
        b.emit("vector", lambda e: e.memset(onesP[0:1, 0:1], 1.0), inc="SV")
        otb = [A(f"ot{h}", L, BF16, rows=64) for h in range(H)]
        pij = [A(f"pij{p}", 640, BF16) for p in range(2)]
        oscr = A("oscr", TC)
        den = A("den", TC)
        nmAB = b.sb(f"nmAB{l}", [128, 4])
        bar_all(dma=False)
        exp_ms = {}
        red_ms = {}
        m1_ms = {}
        tr_ms = {}
        ev_ms = {}
        avmm_ms = {}
        avml_ms = {}

        def geo(it):
            I0 = it * 128
            jlo = max(0, I0 - 512)
            return I0, jlo, I0 + 128 - jlo

        def sbanks(it):
            return (pbank[0], pbank[1]) if it % 2 == 0 else (pbank[2], pbank[3])

        def emit_sim(h, it):
            I0, jlo, w = geo(it)
            par = it % 2
            bl_, bh_ = sbanks(it)
            qsl = qt[h // 2]
            ksl = kt[h // 2]
            fhrow = (h % 2) * 64
            if it >= 2:
                b.wait("tensor", "SS", exp_ms[(h, it - 2)])
            elif h > 0:
                b.wait("tensor", "SS", exp_ms[(h - 1, 14 + par)])
            lhsT = qsl[fhrow:fhrow + 64, I0:I0 + 128]
            m1 = b.emit("tensor", lambda e, lhsT=lhsT, ksl=ksl, jlo=jlo, w=w,
                        fhrow=fhrow, bl_=bl_: e.matmul(
                bl_[:, 0:min(w, 512)], lhsT,
                ksl[fhrow:fhrow + 64, jlo:jlo + min(w, 512)],
                start=True, stop=True), inc="ST")
            if w > 512:
                m1 = b.emit("tensor", lambda e, lhsT=lhsT, ksl=ksl, jlo=jlo, w=w,
                            fhrow=fhrow, bh_=bh_: e.matmul(
                    bh_[:, 0:w - 512], lhsT,
                    ksl[fhrow:fhrow + 64, jlo + 512:jlo + w],
                    start=True, stop=True), inc="ST")
            m1_ms[(h, it)] = m1

        def emit_rowmax(h, it):
            I0, jlo, w = geo(it)
            par = it % 2
            bl_, bh_ = sbanks(it)
            nmA = nmAB[:, 2 * par:2 * par + 1]
            nmB = nmAB[:, 2 * par + 1:2 * par + 2]
            b.wait("vector", "ST", m1_ms[(h, it)])
            if I0 >= 512:
                b.emit("vector", lambda e, bl_=bl_: e.tensor_tensor(
                    bl_[:, 0:128], bl_[:, 0:128], mfij[:], OP.add))
            if w > 512:
                b.emit("vector", lambda e, bh_=bh_, w=w: e.tensor_tensor(
                    bh_[:, w - 512 - 128:w - 512], bh_[:, w - 512 - 128:w - 512],
                    mnij[:], OP.add))
                b.emit("vector", lambda e, bl_=bl_, nmA=nmA: e.tensor_reduce(
                    nmA, bl_[:, 0:512], AX.X, OP.max, negate=True))
                b.emit("vector", lambda e, bh_=bh_, w=w, nmB=nmB: e.tensor_reduce(
                    nmB, bh_[:, 0:w - 512], AX.X, OP.max, negate=True))
                red = b.emit("vector", lambda e, nmA=nmA, nmB=nmB: e.tensor_tensor(
                    nmA, nmA, nmB, OP.min), inc="SV")
            else:
                b.emit("vector", lambda e, bl_=bl_, w=w: e.tensor_tensor(
                    bl_[:, w - 128:w], bl_[:, w - 128:w], mnij[:], OP.add))
                red = b.emit("vector", lambda e, bl_=bl_, w=w, nmA=nmA:
                             e.tensor_reduce(
                    nmA, bl_[:, 0:w], AX.X, OP.max, negate=True), inc="SV")
            red_ms[(h, it)] = red

        def emit_exp(h, it):
            I0, jlo, w = geo(it)
            par = it % 2
            bl_, bh_ = sbanks(it)
            nmA = nmAB[:, 2 * par:2 * par + 1]
            pj = pij[par]
            b.wait("scalar", "SV", red_ms[(h, it)])
            if it >= 2:
                b.wait("scalar", "ST", tr_ms[(h, it - 2)])
            elif h > 0:
                b.wait("scalar", "ST", tr_ms[(h - 1, 14 + par)])
            if w > 512:
                b.emit("scalar", lambda e, pj=pj, bh_=bh_, w=w, nmA=nmA:
                       e.activation(pj[:, 512:w], bh_[:, 0:w - 512], AF.Exp,
                                    bias=nmA), inc="SS")
            ex = b.emit("scalar", lambda e, pj=pj, bl_=bl_, w=w, nmA=nmA:
                        e.activation(pj[:, 0:min(w, 512)], bl_[:, 0:min(w, 512)],
                                     AF.Exp, bias=nmA), inc="SS")
            exp_ms[(h, it)] = ex

        def emit_trans(h, it):
            I0, jlo, w = geo(it)
            par = it % 2
            tb = pbank[4] if par == 0 else pbank[6]
            pj = pij[par]
            b.wait("tensor", "SS", exp_ms[(h, it)])
            if it >= 2:
                b.wait("tensor", "SS", ev_ms[(h, it - 2)])
            elif h > 0:
                b.wait("tensor", "SS", ev_ms[(h - 1, 14 + par)])
            ntile = w // 128
            tr = None
            for k in range(ntile):
                dst = (tb[:, k * 128:(k + 1) * 128] if k < 4
                       else pbank[5][:, par * 128:par * 128 + 128])
                tr = b.emit("tensor", lambda e, dst=dst, pj=pj, k=k: e.matmul(
                    dst, pj[:, k * 128:(k + 1) * 128], i128bf[:],
                    start=True, stop=True),
                    inc="ST" if k == ntile - 1 else None)
            tr_ms[(h, it)] = tr

        def emit_evac(h, it):
            I0, jlo, w = geo(it)
            par = it % 2
            tb = pbank[4] if par == 0 else pbank[6]
            b.wait("scalar", "ST", tr_ms[(h, it)])
            if h > 0 and it in (0, 4, 8):
                b.wait("scalar", "ST", avmm_ms[(h - 1, 1 + it // 4)])
            ntile = w // 128
            ev = None
            for k in range(ntile):
                J = jlo // 128 + k
                src = (tb[:, k * 128:(k + 1) * 128] if k < 4
                       else pbank[5][:, par * 128:par * 128 + 128])
                c = 512 + I0 - J * 128
                ptile = ptsl[J]
                ev = b.emit("scalar", lambda e, ptile=ptile, src=src, c=c:
                            e.activation(ptile[:, c:c + 128], src, AF.Copy),
                            inc="SS" if k == ntile - 1 else None)
            ev_ms[(h, it)] = ev

        def emit_av(h, ib, wtok):
            IB0 = ib * 512
            Jlo = max(0, (IB0 - 512) // 128)
            Jhi = (IB0 + 511) // 128
            b7 = pbank[7]
            prev = avml_ms.get((h, ib - 1), avml_ms.get((h - 1, 3), 0))
            b.wait("tensor", "SS", wtok)
            b.wait("tensor", "SV", prev)
            segs2 = list(range(Jlo, Jhi + 1))
            mlast = None
            for idx, J in enumerate(segs2):
                J0 = J * 128
                vtile = vt[J]
                ptile = ptsl[J]
                off = 512 + IB0 - J0
                mlast = b.emit("tensor", lambda e, vtile=vtile, ptile=ptile,
                               h=h, off=off, b7=b7, fj=(idx == 0),
                               lj=(idx == len(segs2) - 1): e.matmul(
                    b7[0:65, :], vtile[:, h * 65:h * 65 + 65],
                    ptile[:, off:off + 512], start=fj, stop=lj),
                    inc="ST" if idx == len(segs2) - 1 else None)
            avmm_ms[(h, ib)] = mlast
            b.wait("scalar", "ST", mlast)
            b.wait("scalar", "SV", prev)
            b.emit("scalar", lambda e, b7=b7: e.activation(
                den[64:65, :], b7[64:65, :], AF.Ln))
            b.emit("scalar", lambda e: e.activation(
                oscr[64:65, :], den[64:65, :], AF.Exp, scale=-1.0), inc="SS")
            ev_s = b.emit("scalar", lambda e, b7=b7: e.activation(
                oscr[0:64, :], b7[0:64, :], AF.Copy), inc="SS")
            b.wait("tensor", "SS", ev_s)
            dn_mm = b.emit("tensor", lambda e, b7=b7: e.matmul(
                b7[0:64, :], onesP[64:65, 0:64], oscr[64:65, :],
                start=True, stop=True), inc="ST")
            b.wait("vector", "ST", dn_mm)
            oth = otb[h]
            avml_ms[(h, ib)] = b.emit("vector", lambda e, oth=oth, IB0=IB0,
                                      b7=b7: e.tensor_tensor(
                oth[:, IB0:IB0 + 512], oscr[0:64, :], b7[0:64, :], OP.mult),
                inc="SV")

        for h in range(H):
            if SERIAL & 2:
                for _e in ENGS:
                    for _s in ("SV", "SS", "ST"):
                        b.wait(_e, _s, b.v(_s))
            for it in range(17):
                if it < 16:
                    emit_sim(h, it)
                    emit_rowmax(h, it)
                if it >= 1:
                    emit_trans(h, it - 1)
                    emit_evac(h, it - 1)
                if it < 16:
                    emit_exp(h, it)
                if it == 1 and h > 0:
                    emit_av(h - 1, 2, ev_ms[(h, 0)])
                elif it == 2 and h > 0:
                    emit_av(h - 1, 3, ev_ms[(h, 1)])
                elif it == 10:
                    emit_av(h, 0, ev_ms[(h, 8)])
                elif it == 14:
                    emit_av(h, 1, ev_ms[(h, 12)])
        d1 = b.emit("scalar", lambda e: e.activation(
            onesP[0:1, 0:1], onesP[0:1, 0:1], AF.Copy), inc="SS")
        emit_av(H - 1, 2, d1)
        d2 = b.emit("scalar", lambda e: e.activation(
            onesP[0:1, 0:1], onesP[0:1, 0:1], AF.Copy), inc="SS")
        emit_av(H - 1, 3, d2)
        if l == DUMP_LAYER and "ot" in dbg:
            for h2 in range(H):
                b.wait("gpsimd", "SV", b.v("SV"))
                b.emit("gpsimd", lambda e, h2=h2: e.dma_start(
                    out=dbg["ot"][h2 * 64:(h2 + 1) * 64, :], in_=otb[h2][0:64, :]),
                    inc="DO2", amt=16)
        bar_all(dma=False)
        b.free(*[f"ptsl{j}" for j in range(16)], "oscr", "den", "pij0", "pij1",
               *[f"qt{i}" for i in range(4)], *[f"kt{i}" for i in range(4)],
               *[f"vt{i}" for i in range(16)])

        woB = A("woB", H * DM, BF16, rows=64)
        hlnb = [A(f"hlnb{dm}", L, BF16) for dm in range(NKM)]
        fs1 = A("fs1", TC)
        x2full = [A(f"x2f{dm}", 1 + L) for dm in range(NKM)]
        wo_tok = b.ldg("gpsimd", [(
            woB.rearrange("p (hh cc) -> p hh cc", hh=H),
            di[f"wo_s_{l}"].rearrange("p (hh cc) -> p hh cc", hh=H))])
        lng = b.sb(f"lng{l}", [128, NKM]); lnb = b.sb(f"lnb{l}", [128, NKM])
        lnp_tok = b.ldg("sync", [(lng[:], di[f"lng_c_{l}"]),
                                 (lnb[:], di[f"lnb_c_{l}"])])
        bar_all(dma=False)
        b.wait("sync", "DS", b.v("DS"))
        x2f_tok = b.ldg("sync", [(
            x2full[dm][:], x2_scr[dm * 128:(dm + 1) * 128, 0:1 + L])
            for dm in range(NKM)])
        res_ms = {}
        for dm in range(NKM):
            bb = (dm % 2) * 4
            b.wait_tok("tensor", wo_tok)
            for fc in range(NCH):
                b.wait("tensor", "SV", res_ms.get((dm - 2, fc), 0))
            mlast = [None] * NCH
            for h in range(H):
                oth = otb[h]
                wv_ = woB[0:64, h * DM + dm * 128:h * DM + (dm + 1) * 128]
                for fc in range(NCH):
                    mlast[fc] = b.emit("tensor", lambda e, wv_=wv_, oth=oth,
                                       fc=fc, bb=bb, h=h: e.matmul(
                        pbank[bb + fc][:], wv_, oth[:, fc * TC:(fc + 1) * TC],
                        start=(h == 0), stop=(h == H - 1)),
                        inc="ST" if h == H - 1 else None)
            xv = x2full[dm]
            for fc in range(NCH):
                b.wait("vector", "ST", mlast[fc])
                if dm == 0 and fc == 0:
                    b.wait_tok("vector", x2f_tok)
                res_ms[(dm, fc)] = b.emit("vector", lambda e, xv=xv, fc=fc,
                                          bb=bb: e.tensor_tensor(
                    xv[:, 1 + fc * TC:1 + (fc + 1) * TC],
                    xv[:, 1 + fc * TC:1 + (fc + 1) * TC],
                    pbank[bb + fc][:], OP.add), inc="SV")
        groups = []
        for fc in range(NCH):
            t0 = 1 + fc * TC
            prew = []
            for dm in range(NKM):
                prew.append(("tensor", "SV", res_ms[(dm, fc)]))
                prew.append(("scalar", "SV", res_ms[(dm, fc)]))
            svs = [x2full[dm][:, t0:t0 + TC] if dm < 4
                   else x2full[dm][:, t0 - 1:t0 - 1 + TC]
                   for dm in range(NKM)]

            def emit_hln(dm, mu_b, rs_b, svs=svs, fc=fc):
                b.emit("vector", lambda e, dm=dm: e.tensor_tensor(
                    fs1[:], svs[dm], mu_b, OP.subtract))
                nv = b.emit("vector", lambda e, dm=dm: e.tensor_tensor(
                    fs1[:], fs1[:], rs_b, OP.mult), inc="SV")
                b.wait_tok("vector", lnp_tok)
                hv = b.emit("vector", lambda e, dm=dm, fc=fc: e.tensor_scalar(
                    hlnb[dm][:, fc * TC:(fc + 1) * TC], fs1[:],
                    lng[:, dm:dm + 1], lnb[:, dm:dm + 1], OP.mult, OP.add),
                    inc="SV")
            groups.append((svs, emit_hln, prew))
        ln_pipeline(groups)
        b.wait("sync", "SV", b.v("SV"))
        x2st_tok = b.ldg("sync", [(
            x2_scr[dm * 128:(dm + 1) * 128, 1:1 + L], x2full[dm][:, 1:1 + L])
            for dm in range(NKM)])
        b.wait_tok("sync", x2st_tok)
        bar_all(dma=False)
        b.free(*[f"ot{h}" for h in range(H)], "woB", "fs1",
               *[f"x2f{dm}" for dm in range(NKM)])

        w1b = [A(f"w1b{p}", 8 * 256, BF16) for p in range(2)]
        hmid = [A(f"hmid{j}", L, BF16) for j in range(NPAIR)]
        gact = [A(f"gact{p}", TC) for p in range(2)]
        b1c = b.sb(f"b1c{l}s", [128, 2 * NPAIR])
        b1_tok = b.ldg("sync", [(b1c[:], di[f"b1c_{l}"])])
        b.wait_tok("scalar", b1_tok)
        bar_all(dma=False)
        w1_dg = [None] * NPAIR
        w1_last = [None] * NPAIR

        def load_w1(j):
            if j >= NPAIR:
                return
            if j >= 2:
                b.wait("gpsimd", "ST", w1_last[j - 2])
            w1_dg[j] = b.ldg("gpsimd", [(
                w1b[j % 2].rearrange("p (ki cc) -> p ki cc", ki=NKM),
                di[f"w1_s_{l}"][j * DM:(j + 1) * DM, :]
                .rearrange("(ki p) cc -> p ki cc", p=128))])
        load_w1(0)
        load_w1(1)
        gg_ms = {}
        gs_ms = {}
        for j in range(NPAIR):
            wslab = w1b[j % 2]
            b.wait_tok("tensor", w1_dg[j])
            ml_a = [None] * NCH
            ml_g = [None] * NCH
            for fc in range(NCH):
                b.wait("tensor", "SV", gg_ms.get((j - 1, fc), 0))
                b.wait("tensor", "SS", gs_ms.get((j - 1, fc), 0))
                for ki in range(NKM):
                    wa = wslab[:, ki * 256:ki * 256 + 128]
                    hk = hlnb[ki]
                    ml_a[fc] = b.emit("tensor", lambda e, wa=wa, hk=hk, fc=fc,
                                      ki=ki: e.matmul(
                        pbank[fc][:], wa, hk[:, fc * TC:(fc + 1) * TC],
                        start=(ki == 0), stop=(ki == NKM - 1)),
                        inc="ST" if ki == NKM - 1 else None)
                for ki in range(NKM):
                    wg = wslab[:, ki * 256 + 128:ki * 256 + 256]
                    hk = hlnb[ki]
                    ml_g[fc] = b.emit("tensor", lambda e, wg=wg, hk=hk, fc=fc,
                                      ki=ki: e.matmul(
                        pbank[4 + fc][:], wg, hk[:, fc * TC:(fc + 1) * TC],
                        start=(ki == 0), stop=(ki == NKM - 1)),
                        inc="ST" if ki == NKM - 1 else None)
                b.wait("scalar", "ST", ml_g[fc])
                if fc >= 2 or j > 0:
                    b.wait("scalar", "SV",
                           gg_ms.get((j, fc - 2), gg_ms.get((j - 1, NCH - 2 + fc), 0)))
                gc = gact[fc % 2]
                g_s = b.emit("scalar", lambda e, j=j, fc=fc, gc=gc: e.activation(
                    gc[:], pbank[4 + fc][:], AF.Gelu,
                    bias=b1c[:, NPAIR + j:NPAIR + j + 1]), inc="SS")
                gs_ms[(j, fc)] = g_s
                b.wait("vector", "ST", ml_a[fc])
                b.wait("vector", "SS", g_s)
                hm = hmid[j]
                gg_ms[(j, fc)] = b.emit("vector", lambda e, hm=hm, j=j, fc=fc,
                                        gc=gc: e.scalar_tensor_tensor(
                    hm[:, fc * TC:(fc + 1) * TC], pbank[fc][:], b1c[:, j:j + 1],
                    gc[:], OP.add, OP.mult), inc="SV")
            w1_last[j] = ml_g[NCH - 1]
            load_w1(j + 2)
        if l == DUMP_LAYER and "hmid" in dbg:
            for j2 in range(NPAIR):
                b.wait("gpsimd", "SV", b.v("SV"))
                b.emit("gpsimd", lambda e, j2=j2: e.dma_start(
                    out=dbg["hmid"][j2 * 128:(j2 + 1) * 128, :], in_=hmid[j2][:]),
                    inc="DO2", amt=16)
        bar_all(dma=False)
        b.free("w1b0", "w1b1", *[f"hlnb{dm}" for dm in range(NKM)],
               "gact0", "gact1")
        w2bA = A("w2bA", 11 * DM, BF16)
        w2bB = A("w2bB", 11 * DM, BF16)
        w2_tok = b.ldg("gpsimd", [
            (w2bA.rearrange("p (ki cc) -> p ki cc", ki=11),
             di[f"w2_s_{l}"][0:11 * 128, :].rearrange("(ki p) cc -> p ki cc", p=128)),
            (w2bB.rearrange("p (ki cc) -> p ki cc", ki=11),
             di[f"w2_s_{l}"][11 * 128:, :].rearrange("(ki p) cc -> p ki cc",
                                                     p=128))])
        w2b_tok = w2_tok

        x2row = [A(f"x2row{p}", L) for p in range(2)]
        x2bfrow = A("x2bfrow", L, BF16)
        b2c = b.sb(f"b2c{l}s", [128, NKM])
        b2_tok = b.ldg("sync", [(b2c[:], di[f"b2c_{l}"])])
        bar_all(dma=False)
        b.wait_tok("vector", b2_tok)
        b.wait("sync", "DS", b.v("DS"))
        r2_ms = {}
        x2r_tok = [None] * NKM
        st_toks = {}
        x2bf_cv = {}
        x2r_tok[0] = b.ldg("sync", [(
            x2row[0][:], x2_scr[0:128, 1:1 + L])])
        if NKM > 1:
            x2r_tok[1] = b.ldg("sync", [(
                x2row[1][:], x2_scr[128:256, 1:1 + L])])
        for dm in range(NKM):
            bb = (dm % 2) * 4
            b.wait_tok("tensor", w2_tok)
            b.wait_tok("tensor", w2b_tok)
            for fc in range(NCH):
                b.wait("tensor", "SV", r2_ms.get((dm - 2, fc), 0))
            mlast = [None] * NCH
            for ki in range(NPAIR):
                wsl2 = w2bA if ki < 11 else w2bB
                kio = ki if ki < 11 else ki - 11
                wv_ = wsl2[:, kio * DM + dm * 128:kio * DM + (dm + 1) * 128]
                hm = hmid[ki]
                for fc in range(NCH):
                    mlast[fc] = b.emit("tensor", lambda e, wv_=wv_, hm=hm, fc=fc,
                                       ki=ki, bb=bb: e.matmul(
                        pbank[bb + fc][:], wv_, hm[:, fc * TC:(fc + 1) * TC],
                        start=(ki == 0), stop=(ki == NPAIR - 1)),
                        inc="ST" if ki == NPAIR - 1 else None)
            xr = x2row[dm % 2]
            for fc in range(NCH):
                b.wait("vector", "ST", mlast[fc])
                if fc == 0:
                    b.wait_tok("vector", x2r_tok[dm])
                r2_ms[(dm, fc)] = b.emit("vector", lambda e, xr=xr, dm=dm,
                                         fc=fc, bb=bb: e.scalar_tensor_tensor(
                    xr[:, fc * TC:(fc + 1) * TC], pbank[bb + fc][:],
                    b2c[:, dm:dm + 1], xr[:, fc * TC:(fc + 1) * TC],
                    OP.add, OP.add), inc="SV")
            b.wait("sync", "SV", r2_ms[(dm, NCH - 1)])
            st_tok = b.ldg("sync", [(
                x2_scr[dm * 128:(dm + 1) * 128, 1:1 + L], xr[:])])
            st_toks[dm] = st_tok
            if l == 0:
                if dm >= 1:
                    b.wait("vector", "SV", x2bf_cv.get(dm - 1, 0))
                cv = b.emit("vector", lambda e, xr=xr: e.tensor_copy(
                    x2bfrow[:], xr[:]), inc="SV")
                x2bf_cv[dm] = cv
                b.wait("sync", "SV", cv)
                sd = b.ldg("sync", [(
                    x2bf_scr[dm * 128:(dm + 1) * 128, 1:1 + L], x2bfrow[:])])
                b.wait_tok("vector", sd)
            if dm + 2 < NKM:
                b.wait_tok("sync", st_tok)
                if l == 0:
                    b.wait("sync", "SV", cv)
                x2r_tok[dm + 2] = b.ldg("sync", [(
                    x2row[(dm + 2) % 2][:],
                    x2_scr[(dm + 2) * 128:(dm + 3) * 128, 1:1 + L])])
        bar_all()
        b.free("w2bA", "w2bB", "x2row0", "x2row1", "x2bfrow",
               *[f"hmid{j}" for j in range(NPAIR)])
        bar_all()

    for l in range(2):
        emit_ffn_layer(l)

    fin = [[A(f"fin{p}_{dm}", TC) for dm in range(NKM)] for p in range(2)]
    x2d = [[A(f"x2d{p}_{dm}", TC) for dm in range(NKM)] for p in range(2)]
    bar_all()
    b.wait("sync", "DO", b.v("DO"))
    fin_toks = [None] * NCH
    dv_toks = [[] for _ in range(NCH)]

    def fin_load(fc):
        par = fc % 2
        if fc >= 2:
            for tok in dv_toks[fc - 2]:
                b.wait_tok("sync", tok)
        fin_toks[fc] = b.ldg("sync", [
            p_ for dm in range(NKM) for p_ in (
                (fin[par][dm][:],
                 x1_scr[dm * 128:(dm + 1) * 128, fc * TC:(fc + 1) * TC]),
                (x2d[par][dm][:],
                 x2_scr[dm * 128:(dm + 1) * 128,
                        1 + fc * TC:1 + (fc + 1) * TC]))])
    fin_load(0)
    fin_load(1)
    groups = []
    for fc in range(NCH):
        par = fc % 2
        svs = [x2d[par][dm][:] for dm in range(NKM)]

        def pre_emit(fc=fc, par=par):
            for dm in range(NKM):
                b.wait_tok("vector", fin_toks[fc])
                xv = x2d[par][dm]
                am = b.emit("vector", lambda e, xv=xv, par=par, dm=dm:
                            e.tensor_tensor(
                    xv[:], xv[:], fin[par][dm][:], OP.add), inc="SV")
                b.wait("tensor", "SV", am)
                b.wait("scalar", "SV", am)

        def emit_out(dm, mu_b, rs_b, svs=svs, fc=fc, par=par):
            b.emit("vector", lambda e, dm=dm: e.tensor_tensor(
                fin[par][dm][:], svs[dm], mu_b, OP.subtract))
            nv = b.emit("vector", lambda e, dm=dm: e.tensor_tensor(
                fin[par][dm][:], fin[par][dm][:], rs_b, OP.mult), inc="SV")
            b.wait("sync", "SV", nv)
            dv = b.ldg("sync", [(
                out[dm * 128:(dm + 1) * 128, fc * TC:(fc + 1) * TC],
                fin[par][dm][:])])
            dv_toks[fc].append(dv)
            b.wait_tok("vector", dv)
            if dm == NKM - 1 and fc + 2 < NCH:
                fin_load(fc + 2)
        groups.append((svs, emit_out, pre_emit))
    ln_pipeline(groups, serial=LNSER)
    for s in b._pool["sync"]:
        b.wait("sync", s, b.v(s))
    b.wait("sync", "DS", b.v("DS"))
    b.wait("sync", "DO", b.v("DO"))

    b.finish()
    cvals_np = np.zeros((128, 8), np.float32)
    for val, i in getattr(b, "cmap", {}).items():
        cvals_np[:, i] = val
    return nc, cvals_np


def kernel(**inputs):
    g, xts = host_prep(inputs)
    nc, cvals_np = build_program(DUMPS)
    in_maps = []
    for c in range(8):
        m = dict(g)
        m["xT"] = xts[c]
        m["cvals"] = cvals_np
        in_maps.append(m)
    res = run_bass_kernel_spmd(nc, in_maps, core_ids=list(range(8)))
    outs = [np.asarray(r["out"]).T for r in res.results]
    return np.stack(outs).astype(np.float32)


# version 14
# speedup vs baseline: 1.1881x; 1.1881x over previous
import contextlib
import numpy as np
import ml_dtypes
import concourse.bass as bass
import concourse.mybir as mybir
from concourse.bass_utils import run_bass_kernel_spmd
from concourse.alu_op_type import AluOpType as OP

F32 = mybir.dt.float32
BF16 = mybir.dt.bfloat16
AF = mybir.ActivationFunctionType
AX = mybir.AxisListType
BFD = ml_dtypes.bfloat16

L, DM, DI, NS, DTR = 2048, 1024, 2048, 16, 64
TC = 512
NCH = L // TC
NDT = DI // 128
NKM = DM // 128
H, DH, WIN = 8, 64, 512
HID, HIDP = 2730, 2816
NPAIR = HIDP // 128
NEGBIG = -1.0e30

ENGS = ("sync", "gpsimd", "scalar", "vector", "tensor")

DUMPS = ()
DUMP_LAYER = 0
INTERLEAVE = 4
DBGPOOL = False
LNSER = True
SERIAL = 0


class KB:
    def __init__(self, nc):
        self.nc = nc
        self.q = {e: [] for e in ENGS}
        self.st = contextlib.ExitStack()
        self.semh, self.semc = {}, {}

    def sem(self, *names):
        for n in names:
            self.semh[n] = self.st.enter_context(self.nc.semaphore(n))
            self.semc[n] = 0

    def sb(self, name, shape, dt=F32):
        return self.st.enter_context(self.nc.sbuf_tensor(name, list(shape), dt))

    def pb(self, name, shape=(128, 512)):
        return self.st.enter_context(self.nc.psum_tensor(name, list(shape), F32))

    def wait(self, eng, sem, val):
        if val is None or val <= 0:
            return
        h = self.semh[sem]
        self.q[eng].append(lambda e, h=h, v=val: e.wait_ge(h, v))

    def wait_tok(self, eng, tok):
        if tok is None or tok == 0:
            return
        self.wait(eng, tok[0], tok[1])

    def dma_pool_init(self, pools):
        self._pool = {}
        self._pidx = {}
        for eng, n in pools.items():
            names = [f"DQ_{eng}_{i}" for i in range(n)]
            self.sem(*names)
            self._pool[eng] = names
            self._pidx[eng] = 0

    def ldg(self, eng, pairs):
        names = self._pool[eng]
        sem = names[self._pidx[eng] % len(names)]
        self._pidx[eng] += 1
        self.wait(eng, sem, self.semc[sem])
        v = None
        for dst, src in pairs:
            v = self.emit(eng, lambda e, d=dst, s=src: e.dma_start(
                out=d, in_=s), inc=sem, amt=16)
        return (sem, v)

    def emit(self, eng, fn, inc=None, amt=1):
        if inc is None:
            self.q[eng].append(fn)
            return None
        h = self.semh[inc]
        self.semc[inc] += amt
        v = self.semc[inc]
        self.q[eng].append(lambda e, fn=fn, h=h, a=amt: fn(e).then_inc(h, a))
        return v

    def v(self, sem):
        return self.semc[sem]

    def arena_init(self, nbytes):
        self._ar = self.st.enter_context(
            self.nc.sbuf_tensor("ARENA", [128, nbytes // 4], F32))
        self._arb = self._ar.bitcast(BF16)
        self._free = [(0, nbytes)]
        self._live = {}

    def alloc(self, name, cols, dt=F32, rows=None):
        esz = 4 if dt == F32 else 2
        nb = (cols * esz + 31) & ~31
        self._hw = max(getattr(self, "_hw", 0),
                       sum(v[1] for v in self._live.values()) + nb)
        for i, (off, sz) in enumerate(self._free):
            if sz >= nb:
                self._free[i] = (off + nb, sz - nb)
                if self._free[i][1] == 0:
                    self._free.pop(i)
                self._live[name] = (off, nb)
                base = self._ar if dt == F32 else self._arb
                ap = base[:, off // esz: off // esz + cols]
                if rows is not None:
                    ap = ap[0:rows, :]
                return ap
        raise MemoryError(f"arena OOM for {name}: need {nb}, free={self._free}")

    def bfview(self, name, cols):
        off, nb = self._live[name]
        assert cols * 2 <= nb
        return self._arb[:, off // 2: off // 2 + cols]

    def free(self, *names):
        for name in names:
            off, nb = self._live.pop(name)
            self._free.append((off, nb))
        self._free.sort()
        out = []
        for off, sz in self._free:
            if out and out[-1][0] + out[-1][1] == off:
                out[-1] = (out[-1][0], out[-1][1] + sz)
            else:
                out.append((off, sz))
        self._free = [tuple(x) for x in out]

    def _cvals_init(self):
        if not hasattr(self, "_cvals"):
            self._cvals = self.sb("cvals_sb", [128, 8], F32)
            self.cmap = {}
        return self._cvals[:]

    def const_col(self, val):
        if not hasattr(self, "_cvals"):
            self._cvals = self.sb("cvals_sb", [128, 8], F32)
            self.cmap = {}
        if val not in self.cmap:
            i = len(self.cmap)
            assert i < 8
            self.nc.const_aps.register(F32, float(val), self._cvals[:, i:i + 1])
            self.cmap[val] = i
        return float(val)

    def finish(self):
        with self.st:
            with self.nc.Block() as blk:
                for eng in ENGS:
                    def mk(eng=eng):
                        def body(e):
                            for fn in self.q[eng]:
                                fn(e)
                        return body
                    getattr(blk, eng)(mk())


def _pack_tiles_rows(w):
    return np.ascontiguousarray(w, dtype=np.float32)


def host_prep(inputs):
    g = {}
    W_in = np.asarray(inputs["L0_attn_W_in"], np.float32)
    win_s = np.stack(
        [np.concatenate([W_in[:, dt * 128:(dt + 1) * 128],
                         W_in[:, DI + dt * 128:DI + (dt + 1) * 128]], axis=1)
         for dt in range(NDT)])
    g["win_s"] = np.ascontiguousarray(win_s.reshape(NDT * DM, 256))
    W_out = np.asarray(inputs["L0_attn_W_out"], np.float32)
    wout_s = np.stack([W_out[:, dm * 128:(dm + 1) * 128] for dm in range(NKM)])
    g["wout_s"] = np.ascontiguousarray(wout_s.reshape(NKM * DI, 128))
    W_x = np.asarray(inputs["L0_attn_W_x"], np.float32)
    g["wx_s"] = np.ascontiguousarray(
        W_x.reshape(NDT, 128, 96).transpose(1, 0, 2).reshape(128, NDT * 96))
    g["wdt_s"] = _pack_tiles_rows(inputs["L0_attn_W_dt"])
    g["bdt_c"] = np.ascontiguousarray(
        np.asarray(inputs["L0_attn_b_dt"], np.float32).reshape(NDT, 128).T)
    g["d_c"] = np.ascontiguousarray(
        np.asarray(inputs["L0_attn_D_param"], np.float32).reshape(NDT, 128).T)
    g["wconv_c"] = np.ascontiguousarray(
        np.asarray(inputs["L0_attn_W_conv"], np.float32)
        .reshape(NDT, 128, 3).transpose(1, 0, 2).reshape(128, NDT * 3))
    g["alog_c"] = np.ascontiguousarray(
        np.asarray(inputs["L0_attn_A_log"], np.float32)
        .reshape(NDT, 128, NS).transpose(1, 0, 2).reshape(128, NDT * NS))

    wqkv = np.asarray(inputs["L0_ffn_wqkv"], np.float32)
    for l in range(2):
        g[f"wq_{l}"] = np.ascontiguousarray(wqkv[l][:, 0:512])
        g[f"wk_{l}"] = np.ascontiguousarray(wqkv[l][:, 512:1024])
        g[f"wv_{l}"] = np.ascontiguousarray(wqkv[l][:, 1024:1536])
        wo = np.asarray(inputs["L0_ffn_wo"], np.float32)[l]
        g[f"wo_s_{l}"] = np.ascontiguousarray(
            wo.reshape(H, DH, DM).transpose(1, 0, 2).reshape(DH, H * DM))
        w1 = np.asarray(inputs["L0_ffn_w1"], np.float32)[l]
        w1p = np.zeros((DM, 2 * HIDP), np.float32)
        w1p[:, :HID] = w1[:, :HID]
        w1p[:, HIDP:HIDP + HID] = w1[:, HID:]
        w1_s = np.stack(
            [np.concatenate([w1p[:, j * 128:(j + 1) * 128],
                             w1p[:, HIDP + j * 128:HIDP + (j + 1) * 128]], 1)
             for j in range(NPAIR)])
        g[f"w1_s_{l}"] = np.ascontiguousarray(w1_s.reshape(NPAIR * DM, 256))
        b1 = np.asarray(inputs["L0_ffn_b1"], np.float32)[l]
        b1p = np.zeros(2 * HIDP, np.float32)
        b1p[:HID] = b1[:HID]
        b1p[HIDP:HIDP + HID] = b1[HID:]
        b1c = np.concatenate([b1p[:HIDP].reshape(NPAIR, 128).T,
                              b1p[HIDP:].reshape(NPAIR, 128).T], axis=1)
        g[f"b1c_{l}"] = np.ascontiguousarray(b1c)
        w2 = np.asarray(inputs["L0_ffn_w2"], np.float32)[l]
        w2p = np.zeros((HIDP, DM), np.float32)
        w2p[:HID] = w2
        g[f"w2_s_{l}"] = np.ascontiguousarray(w2p)
        g[f"b2c_{l}"] = np.ascontiguousarray(
            np.asarray(inputs["L0_ffn_b2"], np.float32)[l].reshape(NKM, 128).T)
        g[f"lng_c_{l}"] = np.ascontiguousarray(
            np.asarray(inputs["L0_ffn_ln_g"], np.float32)[l].reshape(NKM, 128).T)
        g[f"lnb_c_{l}"] = np.ascontiguousarray(
            np.asarray(inputs["L0_ffn_ln_b"], np.float32)[l].reshape(NKM, 128).T)
        qs = np.asarray(inputs["L0_ffn_q_scale"], np.float32)[l]
        ks = np.asarray(inputs["L0_ffn_k_scale"], np.float32)[l]
        g[f"qs2_{l}"] = np.ascontiguousarray(np.tile(qs, 2)[:, None])
        g[f"qs2r_{l}"] = np.ascontiguousarray(
            np.tile(np.concatenate([qs[32:], qs[:32]]), 2)[:, None])
        g[f"ks2_{l}"] = np.ascontiguousarray(np.tile(ks, 2)[:, None])
        g[f"ks2r_{l}"] = np.ascontiguousarray(
            np.tile(np.concatenate([ks[32:], ks[:32]]), 2)[:, None])

    half = np.arange(0, DH, 2, dtype=np.float64)
    inv_freq = 1.0 / (10000.0 ** (half / DH))
    t = np.arange(L, dtype=np.float64)
    freqs = t[:, None] * inv_freq[None, :]
    freqs = np.concatenate([freqs, freqs], 1)
    base = (half + 0.4 * DH) / (1.4 * DH)
    power = (t - L // 2) / float(WIN // 2)
    scale = base[None, :] ** power[:, None]
    scale = np.concatenate([scale, scale], 1)
    cos, sin = np.cos(freqs), np.sin(freqs)
    QK = 8.0
    csq = (QK * cos * scale).T.astype(np.float32)
    ssq = (QK * sin * scale).T.astype(np.float32)
    csk = (cos / scale).T.astype(np.float32)
    ssk = (sin / scale).T.astype(np.float32)
    g["tab_csq"] = np.ascontiguousarray(np.tile(csq, (2, 1)))
    g["tab_ssq"] = np.ascontiguousarray(np.tile(ssq, (2, 1)))
    g["tab_csk"] = np.ascontiguousarray(np.tile(csk, (2, 1)))
    g["tab_ssk"] = np.ascontiguousarray(np.tile(ssk, (2, 1)))
    R = np.zeros((DH, DH), np.float32)
    for f in range(32):
        R[f + 32, f] = -1.0
        R[f, f + 32] = 1.0
    g["r2f"] = np.ascontiguousarray(
        np.kron(np.eye(2, dtype=np.float32), R))
    g["i128"] = np.eye(128, dtype=np.float32)
    g["i128bf"] = np.eye(128, dtype=np.float32).astype(BFD)
    ii, jj = np.meshgrid(np.arange(128), np.arange(128), indexing="ij")
    m_near = np.where(jj <= ii, 0.0, NEGBIG).astype(np.float32)
    m_far = np.where(jj >= ii, 0.0, NEGBIG).astype(np.float32)
    g["mask_near_ij"] = m_near
    g["mask_far_ij"] = m_far
    g["maskT_near"] = np.ascontiguousarray(m_near.T)
    g["maskT_far"] = np.ascontiguousarray(m_far.T)
    g["ones_row"] = np.ones((1, 128), np.float32)
    g["ones_row_bf"] = np.ones((1, 128), np.float32).astype(BFD)
    g["ones_col"] = np.ones((128, 1), np.float32)
    g["ones_col_bf"] = np.ones((128, 1), np.float32).astype(BFD)
    bl2 = np.zeros((128, 2), np.float32)
    bl2[0:64, 0] = 1.0
    bl2[64:128, 1] = 1.0
    g["bl2_bf"] = bl2.astype(BFD)
    hs = np.zeros((2, 128), np.float32)
    hs[0, 0:64] = 1.0
    hs[1, 64:128] = 1.0
    g["headsel"] = hs

    x = np.asarray(inputs["L0_attn_x"], np.float32)
    xts = [np.ascontiguousarray(x[c].T) for c in range(x.shape[0])]
    for k in list(g):
        if (k in ("win_s", "wout_s", "wx_s", "wdt_s")
                or k.startswith(("wq_", "wk_", "wv_", "wo_s_", "w1_s_",
                                 "w2_s_"))):
            g[k] = np.ascontiguousarray(g[k].astype(ml_dtypes.bfloat16))
    return g, xts



def build_program(dumps=()):
    nc = bass.Bass("TRN2", target_bir_lowering=False)
    b = KB(nc)
    di = {}

    def inp(name, shape, dt=F32):
        di[name] = nc.dram_tensor(name, list(shape), dt, kind="ExternalInput").ap()
        return di[name]

    inp("xT", [DM, L])
    inp("win_s", [NDT * DM, 256], BF16); inp("wout_s", [NKM * DI, 128], BF16)
    inp("wx_s", [128, NDT * 96], BF16); inp("wdt_s", [64, DI], BF16)
    inp("bdt_c", [128, NDT]); inp("d_c", [128, NDT])
    inp("wconv_c", [128, NDT * 3]); inp("alog_c", [128, NDT * NS])
    for l in range(2):
        inp(f"wq_{l}", [DM, 512], BF16); inp(f"wk_{l}", [DM, 512], BF16)
        inp(f"wv_{l}", [DM, 512], BF16)
        inp(f"wo_s_{l}", [DH, H * DM], BF16)
        inp(f"w1_s_{l}", [NPAIR * DM, 256], BF16)
        inp(f"b1c_{l}", [128, 2 * NPAIR]); inp(f"w2_s_{l}", [HIDP, DM], BF16)
        inp(f"b2c_{l}", [128, NKM]); inp(f"lng_c_{l}", [128, NKM]); inp(f"lnb_c_{l}", [128, NKM])
        for nm in ("qs2", "qs2r", "ks2", "ks2r"):
            inp(f"{nm}_{l}", [128, 1])
    for nm in ("tab_csq", "tab_ssq", "tab_csk", "tab_ssk"):
        inp(nm, [128, L])
    inp("r2f", [128, 128]); inp("i128", [128, 128]); inp("i128bf", [128, 128], BF16)
    inp("cvals", [128, 8])
    inp("mask_near_ij", [128, 128]); inp("mask_far_ij", [128, 128])
    inp("maskT_near", [128, 128]); inp("maskT_far", [128, 128])
    inp("ones_row", [1, 128]); inp("ones_row_bf", [1, 128], BF16)
    inp("ones_col", [128, 1]); inp("ones_col_bf", [128, 1], BF16)
    inp("bl2_bf", [128, 2], BF16); inp("headsel", [2, 128])

    out = nc.dram_tensor("out", [DM, L], F32, kind="ExternalOutput").ap()
    dbg = {}
    for name in dumps:
        shape = {"x1": [DM, L], "du": [DI, L], "delta": [DI, L], "u0": [DI, L],
                 "yraw": [DI, L], "y3": [DI, L], "xdbl": [96, L],
                 "su": [DI, L], "qt": [512, L], "kt": [512, L],
                 "ot": [512, L], "xwo": [DM, L], "hln": [DM, L],
                 "hmid": [HIDP, L], "x2l": [DM, L],
                 "qraw": [512, L], "wops": [DM, L], "y2": [DI, L],
                 "ch00": [128, L], "s0": [DM, L], "sx": [DM, L]}[name]
        dbg[name] = nc.dram_tensor("dbg_" + name, shape, F32,
                                   kind="ExternalOutput").ap()

    bc_scr = nc.dram_tensor("bc_scr", [NCH * 32, TC], F32, kind="Internal").ap()
    x1_scr = nc.dram_tensor("x1_scr", [DM, L], F32, kind="Internal").ap()
    x1bf_scr = nc.dram_tensor("x1bf_scr", [DM, 1 + L], BF16, kind="Internal").ap()
    x2_scr = nc.dram_tensor("x2_scr", [DM, 1 + L], F32, kind="Internal").ap()
    x2bf_scr = nc.dram_tensor("x2bf_scr", [DM, 1 + L], BF16, kind="Internal").ap()

    b.sem("DW", "DG", "DR", "DS", "DO", "DO2", "SV", "SS", "ST", "SG", "MKS", "MKG")
    b.dma_pool_init({"sync": 12, "gpsimd": 8})

    bdt = b.sb("bdt", [128, NDT])
    dpar = b.sb("dpar", [128, NDT])
    wconv = b.sb("wconv", [128, NDT * 3])
    hstate = b.sb("hstate", [128, NDT * NS])
    ucarry = b.sb("ucarry", [128, NDT * 2])
    i128 = b.sb("i128s", [128, 128])
    i128bf = b.sb("i128bfs", [128, 128], BF16)
    onesr = b.sb("onesr", [1, 128]); onesrb = b.sb("onesrb", [1, 128], BF16)
    onesc = b.sb("onesc", [128, 1]); onescb = b.sb("onescb", [128, 1], BF16)
    bl2 = b.sb("bl2", [128, 2], BF16)
    headsel = b.sb("headsel_s", [2, 128])
    r2f = b.sb("r2fs", [128, 128])
    mnij = b.sb("mnij", [128, 128]); mfij = b.sb("mfij", [128, 128])
    mtn = b.sb("mtn", [128, 128]); mtf = b.sb("mtf", [128, 128])

    b.arena_init(190 * 1024)
    A = b.alloc
    aneg = A("aneg", NDT * NS)
    wx = A("wx", NDT * 96, BF16)
    wdt = A("wdt", DI, BF16, rows=64)
    xbf = [A(f"xbf{k}", TC, BF16) for k in range(NKM)]
    winb = [A(f"winb{p}", 8 * 256, BF16) for p in range(2)]
    woutb = [A(f"woutb{p}", 16 * 128, BF16) for p in range(2)]
    ubuf = A("ubuf", 2 + TC)
    convA = A("convA", TC); convB = A("convB", TC)
    sudu = [A(f"sudu{i}", TC) for i in range(NDT)]
    delta = [A(f"delta{i}", TC) for i in range(NDT)]
    subf = [A(f"subf{i}", TC, BF16) for i in range(NDT)]
    sres = [A(f"sres{i}", TC, BF16) for i in range(NDT)]
    y2buf = A("y2buf", TC)
    dtbf = A("dtbf", TC, BF16, rows=64)
    bcsb = A("bcsb", TC)
    dA2 = [A(f"dA{p}", TC) for p in range(2)]
    dbu2 = [A(f"dbu{p}", TC) for p in range(2)]
    hsl = A("hsl", TC)
    chsl = [A(f"chsl{p}", TC, BF16) for p in range(2)]
    Bsl = [A(f"Bsl{p}", TC) for p in range(2)]
    Csl = [A(f"Csl{p}", TC) for p in range(2)]
    sqs = b.sb("sqs", [128, TC], BF16)
    rowm = b.sb("rowm", [1, TC]); rowv = b.sb("rowv", [1, TC])
    rowr = b.sb("rowr", [1, TC])
    row2 = b.sb("row2", [2, TC])
    x1bf = b.sb("x1bf", [128, TC], BF16)

    xch = sudu[:NKM]
    pbank = [b.pb(f"pbank{i}") for i in range(8)]

    ld = b.ldg

    prelude_tok = b.ldg("sync", [
        (bdt[:], di["bdt_c"]), (dpar[:], di["d_c"]), (wconv[:], di["wconv_c"]),
        (aneg[:], di["alog_c"]), (i128[:], di["i128"]),
        (onesr[:], di["ones_row"]), (onesrb[:], di["ones_row_bf"]),
        (onesc[:], di["ones_col"]), (onescb[:], di["ones_col_bf"]),
        (bl2[:], di["bl2_bf"]), (headsel[:], di["headsel"]),
        (r2f[:], di["r2f"]), (i128bf[:], di["i128bf"]),
        (mnij[:], di["mask_near_ij"]),
        (mfij[:], di["mask_far_ij"]), (mtn[:], di["maskT_near"]),
        (mtf[:], di["maskT_far"]), (b._cvals_init(), di["cvals"])])
    wxdt_tok = b.ldg("gpsimd", [(wx[:], di["wx_s"]), (wdt[:], di["wdt_s"])])

    b.wait_tok("scalar", prelude_tok)
    an1 = b.emit("scalar", lambda e: e.activation(aneg[:], aneg[:], AF.Exp),
                 inc="SS")
    b.wait("scalar", "SS", an1)
    b.emit("scalar", lambda e: e.activation(aneg[:], aneg[:], AF.Copy,
                                            scale=-1.0), inc="SS")
    b.emit("vector", lambda e: e.memset(hstate[:], 0.0))
    b.emit("vector", lambda e: e.memset(ucarry[:], 0.0))
    b.emit("vector", lambda e: e.memset(ubuf[:], 0.0), inc="SV")

    bar = {"SV": b.v("SV"), "SS": b.v("SS"), "ST": b.v("ST"), "SG": b.v("SG")}
    for eng in ENGS:
        for s in ("SV", "SS", "ST", "SG"):
            b.wait(eng, s, bar[s])

    xch = [dA2[0], dA2[1], dbu2[0], dbu2[1], Bsl[0], Bsl[1], Csl[0], Csl[1]]
    y3b = [b.bfview(f"delta{i}", TC) for i in range(NDT)]

    def new_m1_ctx(c, ydp):
        return {"c": c, "tsl": slice(c * TC, (c + 1) * TC), "ydp": ydp,
                "win_dg": [None] * NDT, "umm": [None] * NDT,
                "resmm": [None] * NDT, "evu": [None] * NDT,
                "res_s": [None] * NDT, "conv": [None] * NDT,
                "xdbl_done": None, "xbf_tok": None, "su_s": [None] * NDT,
                "defer_xdbl": False,
                "du_done": [None] * NDT, "sp": [None] * NDT,
                "bc_tok": None, "bc_s": None, "dtbf_s": None}

    def emit_xbf_load(ctx):
        ctx["xbf_tok"] = b.ldg("gpsimd", [
            (xbf[k][:], di["xT"][k * 128:(k + 1) * 128, ctx["tsl"]])
            for k in range(NKM)])

    def load_win(ctx, dt):
        if dt >= NDT:
            return
        if dt >= 2:
            b.wait("gpsimd", "ST", ctx["resmm"][dt - 2])
        ctx["win_dg"][dt] = b.ldg("gpsimd", [(
            winb[dt % 2][:].rearrange("p (ki cc) -> p ki cc", ki=NKM),
            di["win_s"][dt * DM:(dt + 1) * DM, :]
            .rearrange("(ki p) cc -> p ki cc", p=128))])

    def emit_m1_dt(ctx, dt):
        c = ctx["c"]; tsl = ctx["tsl"]; ydp = ctx["ydp"]
        if SERIAL & 8:
            for _e in ENGS:
                for _s in ("SV", "SS", "ST"):
                    b.wait(_e, _s, b.v(_s))
        wslab = winb[dt % 2]
        b.wait_tok("tensor", ctx["win_dg"][dt])
        if dt == 0:
            b.wait_tok("tensor", ctx["xbf_tok"])
            b.wait_tok("tensor", wxdt_tok)
        if dt >= 1:
            b.wait("tensor", "SS", ctx["evu"][dt - 1])
            b.wait("tensor", "SS", ctx["res_s"][dt - 1])
            b.wait("tensor", "SV", ctx["conv"][dt - 1])
        elif ydp is not None:
            b.wait("tensor", "SV", ydp[8])
            b.wait("tensor", "SV", ydp[9])
        for ki in range(NKM):
            wv = wslab[:, ki * 256:ki * 256 + 128]
            last = ki == NKM - 1
            if last:
                ctx["umm"][dt] = b.emit("tensor", lambda e, wv=wv, ki=ki:
                                        e.matmul(
                    pbank[0][:], wv, xbf[ki][:], start=(ki == 0), stop=True),
                    inc="ST")
            else:
                b.emit("tensor", lambda e, wv=wv, ki=ki: e.matmul(
                    pbank[0][:], wv, xbf[ki][:], start=(ki == 0), stop=False))
        for ki in range(NKM):
            wv = wslab[:, ki * 256 + 128:ki * 256 + 256]
            last = ki == NKM - 1
            if last:
                ctx["resmm"][dt] = b.emit("tensor", lambda e, wv=wv, ki=ki:
                                          e.matmul(
                    pbank[1][:], wv, xbf[ki][:], start=(ki == 0), stop=True),
                    inc="ST")
            else:
                b.emit("tensor", lambda e, wv=wv, ki=ki: e.matmul(
                    pbank[1][:], wv, xbf[ki][:], start=(ki == 0), stop=False))
        load_win(ctx, dt + 2)
        b.wait("scalar", "ST", ctx["umm"][dt])
        if dt == 0 and c == 0:
            b.wait("scalar", "SV", b.v("SV"))
        ctx["evu"][dt] = b.emit("scalar", lambda e: e.activation(
            ubuf[:, 2:2 + TC], pbank[0][:], AF.Copy), inc="SS")
        if "u0" in dbg:
            b.wait("sync", "SS", ctx["evu"][dt])
            dver = b.emit("sync", lambda e, dt=dt, tsl=tsl: e.dma_start(
                out=dbg["u0"][dt * 128:(dt + 1) * 128, tsl],
                in_=ubuf[:, 2:2 + TC]), inc="DO", amt=16)
            b.wait("scalar", "DO", dver)
        b.wait("vector", "SS", ctx["evu"][dt])
        cpin = ucarry[:, dt * 2:dt * 2 + 2]
        cp1 = b.emit("vector", lambda e, cpin=cpin: e.tensor_copy(
            ubuf[:, 0:2], cpin), inc="SV")
        b.wait("vector", "SV", cp1)
        b.emit("vector", lambda e, cpin=cpin: e.tensor_copy(
            cpin, pbank[0][:, TC - 2:TC]))
        w0 = wconv[:, dt * 3:dt * 3 + 1]
        w1c = wconv[:, dt * 3 + 1:dt * 3 + 2]
        w2c = wconv[:, dt * 3 + 2:dt * 3 + 3]
        b.emit("vector", lambda e, w0=w0: e.tensor_scalar(
            convA[:], ubuf[:, 0:TC], w0, None, OP.mult))
        b.emit("vector", lambda e, w1c=w1c: e.scalar_tensor_tensor(
            convB[:], ubuf[:, 1:1 + TC], w1c, convA[:], OP.mult, OP.add))
        conv_v = b.emit("vector", lambda e, w2c=w2c: e.scalar_tensor_tensor(
            convA[:], ubuf[:, 2:2 + TC], w2c, convB[:], OP.mult, OP.add),
            inc="SV")
        ctx["conv"][dt] = conv_v
        b.wait("scalar", "SV", conv_v)
        sud = sudu[dt]; sub = subf[dt]; srb = sres[dt]
        b.emit("scalar", lambda e, sud=sud: e.activation(
            sud[:], convA[:], AF.Silu))
        su_s = b.emit("scalar", lambda e, sub=sub: e.activation(
            sub[:], convA[:], AF.Silu), inc="SS")
        b.wait("scalar", "ST", ctx["resmm"][dt])
        ctx["res_s"][dt] = b.emit("scalar", lambda e, srb=srb: e.activation(
            srb[:], pbank[1][:], AF.Silu), inc="SS")
        ctx["su_s"][dt] = su_s
        if not ctx.get("defer_xdbl"):
            emit_xdbl(ctx, dt)

    def emit_xdbl(ctx, dt):
        ydp = ctx["ydp"]
        b.wait("tensor", "SS", ctx["su_s"][dt])
        if dt == 0 and ydp is not None:
            b.wait("tensor", "SV", ydp[10])
        wxs = wx[:, dt * 96:(dt + 1) * 96]
        sub = subf[dt]
        ctx["xdbl_done"] = b.emit("tensor", lambda e, wxs=wxs, sub=sub, dt=dt:
                                  e.matmul(pbank[2][0:96, :], wxs, sub[:],
                                           start=(dt == 0),
                                           stop=(dt == NDT - 1)),
                                  inc="ST")

    def emit_m2(ctx, wout7):
        c = ctx["c"]; tsl = ctx["tsl"]; ydp = ctx["ydp"]
        b.wait("scalar", "ST", ctx["xdbl_done"])
        ctx["dtbf_s"] = b.emit("scalar", lambda e: e.activation(
            dtbf[:], pbank[2][0:64, :], AF.Copy), inc="SS")
        ctx["bc_s"] = b.emit("scalar", lambda e: e.activation(
            bcsb[64:96, :], pbank[2][64:96, :], AF.Copy), inc="SS")
        b.wait("sync", "SS", ctx["bc_s"])
        ctx["bc_tok"] = b.ldg("sync", [(bc_scr[c * 32:(c + 1) * 32, :],
                                        bcsb[64:96, :])])
        if wout7 is not None:
            b.wait("vector", "ST", wout7)
        for dt in range(NDT):
            b.wait("tensor", "SS", ctx["dtbf_s"])
            if dt >= 1:
                b.wait("tensor", "SS", ctx["sp"][dt - 1])
            elif ydp is not None:
                b.wait("tensor", "SV", ydp[11])
            wds = wdt[:, dt * 128:(dt + 1) * 128]
            dmm = b.emit("tensor", lambda e, wds=wds: e.matmul(
                pbank[3][:], wds, dtbf[:], start=True, stop=True), inc="ST")
            b.wait("scalar", "ST", dmm)
            if dt >= 1:
                b.wait("scalar", "SV", ctx["du_done"][dt - 1])
            bcol = bdt[:, dt:dt + 1]
            b.emit("scalar", lambda e, bcol=bcol: e.activation(
                convA[:], pbank[3][:], AF.Abs, bias=bcol))
            b.emit("scalar", lambda e: e.activation(
                convA[:], convA[:], AF.Exp, scale=-1.0))
            b.emit("scalar", lambda e: e.activation(
                convA[:], convA[:], AF.Ln, bias=1.0))
            ctx["sp"][dt] = b.emit("scalar", lambda e, bcol=bcol: e.activation(
                convB[:], pbank[3][:], AF.Relu, bias=bcol), inc="SS")
            b.wait("vector", "SS", ctx["sp"][dt])
            dl_ = delta[dt]; sud = sudu[dt]
            b.emit("vector", lambda e, dl_=dl_: e.tensor_tensor(
                dl_[:], convA[:], convB[:], OP.add))
            ctx["du_done"][dt] = b.emit("vector", lambda e, dl_=dl_, sud=sud:
                                        e.tensor_tensor(sud[:], dl_[:], sud[:],
                                                        OP.mult), inc="SV")
        if "su" in dbg:
            for dt in range(NDT):
                b.wait("gpsimd", "SS", ctx["res_s"][dt])
                b.emit("gpsimd", lambda e, dt=dt, tsl=tsl: e.dma_start(
                    out=dbg["su"][dt * 128:(dt + 1) * 128, tsl],
                    in_=subf[dt][:]), inc="DO2", amt=16)
        if "xdbl" in dbg:
            b.wait("gpsimd", "SS", ctx["bc_s"])
            b.emit("gpsimd", lambda e, tsl=tsl: e.dma_start(
                out=dbg["xdbl"][0:64, tsl], in_=dtbf[:]), inc="DO2", amt=16)
            b.wait("sync", "SS", ctx["bc_s"])
            b.emit("sync", lambda e, tsl=tsl: e.dma_start(
                out=dbg["xdbl"][64:96, tsl], in_=bcsb[64:96, :]), inc="DO", amt=16)
        if "delta" in dbg:
            for dt in range(NDT):
                b.wait("sync", "SV", ctx["du_done"][dt])
                b.emit("sync", lambda e, dt=dt, tsl=tsl: e.dma_start(
                    out=dbg["delta"][dt * 128:(dt + 1) * 128, tsl],
                    in_=delta[dt][:]), inc="DO", amt=16)
        if "du" in dbg:
            for dt in range(NDT):
                b.wait("sync", "SV", ctx["du_done"][dt])
                b.emit("sync", lambda e, dt=dt, tsl=tsl: e.dma_start(
                    out=dbg["du"][dt * 128:(dt + 1) * 128, tsl],
                    in_=sudu[dt][:]), inc="DO", amt=16)

    def emit_scan(ctx, prev_sc):
        c = ctx["c"]; tsl = ctx["tsl"]
        wout_dg = [None] * NKM
        wout_nop = [None] * NKM

        def load_wout(dm):
            if dm >= NKM:
                return
            if dm >= 2:
                b.wait("gpsimd", "ST", wout_nop[dm - 2])
            elif prev_sc is not None:
                b.wait("gpsimd", "ST", prev_sc["wout_nop"][6 + dm])
            wout_dg[dm] = b.ldg("gpsimd", [(
                woutb[dm % 2][:].rearrange("p (ki cc) -> p ki cc", ki=NDT),
                di["wout_s"][dm * DI:(dm + 1) * DI, :]
                .rearrange("(ki p) cc -> p ki cc", p=128))])
        load_wout(0)
        load_wout(1)
        flush = ctx.get("flush")
        bankfree = ctx.get("bankfree")
        if flush is None:
            b.wait("tensor", "SS", b.v("SS"))
            b.wait("tensor", "SV", b.v("SV"))
            b.wait("scalar", "SV", b.v("SV"))
            b.wait("gpsimd", "SV", b.v("SV"))
        scan_ms = {}
        scan_pe = {}
        repl_ms = {}
        nlast_ms = {}
        y3_done = [None] * NDT
        for half in range(2):
            for n in range(NS):
                hn = half * NS + n
                par = hn % 2
                b.wait_tok("sync", ctx["bc_tok"])
                if hn >= 2:
                    b.wait("sync", "SV", nlast_ms[hn - 2])
                elif flush is not None:
                    b.wait_tok("sync", flush[4 + par])
                    b.wait_tok("sync", flush[6 + par])
                repl_ms[hn] = b.ldg("sync", [
                    (Bsl[par][:],
                     bass.AP(bc_scr.tensor, (c * 32 + n) * TC, [[0, 128], [1, TC]])),
                    (Csl[par][:],
                     bass.AP(bc_scr.tensor, (c * 32 + 16 + n) * TC,
                             [[0, 128], [1, TC]]))])
                for k8 in range(8):
                    dt = half * 8 + k8
                    it = hn * 8 + k8
                    apar = it % 2
                    ipar = it % 2
                    b.wait("scalar", "SV", ctx["du_done"][dt])
                    if it >= 2:
                        b.wait("scalar", "SV", scan_ms.get(it - 2, 0))
                    elif flush is not None:
                        b.wait_tok("scalar", flush[it])
                    acol = aneg[:, dt * NS + n:dt * NS + n + 1]
                    dl_ = delta[dt]
                    da_s = b.emit("scalar", lambda e, apar=apar, acol=acol,
                                  dl_=dl_: e.activation(
                        dA2[apar][:], dl_[:], AF.Exp, scale=acol), inc="SS")
                    sud = sudu[dt]
                    b.wait_tok("vector", repl_ms[hn])
                    if it < 2 and flush is not None:
                        b.wait_tok("vector", flush[2 + it])
                    b.emit("vector", lambda e, sud=sud, par=par, ipar=ipar:
                           e.tensor_tensor(dbu2[ipar][:], sud[:],
                                           Bsl[par][:], OP.mult))
                    b.wait("vector", "SS", da_s)
                    if it >= 2:
                        b.wait("vector", "ST", scan_pe.get(it - 2, 0))
                    hinit = hstate[:, dt * NS + n:dt * NS + n + 1]
                    b.emit("vector", lambda e, apar=apar, hinit=hinit, ipar=ipar:
                           e.tensor_tensor_scan(hsl[:], dA2[apar][:],
                                                dbu2[ipar][:], hinit,
                                                OP.mult, OP.add))
                    chv = b.emit("vector", lambda e, par=par, ipar=ipar:
                                 e.tensor_tensor(chsl[ipar][:], hsl[:],
                                                 Csl[par][:], OP.mult), inc="SV")
                    sv1 = b.emit("vector", lambda e, hinit=hinit: e.tensor_copy(
                        hinit, hsl[:, TC - 1:TC]), inc="SV")
                    scan_ms[it] = sv1
                    if k8 == 7:
                        nlast_ms[hn] = sv1
                    b.wait("tensor", "SV", chv)
                    if n == 0 and half == 0 and bankfree is not None:
                        for sem_, val_ in bankfree[k8]:
                            b.wait("tensor", sem_, val_)
                    pe_ = b.emit("tensor", lambda e, k8=k8, ipar=ipar, n=n:
                                 e.matmul(pbank[k8][:], i128bf[:],
                                          chsl[ipar][:], start=(n == 0),
                                          stop=(n == NS - 1)), inc="ST")
                    scan_pe[it] = pe_
            for k8 in range(8):
                dt = half * 8 + k8
                dcol = dpar[:, dt:dt + 1]
                sub = subf[dt]; srb = sres[dt]; dl3 = y3b[dt]
                b.wait("vector", "ST", scan_pe[(half * NS + NS - 1) * 8 + k8])
                b.emit("vector", lambda e, sub=sub, dcol=dcol, k8=k8:
                       e.scalar_tensor_tensor(y2buf[:], sub[:], dcol,
                                              pbank[k8][:], OP.mult, OP.add))
                if "y2" in dbg:
                    yq = b.emit("vector", lambda e: e.memset(
                        onesr[0:1, 0:1], 1.0), inc="SV")
                    b.wait("sync", "SV", yq)
                    dq = b.ldg("sync", [(dbg["y2"][dt * 128:(dt + 1) * 128, tsl],
                                         y2buf[:])])
                    b.wait_tok("vector", dq)
                yev = b.emit("vector", lambda e, dl3=dl3, srb=srb:
                             e.tensor_tensor(dl3, y2buf[:], srb[:],
                                             OP.mult), inc="SV")
                y3_done[dt] = yev
        if "y3" in dbg:
            for dt in range(NDT):
                b.wait("gpsimd", "SV", y3_done[dt])
                b.emit("gpsimd", lambda e, dt=dt, tsl=tsl: e.dma_start(
                    out=dbg["y3"][dt * 128:(dt + 1) * 128, tsl],
                    in_=y3b[dt]), inc="DO2", amt=16)
        return dict(y3_done=y3_done, nlast=nlast_ms[2 * NS - 1],
                    wout_dg=wout_dg, wout_nop=wout_nop, load_wout=load_wout)

    def emit_m5(ctx, sc, nctx):
        c = ctx["c"]; tsl = ctx["tsl"]
        y3_done = sc["y3_done"]
        wout_dg = sc["wout_dg"]; wout_nop = sc["wout_nop"]
        b.wait("sync", "SV", sc["nlast"])
        xch_tok = b.ldg("sync", [
            (xch[k][:], di["xT"][k * 128:(k + 1) * 128, tsl])
            for k in range(NKM)])
        s_done = [None] * NKM
        b.wait("tensor", "SV", y3_done[NDT - 1])
        for dm in range(NKM):
            if nctx is not None and INTERLEAVE in (1, 2, 3):
                emit_m1_dt(nctx, 2 * dm)
                emit_m1_dt(nctx, 2 * dm + 1)
            b.wait_tok("tensor", wout_dg[dm])
            if DBGPOOL:
                for s_ in b._pool["gpsimd"]:
                    b.wait("tensor", s_, b.v(s_))
            if dm >= 2:
                b.wait("tensor", "SV", s_done[dm - 2])
            bank = pbank[4 + dm % 2]
            wslab = woutb[dm % 2]
            b.wait("tensor", "SV", y3_done[8 + 4 + dm % 2])
            for ki in range(NDT):
                wv = wslab[:, ki * 128:(ki + 1) * 128]
                dl3 = y3b[ki]
                last = ki == NDT - 1
                mm = b.emit("tensor", lambda e, wv=wv, dl3=dl3, ki=ki,
                            bank=bank: e.matmul(bank[:], wv, dl3,
                                                start=(ki == 0),
                                                stop=(ki == NDT - 1)),
                            inc="ST" if last else None)
                if last:
                    wout_nop[dm] = mm
            sc["load_wout"](dm + 2)
            b.wait("vector", "ST", wout_nop[dm])
            b.wait_tok("vector", xch_tok)
            if "sx" in dbg:
                b.wait_tok("sync", xch_tok)
                dsx = b.emit("sync", lambda e, dm=dm, tsl=tsl: e.dma_start(
                    out=dbg["sx"][dm * 128:(dm + 1) * 128, tsl],
                    in_=xch[dm][:]), inc="DO", amt=16)
                b.wait("vector", "DO", dsx)
            xc = xch[dm]
            s_done[dm] = b.emit("vector", lambda e, xc=xc, bank=bank:
                                e.tensor_tensor(xc[:], xc[:], bank[:], OP.add),
                                inc="SV")
        if "s0" in dbg:
            for dm in range(NKM):
                b.wait("sync", "SV", s_done[dm])
                b.emit("sync", lambda e, dm=dm, tsl=tsl: e.dma_start(
                    out=dbg["s0"][dm * 128:(dm + 1) * 128, tsl],
                    in_=xch[dm][:]), inc="DO", amt=16)
        mu_ms = None
        for dm in range(NKM):
            b.wait("tensor", "SV", s_done[dm])
            if dm == 0:
                b.wait("tensor", "SV", y3_done[14])
            xc = xch[dm]
            mu_ms = b.emit("tensor", lambda e, xc=xc, dm=dm: e.matmul(
                pbank[6][0:1, :], onesc[:], xc[:],
                start=(dm == 0), stop=(dm == NKM - 1)),
                inc="ST" if dm == NKM - 1 else None)
        sq_mm = [None] * NKM
        for dm in range(NKM):
            b.wait("scalar", "SV", s_done[dm])
            if dm >= 1:
                b.wait("scalar", "ST", sq_mm[dm - 1])
            xc = xch[dm]
            v = b.emit("scalar", lambda e, xc=xc: e.activation(
                sqs[:], xc[:], AF.Square), inc="SS")
            b.wait("tensor", "SS", v)
            if dm == 0:
                b.wait("tensor", "SV", y3_done[15])
            sq_mm[dm] = b.emit("tensor", lambda e, dm=dm: e.matmul(
                pbank[7][0:1, :], onescb[:], sqs[:],
                start=(dm == 0), stop=(dm == NKM - 1)), inc="ST")
        b.wait("scalar", "ST", mu_ms)
        b.emit("scalar", lambda e: e.activation(
            rowm[:], pbank[6][0:1, :], AF.Copy, scale=1.0 / DM))
        b.emit("scalar", lambda e: e.activation(rowv[:], rowm[:], AF.Square))
        b.wait("scalar", "ST", sq_mm[NKM - 1])
        var_s = b.emit("scalar", lambda e: e.activation(
            rowr[:], pbank[7][0:1, :], AF.Copy, scale=1.0 / DM), inc="SS")
        b.wait("vector", "SS", var_s)
        var_v = b.emit("vector", lambda e: e.tensor_tensor(
            rowv[:], rowr[:], rowv[:], OP.subtract), inc="SV")
        b.wait("scalar", "SV", var_v)
        b.emit("scalar", lambda e, bv=b.const_col(1e-5): e.activation(
            rowv[:], rowv[:], AF.Ln, bias=bv))
        rst_s = b.emit("scalar", lambda e: e.activation(
            rowv[:], rowv[:], AF.Exp, scale=-0.5), inc="SS")
        b.wait("tensor", "SS", rst_s)
        b.emit("tensor", lambda e: e.matmul(
            pbank[4][:], onesr[:], rowm[:], start=True, stop=True))
        bc_mm2 = b.emit("tensor", lambda e: e.matmul(
            pbank[5][:], onesr[:], rowv[:], start=True, stop=True), inc="ST")
        flush = [None] * NKM
        nv_last = None
        for dm in range(NKM):
            b.wait("vector", "ST", bc_mm2)
            xc = xch[dm]
            b.emit("vector", lambda e, xc=xc: e.tensor_tensor(
                xc[:], xc[:], pbank[4][:], OP.subtract))
            nv = b.emit("vector", lambda e, xc=xc: e.tensor_tensor(
                xc[:], xc[:], pbank[5][:], OP.mult), inc="SV")
            nv_last = nv
            cv = b.emit("vector", lambda e, xc=xc: e.tensor_copy(
                x1bf[:], xc[:]), inc="SV")
            b.wait("sync", "SV", cv)
            flush[dm] = b.ldg("sync", [
                (x1_scr[dm * 128:(dm + 1) * 128, tsl], xch[dm][:]),
                (x2_scr[dm * 128:(dm + 1) * 128,
                        1 + c * TC:1 + (c + 1) * TC], xch[dm][:]),
                (x1bf_scr[dm * 128:(dm + 1) * 128, 1 + c * TC:1 + (c + 1) * TC],
                 x1bf[:])])
            if "x1" in dbg:
                b.emit("sync", lambda e, dm=dm, tsl=tsl: e.dma_start(
                    out=dbg["x1"][dm * 128:(dm + 1) * 128, tsl],
                    in_=xch[dm][:]), inc="DO", amt=16)
            b.wait_tok("vector", flush[dm])
        if nctx is not None and INTERLEAVE and nctx["defer_xdbl"]:
            for dt2 in range(NDT):
                emit_xdbl(nctx, dt2)
        if nctx is not None and INTERLEAVE == 4:
            for dt2 in range(NDT):
                emit_m1_dt(nctx, dt2)
        if nctx is not None and INTERLEAVE >= 2:
            emit_m2(nctx, wout_nop[NKM - 1])
        if nctx is not None:
            nctx["flush"] = flush
            nctx["bankfree"] = {
                0: [("SV", nctx["conv"][NDT - 1]), ("SS", nctx["evu"][NDT - 1])],
                1: [("SS", nctx["res_s"][NDT - 1])],
                2: [("SS", nctx["bc_s"])],
                3: [("SS", nctx["sp"][NDT - 1])],
                4: [("SV", nv_last)], 5: [("SV", nv_last)],
                6: [("SS", var_s)], 7: [("SS", var_s)]}
        return flush[NKM - 1]

    ctx = new_m1_ctx(0, None)
    emit_xbf_load(ctx)
    load_win(ctx, 0)
    load_win(ctx, 1)
    for dt in range(NDT):
        emit_m1_dt(ctx, dt)
    emit_m2(ctx, None)
    prev_sc = None
    for c in range(NCH):
        sc = emit_scan(ctx, prev_sc)
        prev_sc = sc
        if c + 1 < NCH:
            nctx = new_m1_ctx(c + 1, sc["y3_done"])
            nctx["defer_xdbl"] = (INTERLEAVE == 3)
            b.wait("gpsimd", "ST", ctx["resmm"][NDT - 1])
            emit_xbf_load(nctx)
            load_win(nctx, 0)
            load_win(nctx, 1)
        else:
            nctx = None
        emit_m5(ctx, sc, nctx)
        if nctx is not None:
            if INTERLEAVE == 0:
                for dt in range(NDT):
                    emit_m1_dt(nctx, dt)
            if INTERLEAVE <= 1:
                emit_m2(nctx, sc["wout_nop"][NKM - 1])
            ctx = nctx

    mamba_ds = b.v("DS")

    def bar_all(dma=True):
        cur = {s: b.v(s) for s in ("SV", "SS", "ST", "SG")}
        if dma:
            b.wait("sync", "DS", b.v("DS"))
            b.wait("sync", "DO", b.v("DO"))
            for s in b._pool["sync"]:
                b.wait("sync", s, b.v(s))
            mks = b.emit("sync", lambda e: e.nop(), inc="MKS")
            for s in b._pool["gpsimd"]:
                b.wait("gpsimd", s, b.v(s))
            b.wait("gpsimd", "DO2", b.v("DO2"))
            mkg = b.emit("gpsimd", lambda e: e.nop(), inc="MKG")
            for eng in ENGS:
                b.wait(eng, "MKS", mks)
                b.wait(eng, "MKG", mkg)
        for eng in ENGS:
            for s, v in cur.items():
                b.wait(eng, s, v)

    b.free("aneg", "wx", "wdt")
    b.free(*[f"xbf{k}" for k in range(NKM)], "winb0", "winb1", "woutb0",
           "woutb1", "ubuf", "convA", "convB",
           *[f"sudu{i}" for i in range(NDT)], *[f"delta{i}" for i in range(NDT)],
           *[f"subf{i}" for i in range(NDT)], *[f"sres{i}" for i in range(NDT)],
           "y2buf", "dtbf", "bcsb", "dA0", "dA1",
           "dbu0", "dbu1", "hsl", "chsl0", "chsl1", "Bsl0", "Bsl1", "Csl0", "Csl1")
    bar_all()

    x2c = [A(f"x2c{dm}", 1 + TC) for dm in range(NKM)]
    onesP = b.sb("onesP", [128, 128])
    zcol = b.sb("zcol", [128, 1])

    b.emit("vector", lambda e: e.memset(onesP[:], 1.0), inc="SV")
    b.emit("vector", lambda e: e.memset(zcol[:], 0.0))
    mz = b.emit("vector", lambda e: e.memset(x1bf[:, 0:1], 0.0), inc="SV")
    b.wait("sync", "SV", mz)
    for dm in range(NKM):
        b.emit("sync", lambda e, dm=dm: e.dma_start(
            out=x1bf_scr[dm * 128:(dm + 1) * 128, 0:1], in_=x1bf[:, 0:1],
            allow_slow_non_contiguous=True), inc="DS", amt=16)
        b.emit("sync", lambda e, dm=dm: e.dma_start(
            out=x2bf_scr[dm * 128:(dm + 1) * 128, 0:1], in_=x1bf[:, 0:1],
            allow_slow_non_contiguous=True), inc="DS", amt=16)
        b.emit("sync", lambda e, dm=dm: e.dma_start(
            out=x2_scr[dm * 128:(dm + 1) * 128, 0:1], in_=zcol[:],
            allow_slow_non_contiguous=True), inc="DS", amt=16)
    b.wait("sync", "DS", mamba_ds)

    lnst = {"ss": 0, "sv": 0}

    def ln_stats_norm(s_views, out_emit, tag):
        b.wait("tensor", "SS", lnst["ss"])
        b.wait("tensor", "SV", lnst["sv"])
        mu_ms = None
        for dm in range(NKM):
            sv_ = s_views[dm]
            mu_ms = b.emit("tensor", lambda e, sv_=sv_, dm=dm: e.matmul(
                pbank[5][0:1, :], onesc[:], sv_,
                start=(dm == 0), stop=(dm == NKM - 1)),
                inc="ST" if dm == NKM - 1 else None)
        sq_mm = None
        for dm in range(NKM):
            sv_ = s_views[dm]
            if dm >= 1:
                b.wait("scalar", "ST", sq_mm)
            v = b.emit("scalar", lambda e, sv_=sv_: e.activation(
                sqs[:], sv_, AF.Square), inc="SS")
            b.wait("tensor", "SS", v)
            sq_mm = b.emit("tensor", lambda e, dm=dm: e.matmul(
                pbank[6][0:1, :], onescb[:], sqs[:],
                start=(dm == 0), stop=(dm == NKM - 1)), inc="ST")
        b.wait("scalar", "ST", mu_ms)
        b.emit("scalar", lambda e: e.activation(
            rowm[:], pbank[5][0:1, :], AF.Copy, scale=1.0 / DM))
        b.emit("scalar", lambda e: e.activation(rowv[:], rowm[:], AF.Square))
        b.wait("scalar", "ST", sq_mm)
        var_s = b.emit("scalar", lambda e: e.activation(
            rowr[:], pbank[6][0:1, :], AF.Copy, scale=1.0 / DM), inc="SS")
        b.wait("vector", "SS", var_s)
        var_v = b.emit("vector", lambda e: e.tensor_tensor(
            rowv[:], rowr[:], rowv[:], OP.subtract), inc="SV")
        b.wait("scalar", "SV", var_v)
        b.emit("scalar", lambda e, bv=b.const_col(1e-5): e.activation(
            rowv[:], rowv[:], AF.Ln, bias=bv))
        rst_s = b.emit("scalar", lambda e: e.activation(
            rowv[:], rowv[:], AF.Exp, scale=-0.5), inc="SS")
        b.wait("tensor", "SS", rst_s)
        b.emit("tensor", lambda e: e.matmul(
            pbank[7][:], onesr[:], rowm[:], start=True, stop=True))
        bc_mm2 = b.emit("tensor", lambda e: e.matmul(
            pbank[2][:], onesr[:], rowv[:], start=True, stop=True), inc="ST")
        for dm in range(NKM):
            b.wait("vector", "ST", bc_mm2)
            out_emit(dm, pbank[7][:], pbank[2][:])
        lnst["ss"] = b.v("SS")
        lnst["sv"] = b.v("SV")

    def ln_pipeline(groups, serial=False):
        n = len(groups)
        mu_l = [None] * n
        sq_l = [None] * n
        var_l = [None] * n
        rst_l = [None] * n
        bc2_l = [None] * n
        nv_l = [None] * n

        def stats(i):
            svs, _, prew = groups[i]
            if callable(prew):
                prew()
            else:
                for eng, sem, val in prew:
                    b.wait(eng, sem, val)
            if i >= 1:
                b.wait("tensor", "SS", var_l[i - 1])
            else:
                b.wait("tensor", "SS", lnst["ss"])
                b.wait("tensor", "SV", lnst["sv"])
            mu = None
            for dm in range(NKM):
                sv_ = svs[dm]
                mu = b.emit("tensor", lambda e, sv_=sv_, dm=dm: e.matmul(
                    pbank[5][0:1, :], onesc[:], sv_,
                    start=(dm == 0), stop=(dm == NKM - 1)),
                    inc="ST" if dm == NKM - 1 else None)
            mu_l[i] = mu
            sq = None
            for dm in range(NKM):
                sv_ = svs[dm]
                if dm >= 1:
                    b.wait("scalar", "ST", sq)
                elif i >= 1:
                    b.wait("scalar", "ST", sq_l[i - 1])
                v = b.emit("scalar", lambda e, sv_=sv_: e.activation(
                    sqs[:], sv_, AF.Square), inc="SS")
                b.wait("tensor", "SS", v)
                sq = b.emit("tensor", lambda e, dm=dm: e.matmul(
                    pbank[6][0:1, :], onescb[:], sqs[:],
                    start=(dm == 0), stop=(dm == NKM - 1)), inc="ST")
            sq_l[i] = sq

        def chain(i):
            b.wait("scalar", "ST", mu_l[i])
            if i >= 1:
                b.wait("scalar", "ST", bc2_l[i - 1])
            b.emit("scalar", lambda e: e.activation(
                rowm[:], pbank[5][0:1, :], AF.Copy, scale=1.0 / DM))
            b.emit("scalar", lambda e: e.activation(rowv[:], rowm[:], AF.Square))
            b.wait("scalar", "ST", sq_l[i])
            var_l[i] = b.emit("scalar", lambda e: e.activation(
                rowr[:], pbank[6][0:1, :], AF.Copy, scale=1.0 / DM), inc="SS")
            b.wait("vector", "SS", var_l[i])
            vv = b.emit("vector", lambda e: e.tensor_tensor(
                rowv[:], rowr[:], rowv[:], OP.subtract), inc="SV")
            b.wait("scalar", "SV", vv)
            b.emit("scalar", lambda e, bv=b.const_col(1e-5): e.activation(
                rowv[:], rowv[:], AF.Ln, bias=bv))
            rst_l[i] = b.emit("scalar", lambda e: e.activation(
                rowv[:], rowv[:], AF.Exp, scale=-0.5), inc="SS")

        def bcast_norm(i):
            _, out_emit, _ = groups[i]
            b.wait("tensor", "SS", rst_l[i])
            if i >= 1:
                b.wait("tensor", "SV", nv_l[i - 1])
            b.emit("tensor", lambda e: e.matmul(
                pbank[7][:], onesr[:], rowm[:], start=True, stop=True))
            bc2_l[i] = b.emit("tensor", lambda e: e.matmul(
                pbank[2][:], onesr[:], rowv[:], start=True, stop=True), inc="ST")
            for dm in range(NKM):
                b.wait("vector", "ST", bc2_l[i])
                out_emit(dm, pbank[7][:], pbank[2][:])
            nv_l[i] = b.v("SV")

        if serial:
            for i in range(n):
                stats(i)
                chain(i)
                bcast_norm(i)
        else:
            for i in range(n):
                stats(i)
                if i >= 1:
                    bcast_norm(i - 1)
                chain(i)
            bcast_norm(n - 1)
        lnst["ss"] = b.v("SS")
        lnst["sv"] = b.v("SV")

    def emit_ffn_layer(l):
        lw = b.v("DW")
        wqb = A("wqb", 8 * 512, BF16)
        wkb = A("wkb", 8 * 512, BF16)
        wvb = A("wvb", 8 * 512, BF16)
        tbA = A("tbA", L); tbB = A("tbB", L)
        qt = [A(f"qt{i}", L, BF16) for i in range(4)]
        kt = [A(f"kt{i}", L, BF16) for i in range(4)]
        vt = [A(f"vt{i}", 8 * 65, BF16) for i in range(16)]
        xin = [A(f"xin{k}", 1 + TC, BF16) for k in range(NKM)]
        qtmp = A("qtmp", TC); rq1 = A("rq1", TC); rq2 = A("rq2", TC)
        bar_all()
        qkv_tok = b.ldg("gpsimd", [
            (dst.rearrange("p (ki cc) -> p ki cc", ki=NKM),
             di[src_].rearrange("(ki p) cc -> p ki cc", p=128))
            for dst, src_ in ((wqb, f"wq_{l}"), (wkb, f"wk_{l}"), (wvb, f"wv_{l}"))])
        for dm in range(16):
            b.emit("vector", lambda e, dm=dm: e.memset(
                vt[dm][:].rearrange("p (h c) -> p h c", h=H)[:, :, 64:65], 1.0))
        vones_v = b.emit("vector", lambda e: e.memset(onesP[0:1, 0:1], 1.0), inc="SV")

        xsrc = x1bf_scr if l == 0 else x2bf_scr

        def qk_build(which, wslab, tgt, tb_src_a, tb_src_b, sc2, sc2r):
            b.wait("sync", "SV", b.v("SV"))
            b.wait("sync", "SS", b.v("SS"))
            tb_tok = b.ldg("sync", [(tbA[:], di[tb_src_a]),
                                    (tbB[:], di[tb_src_b])])
            b.wait_tok("scalar", tb_tok)
            b.emit("scalar", lambda e: e.activation(
                tbA[:], tbA[:], AF.Copy, scale=di_sc[sc2]))
            fold_s = b.emit("scalar", lambda e: e.activation(
                tbB[:], tbB[:], AF.Copy, scale=di_sc[sc2r]), inc="SS")
            for fc in range(NCH):
                t0, t1 = fc * TC, (fc + 1) * TC
                b.wait("sync", "ST", b.v("ST"))
                b.wait("sync", "DS", b.v("DS"))
                xin_tok = b.ldg("sync", [
                    (xin[k][:],
                     xsrc[k * 128:(k + 1) * 128, fc * TC:fc * TC + 1 + TC])
                    for k in range(NKM)])
                for fh in range(4):
                    bank = pbank[fh % 2]
                    b.wait_tok("tensor", qkv_tok)
                    b.wait_tok("tensor", xin_tok)
                    g = (qk_ev.get((which, fc, fh - 2))
                         or qk_ev.get((which, fc - 1, fh + 2))
                         or (qk_ev.get(("q", NCH - 1, fh + 2)) if which == "k"
                             else None) or 0)
                    b.wait("tensor", "SV", g)
                    if SERIAL & 1:
                        for _e in ENGS:
                            for _s in ("SV", "SS", "ST"):
                                b.wait(_e, _s, b.v(_s))
                    mlast = None
                    for ki in range(NKM):
                        wv_ = wslab[:, ki * 512 + fh * 128:ki * 512 + fh * 128 + 128]
                        rhs = xin[ki][:, 1:1 + TC] if ki < 4 else xin[ki][:, 0:TC]
                        mlast = b.emit("tensor", lambda e, wv_=wv_, rhs=rhs, ki=ki,
                                       bank=bank: e.matmul(
                            bank[:], wv_, rhs, start=(ki == 0), stop=(ki == NKM - 1)),
                            inc="ST" if ki == NKM - 1 else None)
                    b.wait("scalar", "ST", mlast)
                    ev = b.emit("scalar", lambda e, bank=bank: e.activation(
                        qtmp[:], bank[:], AF.Copy), inc="SS")
                    if l == DUMP_LAYER and which == "q" and "qraw" in dbg:
                        b.wait("sync", "SS", ev)
                        dq = b.emit("sync", lambda e, fh=fh, fc=fc: e.dma_start(
                            out=dbg["qraw"][fh * 128:(fh + 1) * 128,
                                            fc * TC:(fc + 1) * TC],
                            in_=qtmp[:]), inc="DO", amt=16)
                        b.wait("scalar", "DO", dq)
                        b.wait("vector", "DO", dq)
                    b.wait("tensor", "SS", ev)
                    rot_mm = b.emit("tensor", lambda e: e.matmul(
                        pbank[2][:], r2f[:], qtmp[:], start=True, stop=True),
                        inc="ST")
                    sq_s = b.emit("scalar", lambda e: e.activation(
                        sqs[:], qtmp[:], AF.Square), inc="SS")
                    b.wait("tensor", "SS", sq_s)
                    ss_mm = b.emit("tensor", lambda e: e.matmul(
                        pbank[3][0:2, :], bl2[:], sqs[:], start=True, stop=True),
                        inc="ST")
                    b.wait("scalar", "ST", ss_mm)
                    b.emit("scalar", lambda e, bv=b.const_col(1e-24): e.activation(
                        row2[:], pbank[3][0:2, :], AF.Ln, bias=bv))
                    rb_s = b.emit("scalar", lambda e: e.activation(
                        row2[:], row2[:], AF.Exp, scale=-0.5), inc="SS")
                    b.wait("tensor", "SS", rb_s)
                    rb_mm = b.emit("tensor", lambda e: e.matmul(
                        pbank[4][:], headsel[:], row2[:], start=True, stop=True),
                        inc="ST")
                    b.wait("vector", "SS", fold_s)
                    b.wait("vector", "ST", rb_mm)
                    tga = tgt[fh]
                    b.emit("vector", lambda e, t0=t0, t1=t1: e.tensor_tensor(
                        rq1[:], qtmp[:], tbA[:, t0:t1], OP.mult))
                    b.emit("vector", lambda e, t0=t0, t1=t1: e.tensor_tensor(
                        rq2[:], pbank[2][:], tbB[:, t0:t1], OP.mult))
                    b.emit("vector", lambda e: e.tensor_tensor(
                        rq1[:], rq1[:], rq2[:], OP.add))
                    qk_ev[(which, fc, fh)] = b.emit(
                        "vector", lambda e, tga=tga, t0=t0, t1=t1: e.tensor_tensor(
                            tga[:, t0:t1], rq1[:], pbank[4][:], OP.mult), inc="SV")
                    b.wait("tensor", "SV", qk_ev[(which, fc, fh)])
                    b.wait("scalar", "SV", qk_ev[(which, fc, fh)])

        qk_ev = {}
        qscol = b.sb(f"qscol{l}", [128, 4])
        sc_tok = b.ldg("sync", [
            (qscol[:, 0:1], di[f"qs2_{l}"]), (qscol[:, 1:2], di[f"qs2r_{l}"]),
            (qscol[:, 2:3], di[f"ks2_{l}"]), (qscol[:, 3:4], di[f"ks2r_{l}"])])
        di_sc = {f"qs2_{l}": qscol[:, 0:1], f"qs2r_{l}": qscol[:, 1:2],
                 f"ks2_{l}": qscol[:, 2:3], f"ks2r_{l}": qscol[:, 3:4]}
        b.wait_tok("scalar", sc_tok)
        qk_build("q", wqb, qt, "tab_csq", "tab_ssq", f"qs2_{l}", f"qs2r_{l}")
        if l == DUMP_LAYER and "qt" in dbg:
            for i4 in range(4):
                b.wait("gpsimd", "SV", b.v("SV"))
                b.emit("gpsimd", lambda e, i4=i4: e.dma_start(
                    out=dbg["qt"][i4 * 128:(i4 + 1) * 128, :], in_=qt[i4][:]),
                    inc="DO2", amt=16)
        qk_build("k", wkb, kt, "tab_csk", "tab_ssk", f"ks2_{l}", f"ks2r_{l}")
        if l == DUMP_LAYER and "kt" in dbg:
            for i4 in range(4):
                b.wait("gpsimd", "SV", b.v("SV"))
                b.emit("gpsimd", lambda e, i4=i4: e.dma_start(
                    out=dbg["kt"][i4 * 128:(i4 + 1) * 128, :], in_=kt[i4][:]),
                    inc="DO2", amt=16)
        v_ev = {}
        for fc in range(NCH):
            b.wait("sync", "ST", b.v("ST"))
            b.wait("sync", "DS", b.v("DS"))
            xin_tok = b.ldg("sync", [
                (xin[k][:],
                 xsrc[k * 128:(k + 1) * 128, fc * TC:fc * TC + 1 + TC])
                for k in range(NKM)])
            for jt in range(4):
                tix = fc * 4 + jt
                bank = pbank[tix % 2]
                b.wait_tok("tensor", xin_tok)
                b.wait("tensor", "SS", v_ev.get(tix - 2, 0))
                if tix < 2:
                    b.wait("tensor", "SV", qk_ev.get(("k", NCH - 1, 2 + tix), 0))
                mlast = None
                for ki in range(NKM):
                    lhs = (xin[ki][:, 1 + jt * 128:1 + (jt + 1) * 128] if ki < 4
                           else xin[ki][:, jt * 128:(jt + 1) * 128])
                    rhs = wvb[:, ki * 512:(ki + 1) * 512]
                    mlast = b.emit("tensor", lambda e, lhs=lhs, rhs=rhs, ki=ki,
                                   bank=bank: e.matmul(
                        bank[:], lhs, rhs, start=(ki == 0), stop=(ki == NKM - 1)),
                        inc="ST" if ki == NKM - 1 else None)
                b.wait("scalar", "ST", mlast)
                b.wait("scalar", "SV", vones_v)
                vtile = vt[tix]
                v_ev[tix] = b.emit("scalar", lambda e, vtile=vtile, bank=bank:
                                   e.activation(
                    vtile[:].rearrange("p (h c) -> p h c", h=H)[:, :, 0:64],
                    bank[:].rearrange("p (h c) -> p h c", h=H), AF.Copy),
                    inc="SS")
        bar_all(dma=False)
        b.free("wqb", "wkb", "wvb", "tbA", "tbB",
               *[f"xin{k}" for k in range(NKM)], "qtmp", "rq1", "rq2")

        ptsl = [A(f"ptsl{j}", 1536, BF16) for j in range(16)]
        for j in range(16):
            b.emit("vector", lambda e, j=j: e.memset(ptsl[j][:], 0.0))
        b.emit("vector", lambda e: e.memset(onesP[0:1, 0:1], 1.0), inc="SV")
        otb = [A(f"ot{h}", L, BF16, rows=64) for h in range(H)]
        pij = [A(f"pij{p}", 640, BF16) for p in range(2)]
        oscr = A("oscr", TC)
        den = A("den", TC)
        nmAB = b.sb(f"nmAB{l}", [128, 4])
        bar_all(dma=False)
        exp_ms = {}
        red_ms = {}
        m1_ms = {}
        tr_ms = {}
        ev_ms = {}
        avmm_ms = {}
        avml_ms = {}

        def geo(it):
            I0 = it * 128
            jlo = max(0, I0 - 512)
            return I0, jlo, I0 + 128 - jlo

        def sbanks(it):
            return (pbank[0], pbank[1]) if it % 2 == 0 else (pbank[2], pbank[3])

        def emit_sim(h, it):
            I0, jlo, w = geo(it)
            par = it % 2
            bl_, bh_ = sbanks(it)
            qsl = qt[h // 2]
            ksl = kt[h // 2]
            fhrow = (h % 2) * 64
            if it >= 2:
                b.wait("tensor", "SS", exp_ms[(h, it - 2)])
            elif h > 0:
                b.wait("tensor", "SS", exp_ms[(h - 1, 14 + par)])
            lhsT = qsl[fhrow:fhrow + 64, I0:I0 + 128]
            m1 = b.emit("tensor", lambda e, lhsT=lhsT, ksl=ksl, jlo=jlo, w=w,
                        fhrow=fhrow, bl_=bl_: e.matmul(
                bl_[:, 0:min(w, 512)], lhsT,
                ksl[fhrow:fhrow + 64, jlo:jlo + min(w, 512)],
                start=True, stop=True), inc="ST")
            if w > 512:
                m1 = b.emit("tensor", lambda e, lhsT=lhsT, ksl=ksl, jlo=jlo, w=w,
                            fhrow=fhrow, bh_=bh_: e.matmul(
                    bh_[:, 0:w - 512], lhsT,
                    ksl[fhrow:fhrow + 64, jlo + 512:jlo + w],
                    start=True, stop=True), inc="ST")
            m1_ms[(h, it)] = m1

        def emit_rowmax(h, it):
            I0, jlo, w = geo(it)
            par = it % 2
            bl_, bh_ = sbanks(it)
            nmA = nmAB[:, 2 * par:2 * par + 1]
            nmB = nmAB[:, 2 * par + 1:2 * par + 2]
            b.wait("vector", "ST", m1_ms[(h, it)])
            if I0 >= 512:
                b.emit("vector", lambda e, bl_=bl_: e.tensor_tensor(
                    bl_[:, 0:128], bl_[:, 0:128], mfij[:], OP.add))
            if w > 512:
                b.emit("vector", lambda e, bh_=bh_, w=w: e.tensor_tensor(
                    bh_[:, w - 512 - 128:w - 512], bh_[:, w - 512 - 128:w - 512],
                    mnij[:], OP.add))
                b.emit("vector", lambda e, bl_=bl_, nmA=nmA: e.tensor_reduce(
                    nmA, bl_[:, 0:512], AX.X, OP.max, negate=True))
                b.emit("vector", lambda e, bh_=bh_, w=w, nmB=nmB: e.tensor_reduce(
                    nmB, bh_[:, 0:w - 512], AX.X, OP.max, negate=True))
                red = b.emit("vector", lambda e, nmA=nmA, nmB=nmB: e.tensor_tensor(
                    nmA, nmA, nmB, OP.min), inc="SV")
            else:
                b.emit("vector", lambda e, bl_=bl_, w=w: e.tensor_tensor(
                    bl_[:, w - 128:w], bl_[:, w - 128:w], mnij[:], OP.add))
                red = b.emit("vector", lambda e, bl_=bl_, w=w, nmA=nmA:
                             e.tensor_reduce(
                    nmA, bl_[:, 0:w], AX.X, OP.max, negate=True), inc="SV")
            red_ms[(h, it)] = red

        def emit_exp(h, it):
            I0, jlo, w = geo(it)
            par = it % 2
            bl_, bh_ = sbanks(it)
            nmA = nmAB[:, 2 * par:2 * par + 1]
            pj = pij[par]
            b.wait("scalar", "SV", red_ms[(h, it)])
            if it >= 2:
                b.wait("scalar", "ST", tr_ms[(h, it - 2)])
            elif h > 0:
                b.wait("scalar", "ST", tr_ms[(h - 1, 14 + par)])
            if w > 512:
                b.emit("scalar", lambda e, pj=pj, bh_=bh_, w=w, nmA=nmA:
                       e.activation(pj[:, 512:w], bh_[:, 0:w - 512], AF.Exp,
                                    bias=nmA), inc="SS")
            ex = b.emit("scalar", lambda e, pj=pj, bl_=bl_, w=w, nmA=nmA:
                        e.activation(pj[:, 0:min(w, 512)], bl_[:, 0:min(w, 512)],
                                     AF.Exp, bias=nmA), inc="SS")
            exp_ms[(h, it)] = ex

        def emit_trans(h, it):
            I0, jlo, w = geo(it)
            par = it % 2
            tb = pbank[4] if par == 0 else pbank[6]
            pj = pij[par]
            b.wait("tensor", "SS", exp_ms[(h, it)])
            if it >= 2:
                b.wait("tensor", "SS", ev_ms[(h, it - 2)])
            elif h > 0:
                b.wait("tensor", "SS", ev_ms[(h - 1, 14 + par)])
            ntile = w // 128
            tr = None
            for k in range(ntile):
                dst = (tb[:, k * 128:(k + 1) * 128] if k < 4
                       else pbank[5][:, par * 128:par * 128 + 128])
                tr = b.emit("tensor", lambda e, dst=dst, pj=pj, k=k: e.matmul(
                    dst, pj[:, k * 128:(k + 1) * 128], i128bf[:],
                    start=True, stop=True),
                    inc="ST" if k == ntile - 1 else None)
            tr_ms[(h, it)] = tr

        def emit_evac(h, it):
            I0, jlo, w = geo(it)
            par = it % 2
            tb = pbank[4] if par == 0 else pbank[6]
            b.wait("scalar", "ST", tr_ms[(h, it)])
            if h > 0 and it in (0, 4, 8):
                b.wait("scalar", "ST", avmm_ms[(h - 1, 1 + it // 4)])
            ntile = w // 128
            ev = None
            for k in range(ntile):
                J = jlo // 128 + k
                src = (tb[:, k * 128:(k + 1) * 128] if k < 4
                       else pbank[5][:, par * 128:par * 128 + 128])
                c = 512 + I0 - J * 128
                ptile = ptsl[J]
                ev = b.emit("scalar", lambda e, ptile=ptile, src=src, c=c:
                            e.activation(ptile[:, c:c + 128], src, AF.Copy),
                            inc="SS" if k == ntile - 1 else None)
            ev_ms[(h, it)] = ev

        def emit_av(h, ib, wtok):
            IB0 = ib * 512
            Jlo = max(0, (IB0 - 512) // 128)
            Jhi = (IB0 + 511) // 128
            b7 = pbank[7]
            prev = avml_ms.get((h, ib - 1), avml_ms.get((h - 1, 3), 0))
            b.wait("tensor", "SS", wtok)
            b.wait("tensor", "SV", prev)
            segs2 = list(range(Jlo, Jhi + 1))
            mlast = None
            for idx, J in enumerate(segs2):
                J0 = J * 128
                vtile = vt[J]
                ptile = ptsl[J]
                off = 512 + IB0 - J0
                mlast = b.emit("tensor", lambda e, vtile=vtile, ptile=ptile,
                               h=h, off=off, b7=b7, fj=(idx == 0),
                               lj=(idx == len(segs2) - 1): e.matmul(
                    b7[0:65, :], vtile[:, h * 65:h * 65 + 65],
                    ptile[:, off:off + 512], start=fj, stop=lj),
                    inc="ST" if idx == len(segs2) - 1 else None)
            avmm_ms[(h, ib)] = mlast
            b.wait("scalar", "ST", mlast)
            b.wait("scalar", "SV", prev)
            b.emit("scalar", lambda e, b7=b7: e.activation(
                den[64:65, :], b7[64:65, :], AF.Ln))
            b.emit("scalar", lambda e: e.activation(
                oscr[64:65, :], den[64:65, :], AF.Exp, scale=-1.0), inc="SS")
            ev_s = b.emit("scalar", lambda e, b7=b7: e.activation(
                oscr[0:64, :], b7[0:64, :], AF.Copy), inc="SS")
            b.wait("tensor", "SS", ev_s)
            dn_mm = b.emit("tensor", lambda e, b7=b7: e.matmul(
                b7[0:64, :], onesP[64:65, 0:64], oscr[64:65, :],
                start=True, stop=True), inc="ST")
            b.wait("vector", "ST", dn_mm)
            oth = otb[h]
            avml_ms[(h, ib)] = b.emit("vector", lambda e, oth=oth, IB0=IB0,
                                      b7=b7: e.tensor_tensor(
                oth[:, IB0:IB0 + 512], oscr[0:64, :], b7[0:64, :], OP.mult),
                inc="SV")

        for h in range(H):
            if SERIAL & 2:
                for _e in ENGS:
                    for _s in ("SV", "SS", "ST"):
                        b.wait(_e, _s, b.v(_s))
            for it in range(17):
                if it < 16:
                    emit_sim(h, it)
                    emit_rowmax(h, it)
                if it >= 1:
                    emit_trans(h, it - 1)
                    emit_evac(h, it - 1)
                if it < 16:
                    emit_exp(h, it)
                if it == 1 and h > 0:
                    emit_av(h - 1, 2, ev_ms[(h, 0)])
                elif it == 2 and h > 0:
                    emit_av(h - 1, 3, ev_ms[(h, 1)])
                elif it == 10:
                    emit_av(h, 0, ev_ms[(h, 8)])
                elif it == 14:
                    emit_av(h, 1, ev_ms[(h, 12)])
        d1 = b.emit("scalar", lambda e: e.activation(
            onesP[0:1, 0:1], onesP[0:1, 0:1], AF.Copy), inc="SS")
        emit_av(H - 1, 2, d1)
        d2 = b.emit("scalar", lambda e: e.activation(
            onesP[0:1, 0:1], onesP[0:1, 0:1], AF.Copy), inc="SS")
        emit_av(H - 1, 3, d2)
        if l == DUMP_LAYER and "ot" in dbg:
            for h2 in range(H):
                b.wait("gpsimd", "SV", b.v("SV"))
                b.emit("gpsimd", lambda e, h2=h2: e.dma_start(
                    out=dbg["ot"][h2 * 64:(h2 + 1) * 64, :], in_=otb[h2][0:64, :]),
                    inc="DO2", amt=16)
        bar_all(dma=False)
        b.free(*[f"ptsl{j}" for j in range(16)], "oscr", "den", "pij0", "pij1",
               *[f"qt{i}" for i in range(4)], *[f"kt{i}" for i in range(4)],
               *[f"vt{i}" for i in range(16)])

        woB = A("woB", H * DM, BF16, rows=64)
        hlnb = [A(f"hlnb{dm}", L, BF16) for dm in range(NKM)]
        fs1 = A("fs1", TC)
        x2full = [A(f"x2f{dm}", 1 + L) for dm in range(NKM)]
        wo_tok = b.ldg("gpsimd", [(
            woB.rearrange("p (hh cc) -> p hh cc", hh=H),
            di[f"wo_s_{l}"].rearrange("p (hh cc) -> p hh cc", hh=H))])
        lng = b.sb(f"lng{l}", [128, NKM]); lnb = b.sb(f"lnb{l}", [128, NKM])
        lnp_tok = b.ldg("sync", [(lng[:], di[f"lng_c_{l}"]),
                                 (lnb[:], di[f"lnb_c_{l}"])])
        bar_all(dma=False)
        b.wait("sync", "DS", b.v("DS"))
        x2f_tok = b.ldg("sync", [(
            x2full[dm][:], x2_scr[dm * 128:(dm + 1) * 128, 0:1 + L])
            for dm in range(NKM)])
        res_ms = {}
        for dm in range(NKM):
            bb = (dm % 2) * 4
            b.wait_tok("tensor", wo_tok)
            for fc in range(NCH):
                b.wait("tensor", "SV", res_ms.get((dm - 2, fc), 0))
            mlast = [None] * NCH
            for h in range(H):
                oth = otb[h]
                wv_ = woB[0:64, h * DM + dm * 128:h * DM + (dm + 1) * 128]
                for fc in range(NCH):
                    mlast[fc] = b.emit("tensor", lambda e, wv_=wv_, oth=oth,
                                       fc=fc, bb=bb, h=h: e.matmul(
                        pbank[bb + fc][:], wv_, oth[:, fc * TC:(fc + 1) * TC],
                        start=(h == 0), stop=(h == H - 1)),
                        inc="ST" if h == H - 1 else None)
            xv = x2full[dm]
            for fc in range(NCH):
                b.wait("vector", "ST", mlast[fc])
                if dm == 0 and fc == 0:
                    b.wait_tok("vector", x2f_tok)
                res_ms[(dm, fc)] = b.emit("vector", lambda e, xv=xv, fc=fc,
                                          bb=bb: e.tensor_tensor(
                    xv[:, 1 + fc * TC:1 + (fc + 1) * TC],
                    xv[:, 1 + fc * TC:1 + (fc + 1) * TC],
                    pbank[bb + fc][:], OP.add), inc="SV")
        groups = []
        for fc in range(NCH):
            t0 = 1 + fc * TC
            prew = []
            for dm in range(NKM):
                prew.append(("tensor", "SV", res_ms[(dm, fc)]))
                prew.append(("scalar", "SV", res_ms[(dm, fc)]))
            svs = [x2full[dm][:, t0:t0 + TC] if dm < 4
                   else x2full[dm][:, t0 - 1:t0 - 1 + TC]
                   for dm in range(NKM)]

            def emit_hln(dm, mu_b, rs_b, svs=svs, fc=fc):
                b.emit("vector", lambda e, dm=dm: e.tensor_tensor(
                    fs1[:], svs[dm], mu_b, OP.subtract))
                nv = b.emit("vector", lambda e, dm=dm: e.tensor_tensor(
                    fs1[:], fs1[:], rs_b, OP.mult), inc="SV")
                b.wait_tok("vector", lnp_tok)
                hv = b.emit("vector", lambda e, dm=dm, fc=fc: e.tensor_scalar(
                    hlnb[dm][:, fc * TC:(fc + 1) * TC], fs1[:],
                    lng[:, dm:dm + 1], lnb[:, dm:dm + 1], OP.mult, OP.add),
                    inc="SV")
            groups.append((svs, emit_hln, prew))
        ln_pipeline(groups)
        b.wait("sync", "SV", b.v("SV"))
        x2st_tok = b.ldg("sync", [(
            x2_scr[dm * 128:(dm + 1) * 128, 1:1 + L], x2full[dm][:, 1:1 + L])
            for dm in range(NKM)])
        b.wait_tok("sync", x2st_tok)
        bar_all(dma=False)
        b.free(*[f"ot{h}" for h in range(H)], "woB", "fs1",
               *[f"x2f{dm}" for dm in range(NKM)])

        w1b = [A(f"w1b{p}", 8 * 256, BF16) for p in range(2)]
        hmid = [A(f"hmid{j}", L, BF16) for j in range(NPAIR)]
        gact = [A(f"gact{p}", TC) for p in range(2)]
        b1c = b.sb(f"b1c{l}s", [128, 2 * NPAIR])
        b1_tok = b.ldg("sync", [(b1c[:], di[f"b1c_{l}"])])
        b.wait_tok("scalar", b1_tok)
        bar_all(dma=False)
        w1_dg = [None] * NPAIR
        w1_last = [None] * NPAIR

        def load_w1(j):
            if j >= NPAIR:
                return
            if j >= 2:
                b.wait("gpsimd", "ST", w1_last[j - 2])
            w1_dg[j] = b.ldg("gpsimd", [(
                w1b[j % 2].rearrange("p (ki cc) -> p ki cc", ki=NKM),
                di[f"w1_s_{l}"][j * DM:(j + 1) * DM, :]
                .rearrange("(ki p) cc -> p ki cc", p=128))])
        load_w1(0)
        load_w1(1)
        gg_ms = {}
        gs_ms = {}
        for j in range(NPAIR):
            wslab = w1b[j % 2]
            b.wait_tok("tensor", w1_dg[j])
            ml_a = [None] * NCH
            ml_g = [None] * NCH
            for fc in range(NCH):
                b.wait("tensor", "SV", gg_ms.get((j - 1, fc), 0))
                b.wait("tensor", "SS", gs_ms.get((j - 1, fc), 0))
                for ki in range(NKM):
                    wa = wslab[:, ki * 256:ki * 256 + 128]
                    hk = hlnb[ki]
                    ml_a[fc] = b.emit("tensor", lambda e, wa=wa, hk=hk, fc=fc,
                                      ki=ki: e.matmul(
                        pbank[fc][:], wa, hk[:, fc * TC:(fc + 1) * TC],
                        start=(ki == 0), stop=(ki == NKM - 1)),
                        inc="ST" if ki == NKM - 1 else None)
                for ki in range(NKM):
                    wg = wslab[:, ki * 256 + 128:ki * 256 + 256]
                    hk = hlnb[ki]
                    ml_g[fc] = b.emit("tensor", lambda e, wg=wg, hk=hk, fc=fc,
                                      ki=ki: e.matmul(
                        pbank[4 + fc][:], wg, hk[:, fc * TC:(fc + 1) * TC],
                        start=(ki == 0), stop=(ki == NKM - 1)),
                        inc="ST" if ki == NKM - 1 else None)
                b.wait("scalar", "ST", ml_g[fc])
                if fc >= 2 or j > 0:
                    b.wait("scalar", "SV",
                           gg_ms.get((j, fc - 2), gg_ms.get((j - 1, NCH - 2 + fc), 0)))
                gc = gact[fc % 2]
                g_s = b.emit("scalar", lambda e, j=j, fc=fc, gc=gc: e.activation(
                    gc[:], pbank[4 + fc][:], AF.Gelu,
                    bias=b1c[:, NPAIR + j:NPAIR + j + 1]), inc="SS")
                gs_ms[(j, fc)] = g_s
                b.wait("vector", "ST", ml_a[fc])
                b.wait("vector", "SS", g_s)
                hm = hmid[j]
                gg_ms[(j, fc)] = b.emit("vector", lambda e, hm=hm, j=j, fc=fc,
                                        gc=gc: e.scalar_tensor_tensor(
                    hm[:, fc * TC:(fc + 1) * TC], pbank[fc][:], b1c[:, j:j + 1],
                    gc[:], OP.add, OP.mult), inc="SV")
            w1_last[j] = ml_g[NCH - 1]
            load_w1(j + 2)
        if l == DUMP_LAYER and "hmid" in dbg:
            for j2 in range(NPAIR):
                b.wait("gpsimd", "SV", b.v("SV"))
                b.emit("gpsimd", lambda e, j2=j2: e.dma_start(
                    out=dbg["hmid"][j2 * 128:(j2 + 1) * 128, :], in_=hmid[j2][:]),
                    inc="DO2", amt=16)
        bar_all(dma=False)
        b.free("w1b0", "w1b1", *[f"hlnb{dm}" for dm in range(NKM)],
               "gact0", "gact1")
        w2bA = A("w2bA", 11 * DM, BF16)
        w2bB = A("w2bB", 11 * DM, BF16)
        w2_tok = b.ldg("gpsimd", [
            (w2bA.rearrange("p (ki cc) -> p ki cc", ki=11),
             di[f"w2_s_{l}"][0:11 * 128, :].rearrange("(ki p) cc -> p ki cc", p=128)),
            (w2bB.rearrange("p (ki cc) -> p ki cc", ki=11),
             di[f"w2_s_{l}"][11 * 128:, :].rearrange("(ki p) cc -> p ki cc",
                                                     p=128))])
        w2b_tok = w2_tok

        x2row = [A(f"x2row{p}", L) for p in range(2)]
        x2bfrow = A("x2bfrow", L, BF16)
        b2c = b.sb(f"b2c{l}s", [128, NKM])
        b2_tok = b.ldg("sync", [(b2c[:], di[f"b2c_{l}"])])
        bar_all(dma=False)
        b.wait_tok("vector", b2_tok)
        b.wait("sync", "DS", b.v("DS"))
        r2_ms = {}
        x2r_tok = [None] * NKM
        st_toks = {}
        x2bf_cv = {}
        x2r_tok[0] = b.ldg("sync", [(
            x2row[0][:], x2_scr[0:128, 1:1 + L])])
        if NKM > 1:
            x2r_tok[1] = b.ldg("sync", [(
                x2row[1][:], x2_scr[128:256, 1:1 + L])])
        for dm in range(NKM):
            bb = (dm % 2) * 4
            b.wait_tok("tensor", w2_tok)
            b.wait_tok("tensor", w2b_tok)
            for fc in range(NCH):
                b.wait("tensor", "SV", r2_ms.get((dm - 2, fc), 0))
            mlast = [None] * NCH
            for ki in range(NPAIR):
                wsl2 = w2bA if ki < 11 else w2bB
                kio = ki if ki < 11 else ki - 11
                wv_ = wsl2[:, kio * DM + dm * 128:kio * DM + (dm + 1) * 128]
                hm = hmid[ki]
                for fc in range(NCH):
                    mlast[fc] = b.emit("tensor", lambda e, wv_=wv_, hm=hm, fc=fc,
                                       ki=ki, bb=bb: e.matmul(
                        pbank[bb + fc][:], wv_, hm[:, fc * TC:(fc + 1) * TC],
                        start=(ki == 0), stop=(ki == NPAIR - 1)),
                        inc="ST" if ki == NPAIR - 1 else None)
            xr = x2row[dm % 2]
            for fc in range(NCH):
                b.wait("vector", "ST", mlast[fc])
                if fc == 0:
                    b.wait_tok("vector", x2r_tok[dm])
                r2_ms[(dm, fc)] = b.emit("vector", lambda e, xr=xr, dm=dm,
                                         fc=fc, bb=bb: e.scalar_tensor_tensor(
                    xr[:, fc * TC:(fc + 1) * TC], pbank[bb + fc][:],
                    b2c[:, dm:dm + 1], xr[:, fc * TC:(fc + 1) * TC],
                    OP.add, OP.add), inc="SV")
            b.wait("sync", "SV", r2_ms[(dm, NCH - 1)])
            st_tok = b.ldg("sync", [(
                x2_scr[dm * 128:(dm + 1) * 128, 1:1 + L], xr[:])])
            st_toks[dm] = st_tok
            if l == 0:
                if dm >= 1:
                    b.wait("vector", "SV", x2bf_cv.get(dm - 1, 0))
                cv = b.emit("vector", lambda e, xr=xr: e.tensor_copy(
                    x2bfrow[:], xr[:]), inc="SV")
                x2bf_cv[dm] = cv
                b.wait("sync", "SV", cv)
                sd = b.ldg("sync", [(
                    x2bf_scr[dm * 128:(dm + 1) * 128, 1:1 + L], x2bfrow[:])])
                b.wait_tok("vector", sd)
            if dm + 2 < NKM:
                b.wait_tok("sync", st_tok)
                if l == 0:
                    b.wait("sync", "SV", cv)
                x2r_tok[dm + 2] = b.ldg("sync", [(
                    x2row[(dm + 2) % 2][:],
                    x2_scr[(dm + 2) * 128:(dm + 3) * 128, 1:1 + L])])
        bar_all()
        b.free("w2bA", "w2bB", "x2row0", "x2row1", "x2bfrow",
               *[f"hmid{j}" for j in range(NPAIR)])
        bar_all()

    for l in range(2):
        emit_ffn_layer(l)

    fin = [[A(f"fin{p}_{dm}", TC) for dm in range(NKM)] for p in range(2)]
    x2d = [[A(f"x2d{p}_{dm}", TC) for dm in range(NKM)] for p in range(2)]
    bar_all()
    b.wait("sync", "DO", b.v("DO"))
    fin_toks = [None] * NCH
    dv_toks = [[] for _ in range(NCH)]

    def fin_load(fc):
        par = fc % 2
        if fc >= 2:
            for tok in dv_toks[fc - 2]:
                b.wait_tok("sync", tok)
        fin_toks[fc] = b.ldg("sync", [
            p_ for dm in range(NKM) for p_ in (
                (fin[par][dm][:],
                 x1_scr[dm * 128:(dm + 1) * 128, fc * TC:(fc + 1) * TC]),
                (x2d[par][dm][:],
                 x2_scr[dm * 128:(dm + 1) * 128,
                        1 + fc * TC:1 + (fc + 1) * TC]))])
    fin_load(0)
    fin_load(1)
    groups = []
    for fc in range(NCH):
        par = fc % 2
        svs = [x2d[par][dm][:] for dm in range(NKM)]

        def pre_emit(fc=fc, par=par):
            for dm in range(NKM):
                b.wait_tok("vector", fin_toks[fc])
                xv = x2d[par][dm]
                am = b.emit("vector", lambda e, xv=xv, par=par, dm=dm:
                            e.tensor_tensor(
                    xv[:], xv[:], fin[par][dm][:], OP.add), inc="SV")
                b.wait("tensor", "SV", am)
                b.wait("scalar", "SV", am)

        def emit_out(dm, mu_b, rs_b, svs=svs, fc=fc, par=par):
            b.emit("vector", lambda e, dm=dm: e.tensor_tensor(
                fin[par][dm][:], svs[dm], mu_b, OP.subtract))
            nv = b.emit("vector", lambda e, dm=dm: e.tensor_tensor(
                fin[par][dm][:], fin[par][dm][:], rs_b, OP.mult), inc="SV")
            b.wait("sync", "SV", nv)
            dv = b.ldg("sync", [(
                out[dm * 128:(dm + 1) * 128, fc * TC:(fc + 1) * TC],
                fin[par][dm][:])])
            dv_toks[fc].append(dv)
            b.wait_tok("vector", dv)
            if dm == NKM - 1 and fc + 2 < NCH:
                fin_load(fc + 2)
        groups.append((svs, emit_out, pre_emit))
    ln_pipeline(groups, serial=LNSER)
    for s in b._pool["sync"]:
        b.wait("sync", s, b.v(s))
    b.wait("sync", "DS", b.v("DS"))
    b.wait("sync", "DO", b.v("DO"))

    b.finish()
    cvals_np = np.zeros((128, 8), np.float32)
    for val, i in getattr(b, "cmap", {}).items():
        cvals_np[:, i] = val
    return nc, cvals_np


def kernel(**inputs):
    g, xts = host_prep(inputs)
    nc, cvals_np = build_program(DUMPS)
    in_maps = []
    for c in range(8):
        m = dict(g)
        m["xT"] = xts[c]
        m["cvals"] = cvals_np
        in_maps.append(m)
    res = run_bass_kernel_spmd(nc, in_maps, core_ids=list(range(8)))
    outs = [np.asarray(r["out"]).T for r in res.results]
    return np.stack(outs).astype(np.float32)


# version 15
# speedup vs baseline: 1.2176x; 1.0248x over previous
import contextlib
import numpy as np
import ml_dtypes
import concourse.bass as bass
import concourse.mybir as mybir
from concourse.bass_utils import run_bass_kernel_spmd
from concourse.alu_op_type import AluOpType as OP

F32 = mybir.dt.float32
BF16 = mybir.dt.bfloat16
AF = mybir.ActivationFunctionType
AX = mybir.AxisListType
BFD = ml_dtypes.bfloat16

L, DM, DI, NS, DTR = 2048, 1024, 2048, 16, 64
TC = 512
NCH = L // TC
NDT = DI // 128
NKM = DM // 128
H, DH, WIN = 8, 64, 512
HID, HIDP = 2730, 2816
NPAIR = HIDP // 128
NEGBIG = -1.0e30

ENGS = ("sync", "gpsimd", "scalar", "vector", "tensor")

DUMPS = ()
DUMP_LAYER = 0
INTERLEAVE = 4
DBGPOOL = False
LNSER = True
SERIAL = 0


class KB:
    def __init__(self, nc):
        self.nc = nc
        self.q = {e: [] for e in ENGS}
        self.st = contextlib.ExitStack()
        self.semh, self.semc = {}, {}

    def sem(self, *names):
        for n in names:
            self.semh[n] = self.st.enter_context(self.nc.semaphore(n))
            self.semc[n] = 0

    def sb(self, name, shape, dt=F32):
        return self.st.enter_context(self.nc.sbuf_tensor(name, list(shape), dt))

    def pb(self, name, shape=(128, 512)):
        return self.st.enter_context(self.nc.psum_tensor(name, list(shape), F32))

    def wait(self, eng, sem, val):
        if val is None or val <= 0:
            return
        h = self.semh[sem]
        self.q[eng].append(lambda e, h=h, v=val: e.wait_ge(h, v))

    def wait_tok(self, eng, tok):
        if tok is None or tok == 0:
            return
        self.wait(eng, tok[0], tok[1])

    def dma_pool_init(self, pools):
        self._pool = {}
        self._pidx = {}
        for eng, n in pools.items():
            names = [f"DQ_{eng}_{i}" for i in range(n)]
            self.sem(*names)
            self._pool[eng] = names
            self._pidx[eng] = 0

    def ldg(self, eng, pairs):
        names = self._pool[eng]
        sem = names[self._pidx[eng] % len(names)]
        self._pidx[eng] += 1
        self.wait(eng, sem, self.semc[sem])
        v = None
        for dst, src in pairs:
            v = self.emit(eng, lambda e, d=dst, s=src: e.dma_start(
                out=d, in_=s), inc=sem, amt=16)
        return (sem, v)

    def emit(self, eng, fn, inc=None, amt=1):
        if inc is None:
            self.q[eng].append(fn)
            return None
        h = self.semh[inc]
        self.semc[inc] += amt
        v = self.semc[inc]
        self.q[eng].append(lambda e, fn=fn, h=h, a=amt: fn(e).then_inc(h, a))
        return v

    def v(self, sem):
        return self.semc[sem]

    def arena_init(self, nbytes):
        self._ar = self.st.enter_context(
            self.nc.sbuf_tensor("ARENA", [128, nbytes // 4], F32))
        self._arb = self._ar.bitcast(BF16)
        self._free = [(0, nbytes)]
        self._live = {}

    def alloc(self, name, cols, dt=F32, rows=None):
        esz = 4 if dt == F32 else 2
        nb = (cols * esz + 31) & ~31
        self._hw = max(getattr(self, "_hw", 0),
                       sum(v[1] for v in self._live.values()) + nb)
        for i, (off, sz) in enumerate(self._free):
            if sz >= nb:
                self._free[i] = (off + nb, sz - nb)
                if self._free[i][1] == 0:
                    self._free.pop(i)
                self._live[name] = (off, nb)
                base = self._ar if dt == F32 else self._arb
                ap = base[:, off // esz: off // esz + cols]
                if rows is not None:
                    ap = ap[0:rows, :]
                return ap
        raise MemoryError(f"arena OOM for {name}: need {nb}, free={self._free}")

    def bfview(self, name, cols):
        off, nb = self._live[name]
        assert cols * 2 <= nb
        return self._arb[:, off // 2: off // 2 + cols]

    def free(self, *names):
        for name in names:
            off, nb = self._live.pop(name)
            self._free.append((off, nb))
        self._free.sort()
        out = []
        for off, sz in self._free:
            if out and out[-1][0] + out[-1][1] == off:
                out[-1] = (out[-1][0], out[-1][1] + sz)
            else:
                out.append((off, sz))
        self._free = [tuple(x) for x in out]

    def _cvals_init(self):
        if not hasattr(self, "_cvals"):
            self._cvals = self.sb("cvals_sb", [128, 8], F32)
            self.cmap = {}
        return self._cvals[:]

    def const_col(self, val):
        if not hasattr(self, "_cvals"):
            self._cvals = self.sb("cvals_sb", [128, 8], F32)
            self.cmap = {}
        if val not in self.cmap:
            i = len(self.cmap)
            assert i < 8
            self.nc.const_aps.register(F32, float(val), self._cvals[:, i:i + 1])
            self.cmap[val] = i
        return float(val)

    def finish(self):
        with self.st:
            with self.nc.Block() as blk:
                for eng in ENGS:
                    def mk(eng=eng):
                        def body(e):
                            for fn in self.q[eng]:
                                fn(e)
                        return body
                    getattr(blk, eng)(mk())


def _pack_tiles_rows(w):
    return np.ascontiguousarray(w, dtype=np.float32)


def host_prep(inputs):
    g = {}
    W_in = np.asarray(inputs["L0_attn_W_in"], np.float32)
    win_s = np.stack(
        [np.concatenate([W_in[:, dt * 128:(dt + 1) * 128],
                         W_in[:, DI + dt * 128:DI + (dt + 1) * 128]], axis=1)
         for dt in range(NDT)])
    g["win_s"] = np.ascontiguousarray(win_s.reshape(NDT * DM, 256))
    W_out = np.asarray(inputs["L0_attn_W_out"], np.float32)
    wout_s = np.stack([W_out[:, dm * 128:(dm + 1) * 128] for dm in range(NKM)])
    g["wout_s"] = np.ascontiguousarray(wout_s.reshape(NKM * DI, 128))
    W_x = np.asarray(inputs["L0_attn_W_x"], np.float32)
    g["wx_s"] = np.ascontiguousarray(
        W_x.reshape(NDT, 128, 96).transpose(1, 0, 2).reshape(128, NDT * 96))
    g["wdt_s"] = _pack_tiles_rows(inputs["L0_attn_W_dt"])
    g["bdt_c"] = np.ascontiguousarray(
        np.asarray(inputs["L0_attn_b_dt"], np.float32).reshape(NDT, 128).T)
    g["d_c"] = np.ascontiguousarray(
        np.asarray(inputs["L0_attn_D_param"], np.float32).reshape(NDT, 128).T)
    g["wconv_c"] = np.ascontiguousarray(
        np.asarray(inputs["L0_attn_W_conv"], np.float32)
        .reshape(NDT, 128, 3).transpose(1, 0, 2).reshape(128, NDT * 3))
    g["alog_c"] = np.ascontiguousarray(
        np.asarray(inputs["L0_attn_A_log"], np.float32)
        .reshape(NDT, 128, NS).transpose(1, 0, 2).reshape(128, NDT * NS))

    wqkv = np.asarray(inputs["L0_ffn_wqkv"], np.float32)
    for l in range(2):
        g[f"wq_{l}"] = np.ascontiguousarray(wqkv[l][:, 0:512])
        g[f"wk_{l}"] = np.ascontiguousarray(wqkv[l][:, 512:1024])
        g[f"wv_{l}"] = np.ascontiguousarray(wqkv[l][:, 1024:1536])
        wo = np.asarray(inputs["L0_ffn_wo"], np.float32)[l]
        g[f"wo_s_{l}"] = np.ascontiguousarray(
            wo.reshape(H, DH, DM).transpose(1, 0, 2).reshape(DH, H * DM))
        w1 = np.asarray(inputs["L0_ffn_w1"], np.float32)[l]
        w1p = np.zeros((DM, 2 * HIDP), np.float32)
        w1p[:, :HID] = w1[:, :HID]
        w1p[:, HIDP:HIDP + HID] = w1[:, HID:]
        w1_s = np.stack(
            [np.concatenate([w1p[:, j * 128:(j + 1) * 128],
                             w1p[:, HIDP + j * 128:HIDP + (j + 1) * 128]], 1)
             for j in range(NPAIR)])
        g[f"w1_s_{l}"] = np.ascontiguousarray(w1_s.reshape(NPAIR * DM, 256))
        b1 = np.asarray(inputs["L0_ffn_b1"], np.float32)[l]
        b1p = np.zeros(2 * HIDP, np.float32)
        b1p[:HID] = b1[:HID]
        b1p[HIDP:HIDP + HID] = b1[HID:]
        b1c = np.concatenate([b1p[:HIDP].reshape(NPAIR, 128).T,
                              b1p[HIDP:].reshape(NPAIR, 128).T], axis=1)
        g[f"b1c_{l}"] = np.ascontiguousarray(b1c)
        w2 = np.asarray(inputs["L0_ffn_w2"], np.float32)[l]
        w2p = np.zeros((HIDP, DM), np.float32)
        w2p[:HID] = w2
        g[f"w2_s_{l}"] = np.ascontiguousarray(w2p)
        g[f"b2c_{l}"] = np.ascontiguousarray(
            np.asarray(inputs["L0_ffn_b2"], np.float32)[l].reshape(NKM, 128).T)
        g[f"lng_c_{l}"] = np.ascontiguousarray(
            np.asarray(inputs["L0_ffn_ln_g"], np.float32)[l].reshape(NKM, 128).T)
        g[f"lnb_c_{l}"] = np.ascontiguousarray(
            np.asarray(inputs["L0_ffn_ln_b"], np.float32)[l].reshape(NKM, 128).T)
        qs = np.asarray(inputs["L0_ffn_q_scale"], np.float32)[l]
        ks = np.asarray(inputs["L0_ffn_k_scale"], np.float32)[l]
        g[f"qs2_{l}"] = np.ascontiguousarray(np.tile(qs, 2)[:, None])
        g[f"qs2r_{l}"] = np.ascontiguousarray(
            np.tile(np.concatenate([qs[32:], qs[:32]]), 2)[:, None])
        g[f"ks2_{l}"] = np.ascontiguousarray(np.tile(ks, 2)[:, None])
        g[f"ks2r_{l}"] = np.ascontiguousarray(
            np.tile(np.concatenate([ks[32:], ks[:32]]), 2)[:, None])

    half = np.arange(0, DH, 2, dtype=np.float64)
    inv_freq = 1.0 / (10000.0 ** (half / DH))
    t = np.arange(L, dtype=np.float64)
    freqs = t[:, None] * inv_freq[None, :]
    freqs = np.concatenate([freqs, freqs], 1)
    base = (half + 0.4 * DH) / (1.4 * DH)
    power = (t - L // 2) / float(WIN // 2)
    scale = base[None, :] ** power[:, None]
    scale = np.concatenate([scale, scale], 1)
    cos, sin = np.cos(freqs), np.sin(freqs)
    QK = 8.0
    csq = (QK * cos * scale).T.astype(np.float32)
    ssq = (QK * sin * scale).T.astype(np.float32)
    csk = (cos / scale).T.astype(np.float32)
    ssk = (sin / scale).T.astype(np.float32)
    g["tab_csq"] = np.ascontiguousarray(np.tile(csq, (2, 1)))
    g["tab_ssq"] = np.ascontiguousarray(np.tile(ssq, (2, 1)))
    g["tab_csk"] = np.ascontiguousarray(np.tile(csk, (2, 1)))
    g["tab_ssk"] = np.ascontiguousarray(np.tile(ssk, (2, 1)))
    R = np.zeros((DH, DH), np.float32)
    for f in range(32):
        R[f + 32, f] = -1.0
        R[f, f + 32] = 1.0
    g["r2f"] = np.ascontiguousarray(
        np.kron(np.eye(2, dtype=np.float32), R))
    g["i128"] = np.eye(128, dtype=np.float32)
    g["i128bf"] = np.eye(128, dtype=np.float32).astype(BFD)
    ii, jj = np.meshgrid(np.arange(128), np.arange(128), indexing="ij")
    m_near = np.where(jj <= ii, 0.0, NEGBIG).astype(np.float32)
    m_far = np.where(jj >= ii, 0.0, NEGBIG).astype(np.float32)
    g["mask_near_ij"] = m_near
    g["mask_far_ij"] = m_far
    g["maskT_near"] = np.ascontiguousarray(m_near.T)
    g["maskT_far"] = np.ascontiguousarray(m_far.T)
    g["ones_row"] = np.ones((1, 128), np.float32)
    g["ones_row_bf"] = np.ones((1, 128), np.float32).astype(BFD)
    g["ones_col"] = np.ones((128, 1), np.float32)
    g["ones_col_bf"] = np.ones((128, 1), np.float32).astype(BFD)
    bl2 = np.zeros((128, 2), np.float32)
    bl2[0:64, 0] = 1.0
    bl2[64:128, 1] = 1.0
    g["bl2_bf"] = bl2.astype(BFD)
    hs = np.zeros((2, 128), np.float32)
    hs[0, 0:64] = 1.0
    hs[1, 64:128] = 1.0
    g["headsel"] = hs

    x = np.asarray(inputs["L0_attn_x"], np.float32)
    xts = [np.ascontiguousarray(x[c].T) for c in range(x.shape[0])]
    for k in list(g):
        if (k in ("win_s", "wout_s", "wx_s", "wdt_s")
                or k.startswith(("wq_", "wk_", "wv_", "wo_s_", "w1_s_",
                                 "w2_s_"))):
            g[k] = np.ascontiguousarray(g[k].astype(ml_dtypes.bfloat16))
    return g, xts



def build_program(dumps=()):
    nc = bass.Bass("TRN2", target_bir_lowering=False)
    b = KB(nc)
    di = {}

    def inp(name, shape, dt=F32):
        di[name] = nc.dram_tensor(name, list(shape), dt, kind="ExternalInput").ap()
        return di[name]

    inp("xT", [DM, L])
    inp("win_s", [NDT * DM, 256], BF16); inp("wout_s", [NKM * DI, 128], BF16)
    inp("wx_s", [128, NDT * 96], BF16); inp("wdt_s", [64, DI], BF16)
    inp("bdt_c", [128, NDT]); inp("d_c", [128, NDT])
    inp("wconv_c", [128, NDT * 3]); inp("alog_c", [128, NDT * NS])
    for l in range(2):
        inp(f"wq_{l}", [DM, 512], BF16); inp(f"wk_{l}", [DM, 512], BF16)
        inp(f"wv_{l}", [DM, 512], BF16)
        inp(f"wo_s_{l}", [DH, H * DM], BF16)
        inp(f"w1_s_{l}", [NPAIR * DM, 256], BF16)
        inp(f"b1c_{l}", [128, 2 * NPAIR]); inp(f"w2_s_{l}", [HIDP, DM], BF16)
        inp(f"b2c_{l}", [128, NKM]); inp(f"lng_c_{l}", [128, NKM]); inp(f"lnb_c_{l}", [128, NKM])
        for nm in ("qs2", "qs2r", "ks2", "ks2r"):
            inp(f"{nm}_{l}", [128, 1])
    for nm in ("tab_csq", "tab_ssq", "tab_csk", "tab_ssk"):
        inp(nm, [128, L])
    inp("r2f", [128, 128]); inp("i128", [128, 128]); inp("i128bf", [128, 128], BF16)
    inp("cvals", [128, 8])
    inp("mask_near_ij", [128, 128]); inp("mask_far_ij", [128, 128])
    inp("maskT_near", [128, 128]); inp("maskT_far", [128, 128])
    inp("ones_row", [1, 128]); inp("ones_row_bf", [1, 128], BF16)
    inp("ones_col", [128, 1]); inp("ones_col_bf", [128, 1], BF16)
    inp("bl2_bf", [128, 2], BF16); inp("headsel", [2, 128])

    out = nc.dram_tensor("out", [DM, L], F32, kind="ExternalOutput").ap()
    dbg = {}
    for name in dumps:
        shape = {"x1": [DM, L], "du": [DI, L], "delta": [DI, L], "u0": [DI, L],
                 "yraw": [DI, L], "y3": [DI, L], "xdbl": [96, L],
                 "su": [DI, L], "qt": [512, L], "kt": [512, L],
                 "ot": [512, L], "xwo": [DM, L], "hln": [DM, L],
                 "hmid": [HIDP, L], "x2l": [DM, L],
                 "qraw": [512, L], "wops": [DM, L], "y2": [DI, L],
                 "ch00": [128, L], "s0": [DM, L], "sx": [DM, L]}[name]
        dbg[name] = nc.dram_tensor("dbg_" + name, shape, F32,
                                   kind="ExternalOutput").ap()

    bc_scr = nc.dram_tensor("bc_scr", [NCH * 32, TC], F32, kind="Internal").ap()
    x1_scr = nc.dram_tensor("x1_scr", [DM, L], F32, kind="Internal").ap()
    x1bf_scr = nc.dram_tensor("x1bf_scr", [DM, 1 + L], BF16, kind="Internal").ap()
    x2_scr = nc.dram_tensor("x2_scr", [DM, 1 + L], F32, kind="Internal").ap()
    x2bf_scr = nc.dram_tensor("x2bf_scr", [DM, 1 + L], BF16, kind="Internal").ap()

    b.sem("DW", "DG", "DR", "DS", "DO", "DO2", "SV", "SS", "ST", "SG", "MKS", "MKG")
    b.dma_pool_init({"sync": 12, "gpsimd": 8})

    bdt = b.sb("bdt", [128, NDT])
    dpar = b.sb("dpar", [128, NDT])
    wconv = b.sb("wconv", [128, NDT * 3])
    hstate = b.sb("hstate", [128, NDT * NS])
    ucarry = b.sb("ucarry", [128, NDT * 2])
    i128 = b.sb("i128s", [128, 128])
    i128bf = b.sb("i128bfs", [128, 128], BF16)
    onesr = b.sb("onesr", [1, 128]); onesrb = b.sb("onesrb", [1, 128], BF16)
    onesc = b.sb("onesc", [128, 1]); onescb = b.sb("onescb", [128, 1], BF16)
    bl2 = b.sb("bl2", [128, 2], BF16)
    headsel = b.sb("headsel_s", [2, 128])
    r2f = b.sb("r2fs", [128, 128])
    mnij = b.sb("mnij", [128, 128]); mfij = b.sb("mfij", [128, 128])
    mtn = b.sb("mtn", [128, 128]); mtf = b.sb("mtf", [128, 128])

    b.arena_init(190 * 1024)
    A = b.alloc
    aneg = A("aneg", NDT * NS)
    wx = A("wx", NDT * 96, BF16)
    wdt = A("wdt", DI, BF16, rows=64)
    xbf = [A(f"xbf{k}", TC, BF16) for k in range(NKM)]
    winb = [A(f"winb{p}", 8 * 256, BF16) for p in range(2)]
    woutb = [A(f"woutb{p}", 16 * 128, BF16) for p in range(2)]
    ubuf = A("ubuf", 2 + TC)
    convA = A("convA", TC); convB = A("convB", TC)
    sudu = [A(f"sudu{i}", TC) for i in range(NDT)]
    delta = [A(f"delta{i}", TC) for i in range(NDT)]
    subf = [A(f"subf{i}", TC, BF16) for i in range(NDT)]
    sres = [A(f"sres{i}", TC, BF16) for i in range(NDT)]
    y2buf = A("y2buf", TC)
    dtbf = A("dtbf", TC, BF16, rows=64)
    bcsb = A("bcsb", TC)
    dA2 = [A(f"dA{p}", TC) for p in range(2)]
    dbu2 = [A(f"dbu{p}", TC) for p in range(2)]
    hsl = A("hsl", TC)
    chsl = [A(f"chsl{p}", TC, BF16) for p in range(2)]
    Bsl = [A(f"Bsl{p}", TC) for p in range(2)]
    Csl = [A(f"Csl{p}", TC) for p in range(2)]
    sqs = b.sb("sqs", [128, TC], BF16)
    rowm = b.sb("rowm", [1, TC]); rowv = b.sb("rowv", [1, TC])
    rowr = b.sb("rowr", [1, TC])
    row2 = b.sb("row2", [2, TC])
    x1bf = b.sb("x1bf", [128, TC], BF16)

    xch = sudu[:NKM]
    pbank = [b.pb(f"pbank{i}") for i in range(8)]

    ld = b.ldg

    prelude_tok = b.ldg("sync", [
        (bdt[:], di["bdt_c"]), (dpar[:], di["d_c"]), (wconv[:], di["wconv_c"]),
        (aneg[:], di["alog_c"]), (i128[:], di["i128"]),
        (onesr[:], di["ones_row"]), (onesrb[:], di["ones_row_bf"]),
        (onesc[:], di["ones_col"]), (onescb[:], di["ones_col_bf"]),
        (bl2[:], di["bl2_bf"]), (headsel[:], di["headsel"]),
        (r2f[:], di["r2f"]), (i128bf[:], di["i128bf"]),
        (mnij[:], di["mask_near_ij"]),
        (mfij[:], di["mask_far_ij"]), (mtn[:], di["maskT_near"]),
        (mtf[:], di["maskT_far"]), (b._cvals_init(), di["cvals"])])
    wxdt_tok = b.ldg("gpsimd", [(wx[:], di["wx_s"]), (wdt[:], di["wdt_s"])])

    b.wait_tok("scalar", prelude_tok)
    an1 = b.emit("scalar", lambda e: e.activation(aneg[:], aneg[:], AF.Exp),
                 inc="SS")
    b.wait("scalar", "SS", an1)
    b.emit("scalar", lambda e: e.activation(aneg[:], aneg[:], AF.Copy,
                                            scale=-1.0), inc="SS")
    b.emit("vector", lambda e: e.memset(hstate[:], 0.0))
    b.emit("vector", lambda e: e.memset(ucarry[:], 0.0))
    b.emit("vector", lambda e: e.memset(ubuf[:], 0.0), inc="SV")

    bar = {"SV": b.v("SV"), "SS": b.v("SS"), "ST": b.v("ST"), "SG": b.v("SG")}
    for eng in ENGS:
        for s in ("SV", "SS", "ST", "SG"):
            b.wait(eng, s, bar[s])

    xch = [dA2[0], dA2[1], dbu2[0], dbu2[1], Bsl[0], Bsl[1], Csl[0], Csl[1]]
    y3b = [b.bfview(f"delta{i}", TC) for i in range(NDT)]

    def new_m1_ctx(c, ydp):
        return {"c": c, "tsl": slice(c * TC, (c + 1) * TC), "ydp": ydp,
                "win_dg": [None] * NDT, "umm": [None] * NDT,
                "resmm": [None] * NDT, "evu": [None] * NDT,
                "res_s": [None] * NDT, "conv": [None] * NDT,
                "xdbl_done": None, "xbf_tok": None, "su_s": [None] * NDT,
                "defer_xdbl": False,
                "du_done": [None] * NDT, "sp": [None] * NDT,
                "bc_tok": None, "bc_s": None, "dtbf_s": None}

    def emit_xbf_load(ctx):
        ctx["xbf_tok"] = b.ldg("gpsimd", [
            (xbf[k][:], di["xT"][k * 128:(k + 1) * 128, ctx["tsl"]])
            for k in range(NKM)])

    def load_win(ctx, dt):
        if dt >= NDT:
            return
        if dt >= 2:
            b.wait("gpsimd", "ST", ctx["resmm"][dt - 2])
        ctx["win_dg"][dt] = b.ldg("gpsimd", [(
            winb[dt % 2][:].rearrange("p (ki cc) -> p ki cc", ki=NKM),
            di["win_s"][dt * DM:(dt + 1) * DM, :]
            .rearrange("(ki p) cc -> p ki cc", p=128))])

    def emit_m1_dt(ctx, dt):
        c = ctx["c"]; tsl = ctx["tsl"]; ydp = ctx["ydp"]
        if SERIAL & 8:
            for _e in ENGS:
                for _s in ("SV", "SS", "ST"):
                    b.wait(_e, _s, b.v(_s))
        wslab = winb[dt % 2]
        b.wait_tok("tensor", ctx["win_dg"][dt])
        if dt == 0:
            b.wait_tok("tensor", ctx["xbf_tok"])
            b.wait_tok("tensor", wxdt_tok)
        if dt >= 1:
            b.wait("tensor", "SS", ctx["evu"][dt - 1])
            b.wait("tensor", "SS", ctx["res_s"][dt - 1])
            b.wait("tensor", "SV", ctx["conv"][dt - 1])
        elif ydp is not None:
            b.wait("tensor", "SV", ydp[8])
            b.wait("tensor", "SV", ydp[9])
        for ki in range(NKM):
            wv = wslab[:, ki * 256:ki * 256 + 128]
            last = ki == NKM - 1
            if last:
                ctx["umm"][dt] = b.emit("tensor", lambda e, wv=wv, ki=ki:
                                        e.matmul(
                    pbank[0][:], wv, xbf[ki][:], start=(ki == 0), stop=True),
                    inc="ST")
            else:
                b.emit("tensor", lambda e, wv=wv, ki=ki: e.matmul(
                    pbank[0][:], wv, xbf[ki][:], start=(ki == 0), stop=False))
        for ki in range(NKM):
            wv = wslab[:, ki * 256 + 128:ki * 256 + 256]
            last = ki == NKM - 1
            if last:
                ctx["resmm"][dt] = b.emit("tensor", lambda e, wv=wv, ki=ki:
                                          e.matmul(
                    pbank[1][:], wv, xbf[ki][:], start=(ki == 0), stop=True),
                    inc="ST")
            else:
                b.emit("tensor", lambda e, wv=wv, ki=ki: e.matmul(
                    pbank[1][:], wv, xbf[ki][:], start=(ki == 0), stop=False))
        load_win(ctx, dt + 2)
        b.wait("scalar", "ST", ctx["umm"][dt])
        if dt == 0 and c == 0:
            b.wait("scalar", "SV", b.v("SV"))
        ctx["evu"][dt] = b.emit("scalar", lambda e: e.activation(
            ubuf[:, 2:2 + TC], pbank[0][:], AF.Copy), inc="SS")
        if "u0" in dbg:
            b.wait("sync", "SS", ctx["evu"][dt])
            dver = b.emit("sync", lambda e, dt=dt, tsl=tsl: e.dma_start(
                out=dbg["u0"][dt * 128:(dt + 1) * 128, tsl],
                in_=ubuf[:, 2:2 + TC]), inc="DO", amt=16)
            b.wait("scalar", "DO", dver)
        b.wait("vector", "SS", ctx["evu"][dt])
        cpin = ucarry[:, dt * 2:dt * 2 + 2]
        cp1 = b.emit("vector", lambda e, cpin=cpin: e.tensor_copy(
            ubuf[:, 0:2], cpin), inc="SV")
        b.wait("vector", "SV", cp1)
        b.emit("vector", lambda e, cpin=cpin: e.tensor_copy(
            cpin, pbank[0][:, TC - 2:TC]))
        w0 = wconv[:, dt * 3:dt * 3 + 1]
        w1c = wconv[:, dt * 3 + 1:dt * 3 + 2]
        w2c = wconv[:, dt * 3 + 2:dt * 3 + 3]
        b.emit("vector", lambda e, w0=w0: e.tensor_scalar(
            convA[:], ubuf[:, 0:TC], w0, None, OP.mult))
        b.emit("vector", lambda e, w1c=w1c: e.scalar_tensor_tensor(
            convB[:], ubuf[:, 1:1 + TC], w1c, convA[:], OP.mult, OP.add))
        conv_v = b.emit("vector", lambda e, w2c=w2c: e.scalar_tensor_tensor(
            convA[:], ubuf[:, 2:2 + TC], w2c, convB[:], OP.mult, OP.add),
            inc="SV")
        ctx["conv"][dt] = conv_v
        b.wait("scalar", "SV", conv_v)
        sud = sudu[dt]; sub = subf[dt]; srb = sres[dt]
        b.emit("scalar", lambda e, sud=sud: e.activation(
            sud[:], convA[:], AF.Silu))
        su_s = b.emit("scalar", lambda e, sub=sub: e.activation(
            sub[:], convA[:], AF.Silu), inc="SS")
        b.wait("scalar", "ST", ctx["resmm"][dt])
        ctx["res_s"][dt] = b.emit("scalar", lambda e, srb=srb: e.activation(
            srb[:], pbank[1][:], AF.Silu), inc="SS")
        ctx["su_s"][dt] = su_s
        if not ctx.get("defer_xdbl"):
            emit_xdbl(ctx, dt)

    def emit_xdbl(ctx, dt):
        ydp = ctx["ydp"]
        b.wait("tensor", "SS", ctx["su_s"][dt])
        if dt == 0 and ydp is not None:
            b.wait("tensor", "SV", ydp[10])
        wxs = wx[:, dt * 96:(dt + 1) * 96]
        sub = subf[dt]
        ctx["xdbl_done"] = b.emit("tensor", lambda e, wxs=wxs, sub=sub, dt=dt:
                                  e.matmul(pbank[2][0:96, :], wxs, sub[:],
                                           start=(dt == 0),
                                           stop=(dt == NDT - 1)),
                                  inc="ST")

    def emit_m2(ctx, wout7):
        c = ctx["c"]; tsl = ctx["tsl"]; ydp = ctx["ydp"]
        b.wait("scalar", "ST", ctx["xdbl_done"])
        ctx["dtbf_s"] = b.emit("scalar", lambda e: e.activation(
            dtbf[:], pbank[2][0:64, :], AF.Copy), inc="SS")
        ctx["bc_s"] = b.emit("scalar", lambda e: e.activation(
            bcsb[64:96, :], pbank[2][64:96, :], AF.Copy), inc="SS")
        b.wait("sync", "SS", ctx["bc_s"])
        ctx["bc_tok"] = b.ldg("sync", [(bc_scr[c * 32:(c + 1) * 32, :],
                                        bcsb[64:96, :])])
        if wout7 is not None:
            b.wait("vector", "ST", wout7)
        for dt in range(NDT):
            b.wait("tensor", "SS", ctx["dtbf_s"])
            if dt >= 1:
                b.wait("tensor", "SS", ctx["sp"][dt - 1])
            elif ydp is not None:
                b.wait("tensor", "SV", ydp[11])
            wds = wdt[:, dt * 128:(dt + 1) * 128]
            dmm = b.emit("tensor", lambda e, wds=wds: e.matmul(
                pbank[3][:], wds, dtbf[:], start=True, stop=True), inc="ST")
            b.wait("scalar", "ST", dmm)
            if dt >= 1:
                b.wait("scalar", "SV", ctx["du_done"][dt - 1])
            bcol = bdt[:, dt:dt + 1]
            b.emit("scalar", lambda e, bcol=bcol: e.activation(
                convA[:], pbank[3][:], AF.Abs, bias=bcol))
            b.emit("scalar", lambda e: e.activation(
                convA[:], convA[:], AF.Exp, scale=-1.0))
            b.emit("scalar", lambda e: e.activation(
                convA[:], convA[:], AF.Ln, bias=1.0))
            ctx["sp"][dt] = b.emit("scalar", lambda e, bcol=bcol: e.activation(
                convB[:], pbank[3][:], AF.Relu, bias=bcol), inc="SS")
            b.wait("vector", "SS", ctx["sp"][dt])
            dl_ = delta[dt]; sud = sudu[dt]
            b.emit("vector", lambda e, dl_=dl_: e.tensor_tensor(
                dl_[:], convA[:], convB[:], OP.add))
            ctx["du_done"][dt] = b.emit("vector", lambda e, dl_=dl_, sud=sud:
                                        e.tensor_tensor(sud[:], dl_[:], sud[:],
                                                        OP.mult), inc="SV")
        if "su" in dbg:
            for dt in range(NDT):
                b.wait("gpsimd", "SS", ctx["res_s"][dt])
                b.emit("gpsimd", lambda e, dt=dt, tsl=tsl: e.dma_start(
                    out=dbg["su"][dt * 128:(dt + 1) * 128, tsl],
                    in_=subf[dt][:]), inc="DO2", amt=16)
        if "xdbl" in dbg:
            b.wait("gpsimd", "SS", ctx["bc_s"])
            b.emit("gpsimd", lambda e, tsl=tsl: e.dma_start(
                out=dbg["xdbl"][0:64, tsl], in_=dtbf[:]), inc="DO2", amt=16)
            b.wait("sync", "SS", ctx["bc_s"])
            b.emit("sync", lambda e, tsl=tsl: e.dma_start(
                out=dbg["xdbl"][64:96, tsl], in_=bcsb[64:96, :]), inc="DO", amt=16)
        if "delta" in dbg:
            for dt in range(NDT):
                b.wait("sync", "SV", ctx["du_done"][dt])
                b.emit("sync", lambda e, dt=dt, tsl=tsl: e.dma_start(
                    out=dbg["delta"][dt * 128:(dt + 1) * 128, tsl],
                    in_=delta[dt][:]), inc="DO", amt=16)
        if "du" in dbg:
            for dt in range(NDT):
                b.wait("sync", "SV", ctx["du_done"][dt])
                b.emit("sync", lambda e, dt=dt, tsl=tsl: e.dma_start(
                    out=dbg["du"][dt * 128:(dt + 1) * 128, tsl],
                    in_=sudu[dt][:]), inc="DO", amt=16)

    def emit_scan(ctx, prev_sc):
        c = ctx["c"]; tsl = ctx["tsl"]
        wout_dg = [None] * NKM
        wout_nop = [None] * NKM

        def load_wout(dm):
            if dm >= NKM:
                return
            if dm >= 2:
                b.wait("gpsimd", "ST", wout_nop[dm - 2])
            elif prev_sc is not None:
                b.wait("gpsimd", "ST", prev_sc["wout_nop"][6 + dm])
            wout_dg[dm] = b.ldg("gpsimd", [(
                woutb[dm % 2][:].rearrange("p (ki cc) -> p ki cc", ki=NDT),
                di["wout_s"][dm * DI:(dm + 1) * DI, :]
                .rearrange("(ki p) cc -> p ki cc", p=128))])
        load_wout(0)
        load_wout(1)
        flush = ctx.get("flush")
        bankfree = ctx.get("bankfree")
        if flush is None:
            b.wait("tensor", "SS", b.v("SS"))
            b.wait("tensor", "SV", b.v("SV"))
            b.wait("scalar", "SV", b.v("SV"))
            b.wait("gpsimd", "SV", b.v("SV"))
        scan_ms = {}
        scan_pe = {}
        repl_ms = {}
        nlast_ms = {}
        y3_done = [None] * NDT
        for half in range(2):
            for n in range(NS):
                hn = half * NS + n
                par = hn % 2
                b.wait_tok("sync", ctx["bc_tok"])
                if hn >= 2:
                    b.wait("sync", "SV", nlast_ms[hn - 2])
                elif flush is not None:
                    b.wait_tok("sync", flush[4 + par])
                    b.wait_tok("sync", flush[6 + par])
                repl_ms[hn] = b.ldg("sync", [
                    (Bsl[par][:],
                     bass.AP(bc_scr.tensor, (c * 32 + n) * TC, [[0, 128], [1, TC]])),
                    (Csl[par][:],
                     bass.AP(bc_scr.tensor, (c * 32 + 16 + n) * TC,
                             [[0, 128], [1, TC]]))])
                for k8 in range(8):
                    dt = half * 8 + k8
                    it = hn * 8 + k8
                    apar = it % 2
                    ipar = it % 2
                    b.wait("scalar", "SV", ctx["du_done"][dt])
                    if it >= 2:
                        b.wait("scalar", "SV", scan_ms.get(it - 2, 0))
                    elif flush is not None:
                        b.wait_tok("scalar", flush[it])
                    acol = aneg[:, dt * NS + n:dt * NS + n + 1]
                    dl_ = delta[dt]
                    da_s = b.emit("scalar", lambda e, apar=apar, acol=acol,
                                  dl_=dl_: e.activation(
                        dA2[apar][:], dl_[:], AF.Exp, scale=acol), inc="SS")
                    sud = sudu[dt]
                    b.wait_tok("vector", repl_ms[hn])
                    if it < 2 and flush is not None:
                        b.wait_tok("vector", flush[2 + it])
                    b.emit("vector", lambda e, sud=sud, par=par, ipar=ipar:
                           e.tensor_tensor(dbu2[ipar][:], sud[:],
                                           Bsl[par][:], OP.mult))
                    b.wait("vector", "SS", da_s)
                    if it >= 2:
                        b.wait("vector", "ST", scan_pe.get(it - 2, 0))
                    hinit = hstate[:, dt * NS + n:dt * NS + n + 1]
                    b.emit("vector", lambda e, apar=apar, hinit=hinit, ipar=ipar:
                           e.tensor_tensor_scan(hsl[:], dA2[apar][:],
                                                dbu2[ipar][:], hinit,
                                                OP.mult, OP.add))
                    chv = b.emit("vector", lambda e, par=par, ipar=ipar:
                                 e.tensor_tensor(chsl[ipar][:], hsl[:],
                                                 Csl[par][:], OP.mult), inc="SV")
                    sv1 = b.emit("vector", lambda e, hinit=hinit: e.tensor_copy(
                        hinit, hsl[:, TC - 1:TC]), inc="SV")
                    scan_ms[it] = sv1
                    if k8 == 7:
                        nlast_ms[hn] = sv1
                    b.wait("tensor", "SV", chv)
                    if n == 0 and half == 0 and bankfree is not None:
                        for sem_, val_ in bankfree[k8]:
                            b.wait("tensor", sem_, val_)
                    pe_ = b.emit("tensor", lambda e, k8=k8, ipar=ipar, n=n:
                                 e.matmul(pbank[k8][:], i128bf[:],
                                          chsl[ipar][:], start=(n == 0),
                                          stop=(n == NS - 1)), inc="ST")
                    scan_pe[it] = pe_
            for k8 in range(8):
                dt = half * 8 + k8
                dcol = dpar[:, dt:dt + 1]
                sub = subf[dt]; srb = sres[dt]; dl3 = y3b[dt]
                b.wait("vector", "ST", scan_pe[(half * NS + NS - 1) * 8 + k8])
                b.emit("vector", lambda e, sub=sub, dcol=dcol, k8=k8:
                       e.scalar_tensor_tensor(y2buf[:], sub[:], dcol,
                                              pbank[k8][:], OP.mult, OP.add))
                if "y2" in dbg:
                    yq = b.emit("vector", lambda e: e.memset(
                        onesr[0:1, 0:1], 1.0), inc="SV")
                    b.wait("sync", "SV", yq)
                    dq = b.ldg("sync", [(dbg["y2"][dt * 128:(dt + 1) * 128, tsl],
                                         y2buf[:])])
                    b.wait_tok("vector", dq)
                yev = b.emit("vector", lambda e, dl3=dl3, srb=srb:
                             e.tensor_tensor(dl3, y2buf[:], srb[:],
                                             OP.mult), inc="SV")
                y3_done[dt] = yev
        if "y3" in dbg:
            for dt in range(NDT):
                b.wait("gpsimd", "SV", y3_done[dt])
                b.emit("gpsimd", lambda e, dt=dt, tsl=tsl: e.dma_start(
                    out=dbg["y3"][dt * 128:(dt + 1) * 128, tsl],
                    in_=y3b[dt]), inc="DO2", amt=16)
        return dict(y3_done=y3_done, nlast=nlast_ms[2 * NS - 1],
                    wout_dg=wout_dg, wout_nop=wout_nop, load_wout=load_wout)

    def emit_m5(ctx, sc, nctx):
        c = ctx["c"]; tsl = ctx["tsl"]
        y3_done = sc["y3_done"]
        wout_dg = sc["wout_dg"]; wout_nop = sc["wout_nop"]
        b.wait("sync", "SV", sc["nlast"])
        xch_tok = b.ldg("sync", [
            (xch[k][:], di["xT"][k * 128:(k + 1) * 128, tsl])
            for k in range(NKM)])
        s_done = [None] * NKM
        b.wait("tensor", "SV", y3_done[NDT - 1])
        for dm in range(NKM):
            if nctx is not None and INTERLEAVE in (1, 2, 3):
                emit_m1_dt(nctx, 2 * dm)
                emit_m1_dt(nctx, 2 * dm + 1)
            b.wait_tok("tensor", wout_dg[dm])
            if DBGPOOL:
                for s_ in b._pool["gpsimd"]:
                    b.wait("tensor", s_, b.v(s_))
            if dm >= 2:
                b.wait("tensor", "SV", s_done[dm - 2])
            bank = pbank[4 + dm % 2]
            wslab = woutb[dm % 2]
            b.wait("tensor", "SV", y3_done[8 + 4 + dm % 2])
            for ki in range(NDT):
                wv = wslab[:, ki * 128:(ki + 1) * 128]
                dl3 = y3b[ki]
                last = ki == NDT - 1
                mm = b.emit("tensor", lambda e, wv=wv, dl3=dl3, ki=ki,
                            bank=bank: e.matmul(bank[:], wv, dl3,
                                                start=(ki == 0),
                                                stop=(ki == NDT - 1)),
                            inc="ST" if last else None)
                if last:
                    wout_nop[dm] = mm
            sc["load_wout"](dm + 2)
            b.wait("vector", "ST", wout_nop[dm])
            b.wait_tok("vector", xch_tok)
            if "sx" in dbg:
                b.wait_tok("sync", xch_tok)
                dsx = b.emit("sync", lambda e, dm=dm, tsl=tsl: e.dma_start(
                    out=dbg["sx"][dm * 128:(dm + 1) * 128, tsl],
                    in_=xch[dm][:]), inc="DO", amt=16)
                b.wait("vector", "DO", dsx)
            xc = xch[dm]
            s_done[dm] = b.emit("vector", lambda e, xc=xc, bank=bank:
                                e.tensor_tensor(xc[:], xc[:], bank[:], OP.add),
                                inc="SV")
        if "s0" in dbg:
            for dm in range(NKM):
                b.wait("sync", "SV", s_done[dm])
                b.emit("sync", lambda e, dm=dm, tsl=tsl: e.dma_start(
                    out=dbg["s0"][dm * 128:(dm + 1) * 128, tsl],
                    in_=xch[dm][:]), inc="DO", amt=16)
        mu_ms = None
        for dm in range(NKM):
            b.wait("tensor", "SV", s_done[dm])
            if dm == 0:
                b.wait("tensor", "SV", y3_done[14])
            xc = xch[dm]
            mu_ms = b.emit("tensor", lambda e, xc=xc, dm=dm: e.matmul(
                pbank[6][0:1, :], onesc[:], xc[:],
                start=(dm == 0), stop=(dm == NKM - 1)),
                inc="ST" if dm == NKM - 1 else None)
        sq_mm = [None] * NKM
        for dm in range(NKM):
            b.wait("scalar", "SV", s_done[dm])
            if dm >= 1:
                b.wait("scalar", "ST", sq_mm[dm - 1])
            xc = xch[dm]
            v = b.emit("scalar", lambda e, xc=xc: e.activation(
                sqs[:], xc[:], AF.Square), inc="SS")
            b.wait("tensor", "SS", v)
            if dm == 0:
                b.wait("tensor", "SV", y3_done[15])
            sq_mm[dm] = b.emit("tensor", lambda e, dm=dm: e.matmul(
                pbank[7][0:1, :], onescb[:], sqs[:],
                start=(dm == 0), stop=(dm == NKM - 1)), inc="ST")
        b.wait("scalar", "ST", mu_ms)
        b.emit("scalar", lambda e: e.activation(
            rowm[:], pbank[6][0:1, :], AF.Copy, scale=1.0 / DM))
        b.emit("scalar", lambda e: e.activation(rowv[:], rowm[:], AF.Square))
        b.wait("scalar", "ST", sq_mm[NKM - 1])
        var_s = b.emit("scalar", lambda e: e.activation(
            rowr[:], pbank[7][0:1, :], AF.Copy, scale=1.0 / DM), inc="SS")
        b.wait("vector", "SS", var_s)
        var_v = b.emit("vector", lambda e: e.tensor_tensor(
            rowv[:], rowr[:], rowv[:], OP.subtract), inc="SV")
        b.wait("scalar", "SV", var_v)
        b.emit("scalar", lambda e, bv=b.const_col(1e-5): e.activation(
            rowv[:], rowv[:], AF.Ln, bias=bv))
        rst_s = b.emit("scalar", lambda e: e.activation(
            rowv[:], rowv[:], AF.Exp, scale=-0.5), inc="SS")
        b.wait("tensor", "SS", rst_s)
        b.emit("tensor", lambda e: e.matmul(
            pbank[4][:], onesr[:], rowm[:], start=True, stop=True))
        bc_mm2 = b.emit("tensor", lambda e: e.matmul(
            pbank[5][:], onesr[:], rowv[:], start=True, stop=True), inc="ST")
        flush = [None] * NKM
        nv_last = None
        for dm in range(NKM):
            b.wait("vector", "ST", bc_mm2)
            xc = xch[dm]
            b.emit("vector", lambda e, xc=xc: e.tensor_tensor(
                xc[:], xc[:], pbank[4][:], OP.subtract))
            nv = b.emit("vector", lambda e, xc=xc: e.tensor_tensor(
                xc[:], xc[:], pbank[5][:], OP.mult), inc="SV")
            nv_last = nv
            cv = b.emit("vector", lambda e, xc=xc: e.tensor_copy(
                x1bf[:], xc[:]), inc="SV")
            b.wait("sync", "SV", cv)
            flush[dm] = b.ldg("sync", [
                (x1_scr[dm * 128:(dm + 1) * 128, tsl], xch[dm][:]),
                (x2_scr[dm * 128:(dm + 1) * 128,
                        1 + c * TC:1 + (c + 1) * TC], xch[dm][:]),
                (x1bf_scr[dm * 128:(dm + 1) * 128, 1 + c * TC:1 + (c + 1) * TC],
                 x1bf[:])])
            if "x1" in dbg:
                b.emit("sync", lambda e, dm=dm, tsl=tsl: e.dma_start(
                    out=dbg["x1"][dm * 128:(dm + 1) * 128, tsl],
                    in_=xch[dm][:]), inc="DO", amt=16)
            b.wait_tok("vector", flush[dm])
        if nctx is not None and INTERLEAVE and nctx["defer_xdbl"]:
            for dt2 in range(NDT):
                emit_xdbl(nctx, dt2)
        if nctx is not None and INTERLEAVE == 4:
            for dt2 in range(NDT):
                emit_m1_dt(nctx, dt2)
        if nctx is not None and INTERLEAVE >= 2:
            emit_m2(nctx, wout_nop[NKM - 1])
        if nctx is not None:
            nctx["flush"] = flush
            nctx["bankfree"] = {
                0: [("SV", nctx["conv"][NDT - 1]), ("SS", nctx["evu"][NDT - 1])],
                1: [("SS", nctx["res_s"][NDT - 1])],
                2: [("SS", nctx["bc_s"])],
                3: [("SS", nctx["sp"][NDT - 1])],
                4: [("SV", nv_last)], 5: [("SV", nv_last)],
                6: [("SS", var_s)], 7: [("SS", var_s)]}
        return flush[NKM - 1]

    ctx = new_m1_ctx(0, None)
    emit_xbf_load(ctx)
    load_win(ctx, 0)
    load_win(ctx, 1)
    for dt in range(NDT):
        emit_m1_dt(ctx, dt)
    emit_m2(ctx, None)
    prev_sc = None
    for c in range(NCH):
        sc = emit_scan(ctx, prev_sc)
        prev_sc = sc
        if c + 1 < NCH:
            nctx = new_m1_ctx(c + 1, sc["y3_done"])
            nctx["defer_xdbl"] = (INTERLEAVE == 3)
            b.wait("gpsimd", "ST", ctx["resmm"][NDT - 1])
            emit_xbf_load(nctx)
            load_win(nctx, 0)
            load_win(nctx, 1)
        else:
            nctx = None
        emit_m5(ctx, sc, nctx)
        if nctx is not None:
            if INTERLEAVE == 0:
                for dt in range(NDT):
                    emit_m1_dt(nctx, dt)
            if INTERLEAVE <= 1:
                emit_m2(nctx, sc["wout_nop"][NKM - 1])
            ctx = nctx

    mamba_ds = b.v("DS")

    def bar_all(dma=True):
        cur = {s: b.v(s) for s in ("SV", "SS", "ST", "SG")}
        if dma:
            b.wait("sync", "DS", b.v("DS"))
            b.wait("sync", "DO", b.v("DO"))
            for s in b._pool["sync"]:
                b.wait("sync", s, b.v(s))
            mks = b.emit("sync", lambda e: e.nop(), inc="MKS")
            for s in b._pool["gpsimd"]:
                b.wait("gpsimd", s, b.v(s))
            b.wait("gpsimd", "DO2", b.v("DO2"))
            mkg = b.emit("gpsimd", lambda e: e.nop(), inc="MKG")
            for eng in ENGS:
                b.wait(eng, "MKS", mks)
                b.wait(eng, "MKG", mkg)
        for eng in ENGS:
            for s, v in cur.items():
                b.wait(eng, s, v)

    b.free("aneg", "wx", "wdt")
    b.free(*[f"xbf{k}" for k in range(NKM)], "winb0", "winb1", "woutb0",
           "woutb1", "ubuf", "convA", "convB",
           *[f"sudu{i}" for i in range(NDT)], *[f"delta{i}" for i in range(NDT)],
           *[f"subf{i}" for i in range(NDT)], *[f"sres{i}" for i in range(NDT)],
           "y2buf", "dtbf", "bcsb", "dA0", "dA1",
           "dbu0", "dbu1", "hsl", "chsl0", "chsl1", "Bsl0", "Bsl1", "Csl0", "Csl1")
    bar_all()

    x2c = [A(f"x2c{dm}", 1 + TC) for dm in range(NKM)]
    onesP = b.sb("onesP", [128, 128])
    zcol = b.sb("zcol", [128, 1])

    b.emit("vector", lambda e: e.memset(onesP[:], 1.0), inc="SV")
    b.emit("vector", lambda e: e.memset(zcol[:], 0.0))
    mz = b.emit("vector", lambda e: e.memset(x1bf[:, 0:1], 0.0), inc="SV")
    b.wait("sync", "SV", mz)
    for dm in range(NKM):
        b.emit("sync", lambda e, dm=dm: e.dma_start(
            out=x1bf_scr[dm * 128:(dm + 1) * 128, 0:1], in_=x1bf[:, 0:1],
            allow_slow_non_contiguous=True), inc="DS", amt=16)
        b.emit("sync", lambda e, dm=dm: e.dma_start(
            out=x2bf_scr[dm * 128:(dm + 1) * 128, 0:1], in_=x1bf[:, 0:1],
            allow_slow_non_contiguous=True), inc="DS", amt=16)
        b.emit("sync", lambda e, dm=dm: e.dma_start(
            out=x2_scr[dm * 128:(dm + 1) * 128, 0:1], in_=zcol[:],
            allow_slow_non_contiguous=True), inc="DS", amt=16)
    b.wait("sync", "DS", mamba_ds)

    lnst = {"ss": 0, "sv": 0}

    def ln_stats_norm(s_views, out_emit, tag):
        b.wait("tensor", "SS", lnst["ss"])
        b.wait("tensor", "SV", lnst["sv"])
        mu_ms = None
        for dm in range(NKM):
            sv_ = s_views[dm]
            mu_ms = b.emit("tensor", lambda e, sv_=sv_, dm=dm: e.matmul(
                pbank[5][0:1, :], onesc[:], sv_,
                start=(dm == 0), stop=(dm == NKM - 1)),
                inc="ST" if dm == NKM - 1 else None)
        sq_mm = None
        for dm in range(NKM):
            sv_ = s_views[dm]
            if dm >= 1:
                b.wait("scalar", "ST", sq_mm)
            v = b.emit("scalar", lambda e, sv_=sv_: e.activation(
                sqs[:], sv_, AF.Square), inc="SS")
            b.wait("tensor", "SS", v)
            sq_mm = b.emit("tensor", lambda e, dm=dm: e.matmul(
                pbank[6][0:1, :], onescb[:], sqs[:],
                start=(dm == 0), stop=(dm == NKM - 1)), inc="ST")
        b.wait("scalar", "ST", mu_ms)
        b.emit("scalar", lambda e: e.activation(
            rowm[:], pbank[5][0:1, :], AF.Copy, scale=1.0 / DM))
        b.emit("scalar", lambda e: e.activation(rowv[:], rowm[:], AF.Square))
        b.wait("scalar", "ST", sq_mm)
        var_s = b.emit("scalar", lambda e: e.activation(
            rowr[:], pbank[6][0:1, :], AF.Copy, scale=1.0 / DM), inc="SS")
        b.wait("vector", "SS", var_s)
        var_v = b.emit("vector", lambda e: e.tensor_tensor(
            rowv[:], rowr[:], rowv[:], OP.subtract), inc="SV")
        b.wait("scalar", "SV", var_v)
        b.emit("scalar", lambda e, bv=b.const_col(1e-5): e.activation(
            rowv[:], rowv[:], AF.Ln, bias=bv))
        rst_s = b.emit("scalar", lambda e: e.activation(
            rowv[:], rowv[:], AF.Exp, scale=-0.5), inc="SS")
        b.wait("tensor", "SS", rst_s)
        b.emit("tensor", lambda e: e.matmul(
            pbank[7][:], onesr[:], rowm[:], start=True, stop=True))
        bc_mm2 = b.emit("tensor", lambda e: e.matmul(
            pbank[2][:], onesr[:], rowv[:], start=True, stop=True), inc="ST")
        for dm in range(NKM):
            b.wait("vector", "ST", bc_mm2)
            out_emit(dm, pbank[7][:], pbank[2][:])
        lnst["ss"] = b.v("SS")
        lnst["sv"] = b.v("SV")

    def ln_pipeline(groups, serial=False):
        n = len(groups)
        mu_l = [None] * n
        sq_l = [None] * n
        var_l = [None] * n
        rst_l = [None] * n
        bc2_l = [None] * n
        nv_l = [None] * n

        def stats(i):
            svs, _, prew = groups[i]
            if callable(prew):
                prew()
            else:
                for eng, sem, val in prew:
                    b.wait(eng, sem, val)
            if i >= 1:
                b.wait("tensor", "SS", var_l[i - 1])
            else:
                b.wait("tensor", "SS", lnst["ss"])
                b.wait("tensor", "SV", lnst["sv"])
            mu = None
            for dm in range(NKM):
                sv_ = svs[dm]
                mu = b.emit("tensor", lambda e, sv_=sv_, dm=dm: e.matmul(
                    pbank[5][0:1, :], onesc[:], sv_,
                    start=(dm == 0), stop=(dm == NKM - 1)),
                    inc="ST" if dm == NKM - 1 else None)
            mu_l[i] = mu
            sq = None
            for dm in range(NKM):
                sv_ = svs[dm]
                if dm >= 1:
                    b.wait("scalar", "ST", sq)
                elif i >= 1:
                    b.wait("scalar", "ST", sq_l[i - 1])
                v = b.emit("scalar", lambda e, sv_=sv_: e.activation(
                    sqs[:], sv_, AF.Square), inc="SS")
                b.wait("tensor", "SS", v)
                sq = b.emit("tensor", lambda e, dm=dm: e.matmul(
                    pbank[6][0:1, :], onescb[:], sqs[:],
                    start=(dm == 0), stop=(dm == NKM - 1)), inc="ST")
            sq_l[i] = sq

        def chain(i):
            b.wait("scalar", "ST", mu_l[i])
            if i >= 1:
                b.wait("scalar", "ST", bc2_l[i - 1])
            b.emit("scalar", lambda e: e.activation(
                rowm[:], pbank[5][0:1, :], AF.Copy, scale=1.0 / DM))
            b.emit("scalar", lambda e: e.activation(rowv[:], rowm[:], AF.Square))
            b.wait("scalar", "ST", sq_l[i])
            var_l[i] = b.emit("scalar", lambda e: e.activation(
                rowr[:], pbank[6][0:1, :], AF.Copy, scale=1.0 / DM), inc="SS")
            b.wait("vector", "SS", var_l[i])
            vv = b.emit("vector", lambda e: e.tensor_tensor(
                rowv[:], rowr[:], rowv[:], OP.subtract), inc="SV")
            b.wait("scalar", "SV", vv)
            b.emit("scalar", lambda e, bv=b.const_col(1e-5): e.activation(
                rowv[:], rowv[:], AF.Ln, bias=bv))
            rst_l[i] = b.emit("scalar", lambda e: e.activation(
                rowv[:], rowv[:], AF.Exp, scale=-0.5), inc="SS")

        def bcast_norm(i):
            _, out_emit, _ = groups[i]
            b.wait("tensor", "SS", rst_l[i])
            if i >= 1:
                b.wait("tensor", "SV", nv_l[i - 1])
            b.emit("tensor", lambda e: e.matmul(
                pbank[7][:], onesr[:], rowm[:], start=True, stop=True))
            bc2_l[i] = b.emit("tensor", lambda e: e.matmul(
                pbank[2][:], onesr[:], rowv[:], start=True, stop=True), inc="ST")
            for dm in range(NKM):
                b.wait("vector", "ST", bc2_l[i])
                out_emit(dm, pbank[7][:], pbank[2][:])
            nv_l[i] = b.v("SV")

        if serial:
            for i in range(n):
                stats(i)
                chain(i)
                bcast_norm(i)
        else:
            for i in range(n):
                stats(i)
                if i >= 1:
                    bcast_norm(i - 1)
                chain(i)
            bcast_norm(n - 1)
        lnst["ss"] = b.v("SS")
        lnst["sv"] = b.v("SV")

    def emit_ffn_layer(l):
        lw = b.v("DW")
        wqb = A("wqb", 8 * 512, BF16)
        wkb = A("wkb", 8 * 512, BF16)
        wvb = A("wvb", 8 * 512, BF16)
        tbA = A("tbA", L); tbB = A("tbB", L)
        qt = [A(f"qt{i}", L, BF16) for i in range(4)]
        kt = [A(f"kt{i}", L, BF16) for i in range(4)]
        vt = [A(f"vt{i}", 8 * 65, BF16) for i in range(16)]
        xin = [A(f"xin{k}", 1 + TC, BF16) for k in range(NKM)]
        qtmp = A("qtmp", TC); rq1 = A("rq1", TC); rq2 = A("rq2", TC)
        bar_all()
        qkv_tok = b.ldg("gpsimd", [
            (dst.rearrange("p (ki cc) -> p ki cc", ki=NKM),
             di[src_].rearrange("(ki p) cc -> p ki cc", p=128))
            for dst, src_ in ((wqb, f"wq_{l}"), (wkb, f"wk_{l}"), (wvb, f"wv_{l}"))])
        for dm in range(16):
            b.emit("vector", lambda e, dm=dm: e.memset(
                vt[dm][:].rearrange("p (h c) -> p h c", h=H)[:, :, 64:65], 1.0))
        vones_v = b.emit("vector", lambda e: e.memset(onesP[0:1, 0:1], 1.0), inc="SV")

        xsrc = x1bf_scr if l == 0 else x2bf_scr

        def qk_build(which, wslab, tgt, tb_src_a, tb_src_b, sc2, sc2r):
            b.wait("sync", "SV", b.v("SV"))
            b.wait("sync", "SS", b.v("SS"))
            tb_tok = b.ldg("sync", [(tbA[:], di[tb_src_a]),
                                    (tbB[:], di[tb_src_b])])
            b.wait_tok("scalar", tb_tok)
            b.emit("scalar", lambda e: e.activation(
                tbA[:], tbA[:], AF.Copy, scale=di_sc[sc2]))
            fold_s = b.emit("scalar", lambda e: e.activation(
                tbB[:], tbB[:], AF.Copy, scale=di_sc[sc2r]), inc="SS")
            for fc in range(NCH):
                t0, t1 = fc * TC, (fc + 1) * TC
                b.wait("sync", "ST", b.v("ST"))
                b.wait("sync", "DS", b.v("DS"))
                xin_tok = b.ldg("sync", [
                    (xin[k][:],
                     xsrc[k * 128:(k + 1) * 128, fc * TC:fc * TC + 1 + TC])
                    for k in range(NKM)])
                for fh in range(4):
                    bank = pbank[fh % 2]
                    b.wait_tok("tensor", qkv_tok)
                    b.wait_tok("tensor", xin_tok)
                    g = (qk_ev.get((which, fc, fh - 2))
                         or qk_ev.get((which, fc - 1, fh + 2))
                         or (qk_ev.get(("q", NCH - 1, fh + 2)) if which == "k"
                             else None) or 0)
                    b.wait("tensor", "SV", g)
                    if SERIAL & 1:
                        for _e in ENGS:
                            for _s in ("SV", "SS", "ST"):
                                b.wait(_e, _s, b.v(_s))
                    mlast = None
                    for ki in range(NKM):
                        wv_ = wslab[:, ki * 512 + fh * 128:ki * 512 + fh * 128 + 128]
                        rhs = xin[ki][:, 1:1 + TC] if ki < 4 else xin[ki][:, 0:TC]
                        mlast = b.emit("tensor", lambda e, wv_=wv_, rhs=rhs, ki=ki,
                                       bank=bank: e.matmul(
                            bank[:], wv_, rhs, start=(ki == 0), stop=(ki == NKM - 1)),
                            inc="ST" if ki == NKM - 1 else None)
                    b.wait("scalar", "ST", mlast)
                    ev = b.emit("scalar", lambda e, bank=bank: e.activation(
                        qtmp[:], bank[:], AF.Copy), inc="SS")
                    if l == DUMP_LAYER and which == "q" and "qraw" in dbg:
                        b.wait("sync", "SS", ev)
                        dq = b.emit("sync", lambda e, fh=fh, fc=fc: e.dma_start(
                            out=dbg["qraw"][fh * 128:(fh + 1) * 128,
                                            fc * TC:(fc + 1) * TC],
                            in_=qtmp[:]), inc="DO", amt=16)
                        b.wait("scalar", "DO", dq)
                        b.wait("vector", "DO", dq)
                    b.wait("tensor", "SS", ev)
                    rot_mm = b.emit("tensor", lambda e: e.matmul(
                        pbank[2][:], r2f[:], qtmp[:], start=True, stop=True),
                        inc="ST")
                    sq_s = b.emit("scalar", lambda e: e.activation(
                        sqs[:], qtmp[:], AF.Square), inc="SS")
                    b.wait("tensor", "SS", sq_s)
                    ss_mm = b.emit("tensor", lambda e: e.matmul(
                        pbank[3][0:2, :], bl2[:], sqs[:], start=True, stop=True),
                        inc="ST")
                    b.wait("scalar", "ST", ss_mm)
                    b.emit("scalar", lambda e, bv=b.const_col(1e-24): e.activation(
                        row2[:], pbank[3][0:2, :], AF.Ln, bias=bv))
                    rb_s = b.emit("scalar", lambda e: e.activation(
                        row2[:], row2[:], AF.Exp, scale=-0.5), inc="SS")
                    b.wait("tensor", "SS", rb_s)
                    rb_mm = b.emit("tensor", lambda e: e.matmul(
                        pbank[4][:], headsel[:], row2[:], start=True, stop=True),
                        inc="ST")
                    b.wait("vector", "SS", fold_s)
                    b.wait("vector", "ST", rb_mm)
                    tga = tgt[fh]
                    b.emit("vector", lambda e, t0=t0, t1=t1: e.tensor_tensor(
                        rq1[:], qtmp[:], tbA[:, t0:t1], OP.mult))
                    b.emit("vector", lambda e, t0=t0, t1=t1: e.tensor_tensor(
                        rq2[:], pbank[2][:], tbB[:, t0:t1], OP.mult))
                    b.emit("vector", lambda e: e.tensor_tensor(
                        rq1[:], rq1[:], rq2[:], OP.add))
                    qk_ev[(which, fc, fh)] = b.emit(
                        "vector", lambda e, tga=tga, t0=t0, t1=t1: e.tensor_tensor(
                            tga[:, t0:t1], rq1[:], pbank[4][:], OP.mult), inc="SV")
                    b.wait("tensor", "SV", qk_ev[(which, fc, fh)])
                    b.wait("scalar", "SV", qk_ev[(which, fc, fh)])

        qk_ev = {}
        qscol = b.sb(f"qscol{l}", [128, 4])
        sc_tok = b.ldg("sync", [
            (qscol[:, 0:1], di[f"qs2_{l}"]), (qscol[:, 1:2], di[f"qs2r_{l}"]),
            (qscol[:, 2:3], di[f"ks2_{l}"]), (qscol[:, 3:4], di[f"ks2r_{l}"])])
        di_sc = {f"qs2_{l}": qscol[:, 0:1], f"qs2r_{l}": qscol[:, 1:2],
                 f"ks2_{l}": qscol[:, 2:3], f"ks2r_{l}": qscol[:, 3:4]}
        b.wait_tok("scalar", sc_tok)
        qk_build("q", wqb, qt, "tab_csq", "tab_ssq", f"qs2_{l}", f"qs2r_{l}")
        if l == DUMP_LAYER and "qt" in dbg:
            for i4 in range(4):
                b.wait("gpsimd", "SV", b.v("SV"))
                b.emit("gpsimd", lambda e, i4=i4: e.dma_start(
                    out=dbg["qt"][i4 * 128:(i4 + 1) * 128, :], in_=qt[i4][:]),
                    inc="DO2", amt=16)
        qk_build("k", wkb, kt, "tab_csk", "tab_ssk", f"ks2_{l}", f"ks2r_{l}")
        if l == DUMP_LAYER and "kt" in dbg:
            for i4 in range(4):
                b.wait("gpsimd", "SV", b.v("SV"))
                b.emit("gpsimd", lambda e, i4=i4: e.dma_start(
                    out=dbg["kt"][i4 * 128:(i4 + 1) * 128, :], in_=kt[i4][:]),
                    inc="DO2", amt=16)
        v_ev = {}
        for fc in range(NCH):
            b.wait("sync", "ST", b.v("ST"))
            b.wait("sync", "DS", b.v("DS"))
            xin_tok = b.ldg("sync", [
                (xin[k][:],
                 xsrc[k * 128:(k + 1) * 128, fc * TC:fc * TC + 1 + TC])
                for k in range(NKM)])
            for jt in range(4):
                tix = fc * 4 + jt
                bank = pbank[tix % 2]
                b.wait_tok("tensor", xin_tok)
                b.wait("tensor", "SS", v_ev.get(tix - 2, 0))
                if tix < 2:
                    b.wait("tensor", "SV", qk_ev.get(("k", NCH - 1, 2 + tix), 0))
                mlast = None
                for ki in range(NKM):
                    lhs = (xin[ki][:, 1 + jt * 128:1 + (jt + 1) * 128] if ki < 4
                           else xin[ki][:, jt * 128:(jt + 1) * 128])
                    rhs = wvb[:, ki * 512:(ki + 1) * 512]
                    mlast = b.emit("tensor", lambda e, lhs=lhs, rhs=rhs, ki=ki,
                                   bank=bank: e.matmul(
                        bank[:], lhs, rhs, start=(ki == 0), stop=(ki == NKM - 1)),
                        inc="ST" if ki == NKM - 1 else None)
                b.wait("scalar", "ST", mlast)
                b.wait("scalar", "SV", vones_v)
                vtile = vt[tix]
                v_ev[tix] = b.emit("scalar", lambda e, vtile=vtile, bank=bank:
                                   e.activation(
                    vtile[:].rearrange("p (h c) -> p h c", h=H)[:, :, 0:64],
                    bank[:].rearrange("p (h c) -> p h c", h=H), AF.Copy),
                    inc="SS")
        bar_all(dma=False)
        b.free("wqb", "wkb", "wvb", "tbA", "tbB",
               *[f"xin{k}" for k in range(NKM)], "qtmp", "rq1", "rq2")

        ptsl = [A(f"ptsl{j}", 1536, BF16) for j in range(16)]
        for j in range(16):
            b.emit("vector", lambda e, j=j: e.memset(ptsl[j][:], 0.0))
        b.emit("vector", lambda e: e.memset(onesP[0:1, 0:1], 1.0), inc="SV")
        otb = [A(f"ot{h}", L, BF16, rows=64) for h in range(H)]
        pij = [A(f"pij{p}", 640, BF16) for p in range(2)]
        oscr = A("oscr", TC)
        den = A("den", TC)
        nmAB = b.sb(f"nmAB{l}", [128, 4])
        bar_all(dma=False)
        exp_ms = {}
        red_ms = {}
        m1_ms = {}
        tr_ms = {}
        ev_ms = {}
        avmm_ms = {}
        avml_ms = {}

        def geo(it):
            I0 = it * 128
            jlo = max(0, I0 - 512)
            return I0, jlo, I0 + 128 - jlo

        def sbanks(it):
            return (pbank[0], pbank[1]) if it % 2 == 0 else (pbank[2], pbank[3])

        def emit_sim(h, it):
            I0, jlo, w = geo(it)
            par = it % 2
            bl_, bh_ = sbanks(it)
            qsl = qt[h // 2]
            ksl = kt[h // 2]
            fhrow = (h % 2) * 64
            if it >= 2:
                b.wait("tensor", "SS", exp_ms[(h, it - 2)])
            elif h > 0:
                b.wait("tensor", "SS", exp_ms[(h - 1, 14 + par)])
            lhsT = qsl[fhrow:fhrow + 64, I0:I0 + 128]
            m1 = b.emit("tensor", lambda e, lhsT=lhsT, ksl=ksl, jlo=jlo, w=w,
                        fhrow=fhrow, bl_=bl_: e.matmul(
                bl_[:, 0:min(w, 512)], lhsT,
                ksl[fhrow:fhrow + 64, jlo:jlo + min(w, 512)],
                start=True, stop=True), inc="ST")
            if w > 512:
                m1 = b.emit("tensor", lambda e, lhsT=lhsT, ksl=ksl, jlo=jlo, w=w,
                            fhrow=fhrow, bh_=bh_: e.matmul(
                    bh_[:, 0:w - 512], lhsT,
                    ksl[fhrow:fhrow + 64, jlo + 512:jlo + w],
                    start=True, stop=True), inc="ST")
            m1_ms[(h, it)] = m1

        def emit_rowmax(h, it):
            I0, jlo, w = geo(it)
            par = it % 2
            bl_, bh_ = sbanks(it)
            nmA = nmAB[:, 2 * par:2 * par + 1]
            nmB = nmAB[:, 2 * par + 1:2 * par + 2]
            b.wait("vector", "ST", m1_ms[(h, it)])
            if I0 >= 512:
                b.emit("vector", lambda e, bl_=bl_: e.tensor_tensor(
                    bl_[:, 0:128], bl_[:, 0:128], mfij[:], OP.add))
            if w > 512:
                b.emit("vector", lambda e, bh_=bh_, w=w: e.tensor_tensor(
                    bh_[:, w - 512 - 128:w - 512], bh_[:, w - 512 - 128:w - 512],
                    mnij[:], OP.add))
                b.emit("vector", lambda e, bl_=bl_, nmA=nmA: e.tensor_reduce(
                    nmA, bl_[:, 0:512], AX.X, OP.max, negate=True))
                b.emit("vector", lambda e, bh_=bh_, w=w, nmB=nmB: e.tensor_reduce(
                    nmB, bh_[:, 0:w - 512], AX.X, OP.max, negate=True))
                red = b.emit("vector", lambda e, nmA=nmA, nmB=nmB: e.tensor_tensor(
                    nmA, nmA, nmB, OP.min), inc="SV")
            else:
                b.emit("vector", lambda e, bl_=bl_, w=w: e.tensor_tensor(
                    bl_[:, w - 128:w], bl_[:, w - 128:w], mnij[:], OP.add))
                red = b.emit("vector", lambda e, bl_=bl_, w=w, nmA=nmA:
                             e.tensor_reduce(
                    nmA, bl_[:, 0:w], AX.X, OP.max, negate=True), inc="SV")
            red_ms[(h, it)] = red

        def emit_exp(h, it):
            I0, jlo, w = geo(it)
            par = it % 2
            bl_, bh_ = sbanks(it)
            nmA = nmAB[:, 2 * par:2 * par + 1]
            pj = pij[par]
            b.wait("scalar", "SV", red_ms[(h, it)])
            if it >= 2:
                b.wait("scalar", "ST", tr_ms[(h, it - 2)])
            elif h > 0:
                b.wait("scalar", "ST", tr_ms[(h - 1, 14 + par)])
            if w > 512:
                b.emit("scalar", lambda e, pj=pj, bh_=bh_, w=w, nmA=nmA:
                       e.activation(pj[:, 512:w], bh_[:, 0:w - 512], AF.Exp,
                                    bias=nmA), inc="SS")
            ex = b.emit("scalar", lambda e, pj=pj, bl_=bl_, w=w, nmA=nmA:
                        e.activation(pj[:, 0:min(w, 512)], bl_[:, 0:min(w, 512)],
                                     AF.Exp, bias=nmA), inc="SS")
            exp_ms[(h, it)] = ex

        def emit_trans(h, it):
            I0, jlo, w = geo(it)
            par = it % 2
            tb = pbank[4] if par == 0 else pbank[6]
            pj = pij[par]
            b.wait("tensor", "SS", exp_ms[(h, it)])
            if it >= 2:
                b.wait("tensor", "SS", ev_ms[(h, it - 2)])
            elif h > 0:
                b.wait("tensor", "SS", ev_ms[(h - 1, 14 + par)])
            ntile = w // 128
            tr = None
            for k in range(ntile):
                dst = (tb[:, k * 128:(k + 1) * 128] if k < 4
                       else pbank[5][:, par * 128:par * 128 + 128])
                tr = b.emit("tensor", lambda e, dst=dst, pj=pj, k=k: e.matmul(
                    dst, pj[:, k * 128:(k + 1) * 128], i128bf[:],
                    start=True, stop=True),
                    inc="ST" if k == ntile - 1 else None)
            tr_ms[(h, it)] = tr

        def emit_evac(h, it):
            I0, jlo, w = geo(it)
            par = it % 2
            tb = pbank[4] if par == 0 else pbank[6]
            b.wait("scalar", "ST", tr_ms[(h, it)])
            if h > 0 and it in (0, 4, 8):
                b.wait("scalar", "ST", avmm_ms[(h - 1, 1 + it // 4)])
            ntile = w // 128
            ev = None
            for k in range(ntile):
                J = jlo // 128 + k
                src = (tb[:, k * 128:(k + 1) * 128] if k < 4
                       else pbank[5][:, par * 128:par * 128 + 128])
                c = 512 + I0 - J * 128
                ptile = ptsl[J]
                ev = b.emit("scalar", lambda e, ptile=ptile, src=src, c=c:
                            e.activation(ptile[:, c:c + 128], src, AF.Copy),
                            inc="SS" if k == ntile - 1 else None)
            ev_ms[(h, it)] = ev

        avev_ms = {}

        def emit_av_mm(h, ib, wtok):
            IB0 = ib * 512
            Jlo = max(0, (IB0 - 512) // 128)
            Jhi = (IB0 + 511) // 128
            b7 = pbank[7]
            prev = avml_ms.get((h, ib - 1), avml_ms.get((h - 1, 3), 0))
            b.wait("tensor", "SS", wtok)
            b.wait("tensor", "SV", prev)
            segs2 = list(range(Jlo, Jhi + 1))
            mlast = None
            for idx, J in enumerate(segs2):
                J0 = J * 128
                vtile = vt[J]
                ptile = ptsl[J]
                off = 512 + IB0 - J0
                mlast = b.emit("tensor", lambda e, vtile=vtile, ptile=ptile,
                               h=h, off=off, b7=b7, fj=(idx == 0),
                               lj=(idx == len(segs2) - 1): e.matmul(
                    b7[0:65, :], vtile[:, h * 65:h * 65 + 65],
                    ptile[:, off:off + 512], start=fj, stop=lj),
                    inc="ST" if idx == len(segs2) - 1 else None)
            avmm_ms[(h, ib)] = mlast
            b.wait("scalar", "ST", mlast)
            b.wait("scalar", "SV", prev)
            b.emit("scalar", lambda e, b7=b7: e.activation(
                den[64:65, :], b7[64:65, :], AF.Ln))
            b.emit("scalar", lambda e: e.activation(
                oscr[64:65, :], den[64:65, :], AF.Exp, scale=-1.0), inc="SS")
            avev_ms[(h, ib)] = b.emit("scalar", lambda e, b7=b7: e.activation(
                oscr[0:64, :], b7[0:64, :], AF.Copy), inc="SS")

        def emit_av_dn(h, ib):
            IB0 = ib * 512
            b7 = pbank[7]
            b.wait("tensor", "SS", avev_ms[(h, ib)])
            dn_mm = b.emit("tensor", lambda e, b7=b7: e.matmul(
                b7[0:64, :], onesP[64:65, 0:64], oscr[64:65, :],
                start=True, stop=True), inc="ST")
            b.wait("vector", "ST", dn_mm)
            oth = otb[h]
            avml_ms[(h, ib)] = b.emit("vector", lambda e, oth=oth, IB0=IB0,
                                      b7=b7: e.tensor_tensor(
                oth[:, IB0:IB0 + 512], oscr[0:64, :], b7[0:64, :], OP.mult),
                inc="SV")

        for h in range(H):
            if SERIAL & 2:
                for _e in ENGS:
                    for _s in ("SV", "SS", "ST"):
                        b.wait(_e, _s, b.v(_s))
            for it in range(17):
                if it < 16:
                    emit_sim(h, it)
                    emit_rowmax(h, it)
                if it >= 1:
                    emit_trans(h, it - 1)
                    emit_evac(h, it - 1)
                if it < 16:
                    emit_exp(h, it)
                if it == 1 and h > 0:
                    emit_av_mm(h - 1, 2, ev_ms[(h, 0)])
                elif it == 2 and h > 0:
                    emit_av_dn(h - 1, 2)
                    emit_av_mm(h - 1, 3, ev_ms[(h, 1)])
                elif it == 3 and h > 0:
                    emit_av_dn(h - 1, 3)
                elif it == 10:
                    emit_av_mm(h, 0, ev_ms[(h, 8)])
                elif it == 11:
                    emit_av_dn(h, 0)
                elif it == 14:
                    emit_av_mm(h, 1, ev_ms[(h, 12)])
                elif it == 15:
                    emit_av_dn(h, 1)
        d1 = b.emit("scalar", lambda e: e.activation(
            onesP[0:1, 0:1], onesP[0:1, 0:1], AF.Copy), inc="SS")
        emit_av_mm(H - 1, 2, d1)
        emit_av_dn(H - 1, 2)
        d2 = b.emit("scalar", lambda e: e.activation(
            onesP[0:1, 0:1], onesP[0:1, 0:1], AF.Copy), inc="SS")
        emit_av_mm(H - 1, 3, d2)
        emit_av_dn(H - 1, 3)
        if l == DUMP_LAYER and "ot" in dbg:
            for h2 in range(H):
                b.wait("gpsimd", "SV", b.v("SV"))
                b.emit("gpsimd", lambda e, h2=h2: e.dma_start(
                    out=dbg["ot"][h2 * 64:(h2 + 1) * 64, :], in_=otb[h2][0:64, :]),
                    inc="DO2", amt=16)
        bar_all(dma=False)
        b.free(*[f"ptsl{j}" for j in range(16)], "oscr", "den", "pij0", "pij1",
               *[f"qt{i}" for i in range(4)], *[f"kt{i}" for i in range(4)],
               *[f"vt{i}" for i in range(16)])

        woB = A("woB", H * DM, BF16, rows=64)
        hlnb = [A(f"hlnb{dm}", L, BF16) for dm in range(NKM)]
        fs1 = A("fs1", TC)
        x2full = [A(f"x2f{dm}", 1 + L) for dm in range(NKM)]
        wo_tok = b.ldg("gpsimd", [(
            woB.rearrange("p (hh cc) -> p hh cc", hh=H),
            di[f"wo_s_{l}"].rearrange("p (hh cc) -> p hh cc", hh=H))])
        lng = b.sb(f"lng{l}", [128, NKM]); lnb = b.sb(f"lnb{l}", [128, NKM])
        lnp_tok = b.ldg("sync", [(lng[:], di[f"lng_c_{l}"]),
                                 (lnb[:], di[f"lnb_c_{l}"])])
        bar_all(dma=False)
        b.wait("sync", "DS", b.v("DS"))
        x2f_tok = b.ldg("sync", [(
            x2full[dm][:], x2_scr[dm * 128:(dm + 1) * 128, 0:1 + L])
            for dm in range(NKM)])
        res_ms = {}
        for dm in range(NKM):
            bb = (dm % 2) * 4
            b.wait_tok("tensor", wo_tok)
            for fc in range(NCH):
                b.wait("tensor", "SV", res_ms.get((dm - 2, fc), 0))
            mlast = [None] * NCH
            for h in range(H):
                oth = otb[h]
                wv_ = woB[0:64, h * DM + dm * 128:h * DM + (dm + 1) * 128]
                for fc in range(NCH):
                    mlast[fc] = b.emit("tensor", lambda e, wv_=wv_, oth=oth,
                                       fc=fc, bb=bb, h=h: e.matmul(
                        pbank[bb + fc][:], wv_, oth[:, fc * TC:(fc + 1) * TC],
                        start=(h == 0), stop=(h == H - 1)),
                        inc="ST" if h == H - 1 else None)
            xv = x2full[dm]
            for fc in range(NCH):
                b.wait("vector", "ST", mlast[fc])
                if dm == 0 and fc == 0:
                    b.wait_tok("vector", x2f_tok)
                res_ms[(dm, fc)] = b.emit("vector", lambda e, xv=xv, fc=fc,
                                          bb=bb: e.tensor_tensor(
                    xv[:, 1 + fc * TC:1 + (fc + 1) * TC],
                    xv[:, 1 + fc * TC:1 + (fc + 1) * TC],
                    pbank[bb + fc][:], OP.add), inc="SV")
        groups = []
        for fc in range(NCH):
            t0 = 1 + fc * TC
            prew = []
            for dm in range(NKM):
                prew.append(("tensor", "SV", res_ms[(dm, fc)]))
                prew.append(("scalar", "SV", res_ms[(dm, fc)]))
            svs = [x2full[dm][:, t0:t0 + TC] if dm < 4
                   else x2full[dm][:, t0 - 1:t0 - 1 + TC]
                   for dm in range(NKM)]

            def emit_hln(dm, mu_b, rs_b, svs=svs, fc=fc):
                b.emit("vector", lambda e, dm=dm: e.tensor_tensor(
                    fs1[:], svs[dm], mu_b, OP.subtract))
                nv = b.emit("vector", lambda e, dm=dm: e.tensor_tensor(
                    fs1[:], fs1[:], rs_b, OP.mult), inc="SV")
                b.wait_tok("vector", lnp_tok)
                hv = b.emit("vector", lambda e, dm=dm, fc=fc: e.tensor_scalar(
                    hlnb[dm][:, fc * TC:(fc + 1) * TC], fs1[:],
                    lng[:, dm:dm + 1], lnb[:, dm:dm + 1], OP.mult, OP.add),
                    inc="SV")
            groups.append((svs, emit_hln, prew))
        ln_pipeline(groups)
        b.wait("sync", "SV", b.v("SV"))
        x2st_tok = b.ldg("sync", [(
            x2_scr[dm * 128:(dm + 1) * 128, 1:1 + L], x2full[dm][:, 1:1 + L])
            for dm in range(NKM)])
        b.wait_tok("sync", x2st_tok)
        bar_all(dma=False)
        b.free(*[f"ot{h}" for h in range(H)], "woB", "fs1",
               *[f"x2f{dm}" for dm in range(NKM)])

        w1b = [A(f"w1b{p}", 8 * 256, BF16) for p in range(2)]
        hmid = [A(f"hmid{j}", L, BF16) for j in range(NPAIR)]
        gact = [A(f"gact{p}", TC) for p in range(2)]
        b1c = b.sb(f"b1c{l}s", [128, 2 * NPAIR])
        b1_tok = b.ldg("sync", [(b1c[:], di[f"b1c_{l}"])])
        b.wait_tok("scalar", b1_tok)
        bar_all(dma=False)
        w1_dg = [None] * NPAIR
        w1_last = [None] * NPAIR

        def load_w1(j):
            if j >= NPAIR:
                return
            if j >= 2:
                b.wait("gpsimd", "ST", w1_last[j - 2])
            w1_dg[j] = b.ldg("gpsimd", [(
                w1b[j % 2].rearrange("p (ki cc) -> p ki cc", ki=NKM),
                di[f"w1_s_{l}"][j * DM:(j + 1) * DM, :]
                .rearrange("(ki p) cc -> p ki cc", p=128))])
        load_w1(0)
        load_w1(1)
        gg_ms = {}
        gs_ms = {}
        for j in range(NPAIR):
            wslab = w1b[j % 2]
            b.wait_tok("tensor", w1_dg[j])
            ml_a = [None] * NCH
            ml_g = [None] * NCH
            for fc in range(NCH):
                b.wait("tensor", "SV", gg_ms.get((j - 1, fc), 0))
                b.wait("tensor", "SS", gs_ms.get((j - 1, fc), 0))
                for ki in range(NKM):
                    wa = wslab[:, ki * 256:ki * 256 + 128]
                    hk = hlnb[ki]
                    ml_a[fc] = b.emit("tensor", lambda e, wa=wa, hk=hk, fc=fc,
                                      ki=ki: e.matmul(
                        pbank[fc][:], wa, hk[:, fc * TC:(fc + 1) * TC],
                        start=(ki == 0), stop=(ki == NKM - 1)),
                        inc="ST" if ki == NKM - 1 else None)
                for ki in range(NKM):
                    wg = wslab[:, ki * 256 + 128:ki * 256 + 256]
                    hk = hlnb[ki]
                    ml_g[fc] = b.emit("tensor", lambda e, wg=wg, hk=hk, fc=fc,
                                      ki=ki: e.matmul(
                        pbank[4 + fc][:], wg, hk[:, fc * TC:(fc + 1) * TC],
                        start=(ki == 0), stop=(ki == NKM - 1)),
                        inc="ST" if ki == NKM - 1 else None)
                b.wait("scalar", "ST", ml_g[fc])
                if fc >= 2 or j > 0:
                    b.wait("scalar", "SV",
                           gg_ms.get((j, fc - 2), gg_ms.get((j - 1, NCH - 2 + fc), 0)))
                gc = gact[fc % 2]
                g_s = b.emit("scalar", lambda e, j=j, fc=fc, gc=gc: e.activation(
                    gc[:], pbank[4 + fc][:], AF.Gelu,
                    bias=b1c[:, NPAIR + j:NPAIR + j + 1]), inc="SS")
                gs_ms[(j, fc)] = g_s
                b.wait("vector", "ST", ml_a[fc])
                b.wait("vector", "SS", g_s)
                hm = hmid[j]
                gg_ms[(j, fc)] = b.emit("vector", lambda e, hm=hm, j=j, fc=fc,
                                        gc=gc: e.scalar_tensor_tensor(
                    hm[:, fc * TC:(fc + 1) * TC], pbank[fc][:], b1c[:, j:j + 1],
                    gc[:], OP.add, OP.mult), inc="SV")
            w1_last[j] = ml_g[NCH - 1]
            load_w1(j + 2)
        if l == DUMP_LAYER and "hmid" in dbg:
            for j2 in range(NPAIR):
                b.wait("gpsimd", "SV", b.v("SV"))
                b.emit("gpsimd", lambda e, j2=j2: e.dma_start(
                    out=dbg["hmid"][j2 * 128:(j2 + 1) * 128, :], in_=hmid[j2][:]),
                    inc="DO2", amt=16)
        bar_all(dma=False)
        b.free("w1b0", "w1b1", *[f"hlnb{dm}" for dm in range(NKM)],
               "gact0", "gact1")
        w2bA = A("w2bA", 11 * DM, BF16)
        w2bB = A("w2bB", 11 * DM, BF16)
        w2_tok = b.ldg("gpsimd", [
            (w2bA.rearrange("p (ki cc) -> p ki cc", ki=11),
             di[f"w2_s_{l}"][0:11 * 128, :].rearrange("(ki p) cc -> p ki cc", p=128)),
            (w2bB.rearrange("p (ki cc) -> p ki cc", ki=11),
             di[f"w2_s_{l}"][11 * 128:, :].rearrange("(ki p) cc -> p ki cc",
                                                     p=128))])
        w2b_tok = w2_tok

        x2row = [A(f"x2row{p}", L) for p in range(2)]
        x2bfrow = A("x2bfrow", L, BF16)
        b2c = b.sb(f"b2c{l}s", [128, NKM])
        b2_tok = b.ldg("sync", [(b2c[:], di[f"b2c_{l}"])])
        bar_all(dma=False)
        b.wait_tok("vector", b2_tok)
        b.wait("sync", "DS", b.v("DS"))
        r2_ms = {}
        x2r_tok = [None] * NKM
        st_toks = {}
        x2bf_cv = {}
        x2r_tok[0] = b.ldg("sync", [(
            x2row[0][:], x2_scr[0:128, 1:1 + L])])
        if NKM > 1:
            x2r_tok[1] = b.ldg("sync", [(
                x2row[1][:], x2_scr[128:256, 1:1 + L])])
        for dm in range(NKM):
            bb = (dm % 2) * 4
            b.wait_tok("tensor", w2_tok)
            b.wait_tok("tensor", w2b_tok)
            for fc in range(NCH):
                b.wait("tensor", "SV", r2_ms.get((dm - 2, fc), 0))
            mlast = [None] * NCH
            for ki in range(NPAIR):
                wsl2 = w2bA if ki < 11 else w2bB
                kio = ki if ki < 11 else ki - 11
                wv_ = wsl2[:, kio * DM + dm * 128:kio * DM + (dm + 1) * 128]
                hm = hmid[ki]
                for fc in range(NCH):
                    mlast[fc] = b.emit("tensor", lambda e, wv_=wv_, hm=hm, fc=fc,
                                       ki=ki, bb=bb: e.matmul(
                        pbank[bb + fc][:], wv_, hm[:, fc * TC:(fc + 1) * TC],
                        start=(ki == 0), stop=(ki == NPAIR - 1)),
                        inc="ST" if ki == NPAIR - 1 else None)
            xr = x2row[dm % 2]
            for fc in range(NCH):
                b.wait("vector", "ST", mlast[fc])
                if fc == 0:
                    b.wait_tok("vector", x2r_tok[dm])
                r2_ms[(dm, fc)] = b.emit("vector", lambda e, xr=xr, dm=dm,
                                         fc=fc, bb=bb: e.scalar_tensor_tensor(
                    xr[:, fc * TC:(fc + 1) * TC], pbank[bb + fc][:],
                    b2c[:, dm:dm + 1], xr[:, fc * TC:(fc + 1) * TC],
                    OP.add, OP.add), inc="SV")
            b.wait("sync", "SV", r2_ms[(dm, NCH - 1)])
            st_tok = b.ldg("sync", [(
                x2_scr[dm * 128:(dm + 1) * 128, 1:1 + L], xr[:])])
            st_toks[dm] = st_tok
            if l == 0:
                if dm >= 1:
                    b.wait("vector", "SV", x2bf_cv.get(dm - 1, 0))
                cv = b.emit("vector", lambda e, xr=xr: e.tensor_copy(
                    x2bfrow[:], xr[:]), inc="SV")
                x2bf_cv[dm] = cv
                b.wait("sync", "SV", cv)
                sd = b.ldg("sync", [(
                    x2bf_scr[dm * 128:(dm + 1) * 128, 1:1 + L], x2bfrow[:])])
                b.wait_tok("vector", sd)
            if dm + 2 < NKM:
                b.wait_tok("sync", st_tok)
                if l == 0:
                    b.wait("sync", "SV", cv)
                x2r_tok[dm + 2] = b.ldg("sync", [(
                    x2row[(dm + 2) % 2][:],
                    x2_scr[(dm + 2) * 128:(dm + 3) * 128, 1:1 + L])])
        bar_all()
        b.free("w2bA", "w2bB", "x2row0", "x2row1", "x2bfrow",
               *[f"hmid{j}" for j in range(NPAIR)])
        bar_all()

    for l in range(2):
        emit_ffn_layer(l)

    fin = [[A(f"fin{p}_{dm}", TC) for dm in range(NKM)] for p in range(2)]
    x2d = [[A(f"x2d{p}_{dm}", TC) for dm in range(NKM)] for p in range(2)]
    bar_all()
    b.wait("sync", "DO", b.v("DO"))
    fin_toks = [None] * NCH
    dv_toks = [[] for _ in range(NCH)]

    def fin_load(fc):
        par = fc % 2
        if fc >= 2:
            for tok in dv_toks[fc - 2]:
                b.wait_tok("sync", tok)
        fin_toks[fc] = b.ldg("sync", [
            p_ for dm in range(NKM) for p_ in (
                (fin[par][dm][:],
                 x1_scr[dm * 128:(dm + 1) * 128, fc * TC:(fc + 1) * TC]),
                (x2d[par][dm][:],
                 x2_scr[dm * 128:(dm + 1) * 128,
                        1 + fc * TC:1 + (fc + 1) * TC]))])
    fin_load(0)
    fin_load(1)
    groups = []
    for fc in range(NCH):
        par = fc % 2
        svs = [x2d[par][dm][:] for dm in range(NKM)]

        def pre_emit(fc=fc, par=par):
            for dm in range(NKM):
                b.wait_tok("vector", fin_toks[fc])
                xv = x2d[par][dm]
                am = b.emit("vector", lambda e, xv=xv, par=par, dm=dm:
                            e.tensor_tensor(
                    xv[:], xv[:], fin[par][dm][:], OP.add), inc="SV")
                b.wait("tensor", "SV", am)
                b.wait("scalar", "SV", am)

        def emit_out(dm, mu_b, rs_b, svs=svs, fc=fc, par=par):
            b.emit("vector", lambda e, dm=dm: e.tensor_tensor(
                fin[par][dm][:], svs[dm], mu_b, OP.subtract))
            nv = b.emit("vector", lambda e, dm=dm: e.tensor_tensor(
                fin[par][dm][:], fin[par][dm][:], rs_b, OP.mult), inc="SV")
            b.wait("sync", "SV", nv)
            dv = b.ldg("sync", [(
                out[dm * 128:(dm + 1) * 128, fc * TC:(fc + 1) * TC],
                fin[par][dm][:])])
            dv_toks[fc].append(dv)
            b.wait_tok("vector", dv)
            if dm == NKM - 1 and fc + 2 < NCH:
                fin_load(fc + 2)
        groups.append((svs, emit_out, pre_emit))
    ln_pipeline(groups, serial=LNSER)
    for s in b._pool["sync"]:
        b.wait("sync", s, b.v(s))
    b.wait("sync", "DS", b.v("DS"))
    b.wait("sync", "DO", b.v("DO"))

    b.finish()
    cvals_np = np.zeros((128, 8), np.float32)
    for val, i in getattr(b, "cmap", {}).items():
        cvals_np[:, i] = val
    return nc, cvals_np


def kernel(**inputs):
    g, xts = host_prep(inputs)
    nc, cvals_np = build_program(DUMPS)
    in_maps = []
    for c in range(8):
        m = dict(g)
        m["xT"] = xts[c]
        m["cvals"] = cvals_np
        in_maps.append(m)
    res = run_bass_kernel_spmd(nc, in_maps, core_ids=list(range(8)))
    outs = [np.asarray(r["out"]).T for r in res.results]
    return np.stack(outs).astype(np.float32)


# version 17
# speedup vs baseline: 1.2435x; 1.0212x over previous
import contextlib
import numpy as np
import ml_dtypes
import concourse.bass as bass
import concourse.mybir as mybir
from concourse.bass_utils import run_bass_kernel_spmd
from concourse.alu_op_type import AluOpType as OP

F32 = mybir.dt.float32
BF16 = mybir.dt.bfloat16
AF = mybir.ActivationFunctionType
AX = mybir.AxisListType
BFD = ml_dtypes.bfloat16

L, DM, DI, NS, DTR = 2048, 1024, 2048, 16, 64
TC = 512
NCH = L // TC
NDT = DI // 128
NKM = DM // 128
H, DH, WIN = 8, 64, 512
HID, HIDP = 2730, 2816
NPAIR = HIDP // 128
NEGBIG = -1.0e30

ENGS = ("sync", "gpsimd", "scalar", "vector", "tensor")

DUMPS = ()
DUMP_LAYER = 0
INTERLEAVE = 4
DBGPOOL = False
LNSER = True
SERIAL = 0


class KB:
    def __init__(self, nc):
        self.nc = nc
        self.q = {e: [] for e in ENGS}
        self.st = contextlib.ExitStack()
        self.semh, self.semc = {}, {}

    def sem(self, *names):
        for n in names:
            self.semh[n] = self.st.enter_context(self.nc.semaphore(n))
            self.semc[n] = 0

    def sb(self, name, shape, dt=F32):
        return self.st.enter_context(self.nc.sbuf_tensor(name, list(shape), dt))

    def pb(self, name, shape=(128, 512)):
        return self.st.enter_context(self.nc.psum_tensor(name, list(shape), F32))

    def wait(self, eng, sem, val):
        if val is None or val <= 0:
            return
        h = self.semh[sem]
        self.q[eng].append(lambda e, h=h, v=val: e.wait_ge(h, v))

    def wait_tok(self, eng, tok):
        if tok is None or tok == 0:
            return
        self.wait(eng, tok[0], tok[1])

    def dma_pool_init(self, pools):
        self._pool = {}
        self._pidx = {}
        for eng, n in pools.items():
            names = [f"DQ_{eng}_{i}" for i in range(n)]
            self.sem(*names)
            self._pool[eng] = names
            self._pidx[eng] = 0

    def ldg(self, eng, pairs):
        names = self._pool[eng]
        sem = names[self._pidx[eng] % len(names)]
        self._pidx[eng] += 1
        self.wait(eng, sem, self.semc[sem])
        v = None
        for dst, src in pairs:
            v = self.emit(eng, lambda e, d=dst, s=src: e.dma_start(
                out=d, in_=s), inc=sem, amt=16)
        return (sem, v)

    def emit(self, eng, fn, inc=None, amt=1):
        if inc is None:
            self.q[eng].append(fn)
            return None
        h = self.semh[inc]
        self.semc[inc] += amt
        v = self.semc[inc]
        self.q[eng].append(lambda e, fn=fn, h=h, a=amt: fn(e).then_inc(h, a))
        return v

    def v(self, sem):
        return self.semc[sem]

    def arena_init(self, nbytes):
        self._ar = self.st.enter_context(
            self.nc.sbuf_tensor("ARENA", [128, nbytes // 4], F32))
        self._arb = self._ar.bitcast(BF16)
        self._free = [(0, nbytes)]
        self._live = {}

    def alloc(self, name, cols, dt=F32, rows=None):
        esz = 4 if dt == F32 else 2
        nb = (cols * esz + 31) & ~31
        self._hw = max(getattr(self, "_hw", 0),
                       sum(v[1] for v in self._live.values()) + nb)
        for i, (off, sz) in enumerate(self._free):
            if sz >= nb:
                self._free[i] = (off + nb, sz - nb)
                if self._free[i][1] == 0:
                    self._free.pop(i)
                self._live[name] = (off, nb)
                base = self._ar if dt == F32 else self._arb
                ap = base[:, off // esz: off // esz + cols]
                if rows is not None:
                    ap = ap[0:rows, :]
                return ap
        raise MemoryError(f"arena OOM for {name}: need {nb}, free={self._free}")

    def bfview(self, name, cols):
        off, nb = self._live[name]
        assert cols * 2 <= nb
        return self._arb[:, off // 2: off // 2 + cols]

    def free(self, *names):
        for name in names:
            off, nb = self._live.pop(name)
            self._free.append((off, nb))
        self._free.sort()
        out = []
        for off, sz in self._free:
            if out and out[-1][0] + out[-1][1] == off:
                out[-1] = (out[-1][0], out[-1][1] + sz)
            else:
                out.append((off, sz))
        self._free = [tuple(x) for x in out]

    def _cvals_init(self):
        if not hasattr(self, "_cvals"):
            self._cvals = self.sb("cvals_sb", [128, 8], F32)
            self.cmap = {}
        return self._cvals[:]

    def const_col(self, val):
        if not hasattr(self, "_cvals"):
            self._cvals = self.sb("cvals_sb", [128, 8], F32)
            self.cmap = {}
        if val not in self.cmap:
            i = len(self.cmap)
            assert i < 8
            self.nc.const_aps.register(F32, float(val), self._cvals[:, i:i + 1])
            self.cmap[val] = i
        return float(val)

    def finish(self):
        with self.st:
            with self.nc.Block() as blk:
                for eng in ENGS:
                    def mk(eng=eng):
                        def body(e):
                            for fn in self.q[eng]:
                                fn(e)
                        return body
                    getattr(blk, eng)(mk())


def _pack_tiles_rows(w):
    return np.ascontiguousarray(w, dtype=np.float32)


def host_prep(inputs):
    g = {}
    W_in = np.asarray(inputs["L0_attn_W_in"], np.float32)
    win_s = np.stack(
        [np.concatenate([W_in[:, dt * 128:(dt + 1) * 128],
                         W_in[:, DI + dt * 128:DI + (dt + 1) * 128]], axis=1)
         for dt in range(NDT)])
    g["win_s"] = np.ascontiguousarray(win_s.reshape(NDT * DM, 256))
    W_out = np.asarray(inputs["L0_attn_W_out"], np.float32)
    wout_s = np.stack([W_out[:, dm * 128:(dm + 1) * 128] for dm in range(NKM)])
    g["wout_s"] = np.ascontiguousarray(wout_s.reshape(NKM * DI, 128))
    W_x = np.asarray(inputs["L0_attn_W_x"], np.float32)
    g["wx_s"] = np.ascontiguousarray(
        W_x.reshape(NDT, 128, 96).transpose(1, 0, 2).reshape(128, NDT * 96))
    g["wdt_s"] = _pack_tiles_rows(inputs["L0_attn_W_dt"])
    g["bdt_c"] = np.ascontiguousarray(
        np.asarray(inputs["L0_attn_b_dt"], np.float32).reshape(NDT, 128).T)
    g["d_c"] = np.ascontiguousarray(
        np.asarray(inputs["L0_attn_D_param"], np.float32).reshape(NDT, 128).T)
    g["wconv_c"] = np.ascontiguousarray(
        np.asarray(inputs["L0_attn_W_conv"], np.float32)
        .reshape(NDT, 128, 3).transpose(1, 0, 2).reshape(128, NDT * 3))
    g["alog_c"] = np.ascontiguousarray(
        np.asarray(inputs["L0_attn_A_log"], np.float32)
        .reshape(NDT, 128, NS).transpose(1, 0, 2).reshape(128, NDT * NS))

    wqkv = np.asarray(inputs["L0_ffn_wqkv"], np.float32)
    for l in range(2):
        g[f"wq_{l}"] = np.ascontiguousarray(wqkv[l][:, 0:512])
        g[f"wk_{l}"] = np.ascontiguousarray(wqkv[l][:, 512:1024])
        g[f"wv_{l}"] = np.ascontiguousarray(wqkv[l][:, 1024:1536])
        wo = np.asarray(inputs["L0_ffn_wo"], np.float32)[l]
        g[f"wo_s_{l}"] = np.ascontiguousarray(
            wo.reshape(H, DH, DM).transpose(1, 0, 2).reshape(DH, H * DM))
        w1 = np.asarray(inputs["L0_ffn_w1"], np.float32)[l]
        w1p = np.zeros((DM, 2 * HIDP), np.float32)
        w1p[:, :HID] = w1[:, :HID]
        w1p[:, HIDP:HIDP + HID] = w1[:, HID:]
        w1_s = np.stack(
            [np.concatenate([w1p[:, j * 128:(j + 1) * 128],
                             w1p[:, HIDP + j * 128:HIDP + (j + 1) * 128]], 1)
             for j in range(NPAIR)])
        g[f"w1_s_{l}"] = np.ascontiguousarray(w1_s.reshape(NPAIR * DM, 256))
        b1 = np.asarray(inputs["L0_ffn_b1"], np.float32)[l]
        b1p = np.zeros(2 * HIDP, np.float32)
        b1p[:HID] = b1[:HID]
        b1p[HIDP:HIDP + HID] = b1[HID:]
        b1c = np.concatenate([b1p[:HIDP].reshape(NPAIR, 128).T,
                              b1p[HIDP:].reshape(NPAIR, 128).T], axis=1)
        g[f"b1c_{l}"] = np.ascontiguousarray(b1c)
        w2 = np.asarray(inputs["L0_ffn_w2"], np.float32)[l]
        w2p = np.zeros((HIDP, DM), np.float32)
        w2p[:HID] = w2
        g[f"w2_s_{l}"] = np.ascontiguousarray(w2p)
        g[f"b2c_{l}"] = np.ascontiguousarray(
            np.asarray(inputs["L0_ffn_b2"], np.float32)[l].reshape(NKM, 128).T)
        g[f"lng_c_{l}"] = np.ascontiguousarray(
            np.asarray(inputs["L0_ffn_ln_g"], np.float32)[l].reshape(NKM, 128).T)
        g[f"lnb_c_{l}"] = np.ascontiguousarray(
            np.asarray(inputs["L0_ffn_ln_b"], np.float32)[l].reshape(NKM, 128).T)
        qs = np.asarray(inputs["L0_ffn_q_scale"], np.float32)[l]
        ks = np.asarray(inputs["L0_ffn_k_scale"], np.float32)[l]
        g[f"qs2_{l}"] = np.ascontiguousarray(np.tile(qs, 2)[:, None])
        g[f"qs2r_{l}"] = np.ascontiguousarray(
            np.tile(np.concatenate([qs[32:], qs[:32]]), 2)[:, None])
        g[f"ks2_{l}"] = np.ascontiguousarray(np.tile(ks, 2)[:, None])
        g[f"ks2r_{l}"] = np.ascontiguousarray(
            np.tile(np.concatenate([ks[32:], ks[:32]]), 2)[:, None])

    half = np.arange(0, DH, 2, dtype=np.float64)
    inv_freq = 1.0 / (10000.0 ** (half / DH))
    t = np.arange(L, dtype=np.float64)
    freqs = t[:, None] * inv_freq[None, :]
    freqs = np.concatenate([freqs, freqs], 1)
    base = (half + 0.4 * DH) / (1.4 * DH)
    power = (t - L // 2) / float(WIN // 2)
    scale = base[None, :] ** power[:, None]
    scale = np.concatenate([scale, scale], 1)
    cos, sin = np.cos(freqs), np.sin(freqs)
    QK = 8.0
    csq = (QK * cos * scale).T.astype(np.float32)
    ssq = (QK * sin * scale).T.astype(np.float32)
    csk = (cos / scale).T.astype(np.float32)
    ssk = (sin / scale).T.astype(np.float32)
    g["tab_csq"] = np.ascontiguousarray(np.tile(csq, (2, 1)))
    g["tab_ssq"] = np.ascontiguousarray(np.tile(ssq, (2, 1)))
    g["tab_csk"] = np.ascontiguousarray(np.tile(csk, (2, 1)))
    g["tab_ssk"] = np.ascontiguousarray(np.tile(ssk, (2, 1)))
    R = np.zeros((DH, DH), np.float32)
    for f in range(32):
        R[f + 32, f] = -1.0
        R[f, f + 32] = 1.0
    g["r2f"] = np.ascontiguousarray(
        np.kron(np.eye(2, dtype=np.float32), R))
    g["i128"] = np.eye(128, dtype=np.float32)
    g["i128bf"] = np.eye(128, dtype=np.float32).astype(BFD)
    ii, jj = np.meshgrid(np.arange(128), np.arange(128), indexing="ij")
    m_near = np.where(jj <= ii, 0.0, NEGBIG).astype(np.float32)
    m_far = np.where(jj >= ii, 0.0, NEGBIG).astype(np.float32)
    g["mask_near_ij"] = m_near
    g["mask_far_ij"] = m_far
    g["maskT_near"] = np.ascontiguousarray(m_near.T)
    g["maskT_far"] = np.ascontiguousarray(m_far.T)
    g["ones_row"] = np.ones((1, 128), np.float32)
    g["ones_row_bf"] = np.ones((1, 128), np.float32).astype(BFD)
    g["ones_col"] = np.ones((128, 1), np.float32)
    g["ones_col_bf"] = np.ones((128, 1), np.float32).astype(BFD)
    bl2 = np.zeros((128, 2), np.float32)
    bl2[0:64, 0] = 1.0
    bl2[64:128, 1] = 1.0
    g["bl2_bf"] = bl2.astype(BFD)
    hs = np.zeros((2, 128), np.float32)
    hs[0, 0:64] = 1.0
    hs[1, 64:128] = 1.0
    g["headsel"] = hs

    x = np.asarray(inputs["L0_attn_x"], np.float32)
    xts = [np.ascontiguousarray(x[c].T) for c in range(x.shape[0])]
    for k in list(g):
        if (k in ("win_s", "wout_s", "wx_s", "wdt_s")
                or k.startswith(("wq_", "wk_", "wv_", "wo_s_", "w1_s_",
                                 "w2_s_"))):
            g[k] = np.ascontiguousarray(g[k].astype(ml_dtypes.bfloat16))
    return g, xts



def build_program(dumps=()):
    nc = bass.Bass("TRN2", target_bir_lowering=False)
    b = KB(nc)
    di = {}

    def inp(name, shape, dt=F32):
        di[name] = nc.dram_tensor(name, list(shape), dt, kind="ExternalInput").ap()
        return di[name]

    inp("xT", [DM, L])
    inp("win_s", [NDT * DM, 256], BF16); inp("wout_s", [NKM * DI, 128], BF16)
    inp("wx_s", [128, NDT * 96], BF16); inp("wdt_s", [64, DI], BF16)
    inp("bdt_c", [128, NDT]); inp("d_c", [128, NDT])
    inp("wconv_c", [128, NDT * 3]); inp("alog_c", [128, NDT * NS])
    for l in range(2):
        inp(f"wq_{l}", [DM, 512], BF16); inp(f"wk_{l}", [DM, 512], BF16)
        inp(f"wv_{l}", [DM, 512], BF16)
        inp(f"wo_s_{l}", [DH, H * DM], BF16)
        inp(f"w1_s_{l}", [NPAIR * DM, 256], BF16)
        inp(f"b1c_{l}", [128, 2 * NPAIR]); inp(f"w2_s_{l}", [HIDP, DM], BF16)
        inp(f"b2c_{l}", [128, NKM]); inp(f"lng_c_{l}", [128, NKM]); inp(f"lnb_c_{l}", [128, NKM])
        for nm in ("qs2", "qs2r", "ks2", "ks2r"):
            inp(f"{nm}_{l}", [128, 1])
    for nm in ("tab_csq", "tab_ssq", "tab_csk", "tab_ssk"):
        inp(nm, [128, L])
    inp("r2f", [128, 128]); inp("i128", [128, 128]); inp("i128bf", [128, 128], BF16)
    inp("cvals", [128, 8])
    inp("mask_near_ij", [128, 128]); inp("mask_far_ij", [128, 128])
    inp("maskT_near", [128, 128]); inp("maskT_far", [128, 128])
    inp("ones_row", [1, 128]); inp("ones_row_bf", [1, 128], BF16)
    inp("ones_col", [128, 1]); inp("ones_col_bf", [128, 1], BF16)
    inp("bl2_bf", [128, 2], BF16); inp("headsel", [2, 128])

    out = nc.dram_tensor("out", [DM, L], F32, kind="ExternalOutput").ap()
    dbg = {}
    for name in dumps:
        shape = {"x1": [DM, L], "du": [DI, L], "delta": [DI, L], "u0": [DI, L],
                 "yraw": [DI, L], "y3": [DI, L], "xdbl": [96, L],
                 "su": [DI, L], "qt": [512, L], "kt": [512, L],
                 "ot": [512, L], "xwo": [DM, L], "hln": [DM, L],
                 "hmid": [HIDP, L], "x2l": [DM, L],
                 "qraw": [512, L], "wops": [DM, L], "y2": [DI, L],
                 "ch00": [128, L], "s0": [DM, L], "sx": [DM, L]}[name]
        dbg[name] = nc.dram_tensor("dbg_" + name, shape, F32,
                                   kind="ExternalOutput").ap()

    bc_scr = nc.dram_tensor("bc_scr", [NCH * 32, TC], F32, kind="Internal").ap()
    x1_scr = nc.dram_tensor("x1_scr", [DM, L], F32, kind="Internal").ap()
    x1bf_scr = nc.dram_tensor("x1bf_scr", [DM, 1 + L], BF16, kind="Internal").ap()
    x2_scr = nc.dram_tensor("x2_scr", [DM, 1 + L], F32, kind="Internal").ap()
    x2bf_scr = nc.dram_tensor("x2bf_scr", [DM, 1 + L], BF16, kind="Internal").ap()

    b.sem("DW", "DG", "DR", "DS", "DO", "DO2", "SV", "SS", "ST", "SG", "MKS", "MKG")
    b.dma_pool_init({"sync": 12, "gpsimd": 8})

    bdt = b.sb("bdt", [128, NDT])
    dpar = b.sb("dpar", [128, NDT])
    wconv = b.sb("wconv", [128, NDT * 3])
    hstate = b.sb("hstate", [128, NDT * NS])
    ucarry = b.sb("ucarry", [128, NDT * 2])
    i128 = b.sb("i128s", [128, 128])
    i128bf = b.sb("i128bfs", [128, 128], BF16)
    onesr = b.sb("onesr", [1, 128]); onesrb = b.sb("onesrb", [1, 128], BF16)
    onesc = b.sb("onesc", [128, 1]); onescb = b.sb("onescb", [128, 1], BF16)
    bl2 = b.sb("bl2", [128, 2], BF16)
    headsel = b.sb("headsel_s", [2, 128])
    r2f = b.sb("r2fs", [128, 128])
    mnij = b.sb("mnij", [128, 128]); mfij = b.sb("mfij", [128, 128])
    mtn = b.sb("mtn", [128, 128]); mtf = b.sb("mtf", [128, 128])

    b.arena_init(190 * 1024)
    A = b.alloc
    aneg = A("aneg", NDT * NS)
    wx = A("wx", NDT * 96, BF16)
    wdt = A("wdt", DI, BF16, rows=64)
    xbf = [A(f"xbf{k}", TC, BF16) for k in range(NKM)]
    winb = [A(f"winb{p}", 8 * 256, BF16) for p in range(2)]
    woutb = [A(f"woutb{p}", 16 * 128, BF16) for p in range(2)]
    ubuf = A("ubuf", 2 + TC)
    convA = A("convA", TC); convB = A("convB", TC)
    sudu = [A(f"sudu{i}", TC) for i in range(NDT)]
    delta = [A(f"delta{i}", TC) for i in range(NDT)]
    subf = [A(f"subf{i}", TC, BF16) for i in range(NDT)]
    sres = [A(f"sres{i}", TC, BF16) for i in range(NDT)]
    y2buf = A("y2buf", TC)
    dtbf = A("dtbf", TC, BF16, rows=64)
    bcsb = A("bcsb", TC)
    dA2 = [A(f"dA{p}", TC) for p in range(2)]
    dbu2 = [A(f"dbu{p}", TC) for p in range(2)]
    hsl = A("hsl", TC)
    chsl = [A(f"chsl{p}", TC, BF16) for p in range(2)]
    Bsl = [A(f"Bsl{p}", TC) for p in range(2)]
    Csl = [A(f"Csl{p}", TC) for p in range(2)]
    sqs = b.sb("sqs", [128, TC], BF16)
    rowm = b.sb("rowm", [1, TC]); rowv = b.sb("rowv", [1, TC])
    rowr = b.sb("rowr", [1, TC])
    row2 = b.sb("row2", [2, TC])
    x1bf = b.sb("x1bf", [128, TC], BF16)

    xch = sudu[:NKM]
    pbank = [b.pb(f"pbank{i}") for i in range(8)]

    ld = b.ldg

    prelude_tok = b.ldg("sync", [
        (bdt[:], di["bdt_c"]), (dpar[:], di["d_c"]), (wconv[:], di["wconv_c"]),
        (aneg[:], di["alog_c"]), (i128[:], di["i128"]),
        (onesr[:], di["ones_row"]), (onesrb[:], di["ones_row_bf"]),
        (onesc[:], di["ones_col"]), (onescb[:], di["ones_col_bf"]),
        (bl2[:], di["bl2_bf"]), (headsel[:], di["headsel"]),
        (r2f[:], di["r2f"]), (i128bf[:], di["i128bf"]),
        (mnij[:], di["mask_near_ij"]),
        (mfij[:], di["mask_far_ij"]), (mtn[:], di["maskT_near"]),
        (mtf[:], di["maskT_far"]), (b._cvals_init(), di["cvals"])])
    wxdt_tok = b.ldg("gpsimd", [(wx[:], di["wx_s"]), (wdt[:], di["wdt_s"])])

    b.wait_tok("scalar", prelude_tok)
    an1 = b.emit("scalar", lambda e: e.activation(aneg[:], aneg[:], AF.Exp),
                 inc="SS")
    b.wait("scalar", "SS", an1)
    b.emit("scalar", lambda e: e.activation(aneg[:], aneg[:], AF.Copy,
                                            scale=-1.0), inc="SS")
    b.emit("vector", lambda e: e.memset(hstate[:], 0.0))
    b.emit("vector", lambda e: e.memset(ucarry[:], 0.0))
    b.emit("vector", lambda e: e.memset(ubuf[:], 0.0), inc="SV")

    bar = {"SV": b.v("SV"), "SS": b.v("SS"), "ST": b.v("ST"), "SG": b.v("SG")}
    for eng in ENGS:
        for s in ("SV", "SS", "ST", "SG"):
            b.wait(eng, s, bar[s])

    xch = [dA2[0], dA2[1], dbu2[0], dbu2[1], Bsl[0], Bsl[1], Csl[0], Csl[1]]
    y3b = [b.bfview(f"delta{i}", TC) for i in range(NDT)]

    def new_m1_ctx(c, ydp):
        return {"c": c, "tsl": slice(c * TC, (c + 1) * TC), "ydp": ydp,
                "win_dg": [None] * NDT, "umm": [None] * NDT,
                "resmm": [None] * NDT, "evu": [None] * NDT,
                "res_s": [None] * NDT, "conv": [None] * NDT,
                "xdbl_done": None, "xbf_tok": None, "su_s": [None] * NDT,
                "defer_xdbl": False,
                "du_done": [None] * NDT, "sp": [None] * NDT,
                "bc_tok": None, "bc_s": None, "dtbf_s": None}

    def emit_xbf_load(ctx):
        ctx["xbf_tok"] = b.ldg("gpsimd", [
            (xbf[k][:], di["xT"][k * 128:(k + 1) * 128, ctx["tsl"]])
            for k in range(NKM)])

    def load_win(ctx, dt):
        if dt >= NDT:
            return
        if dt >= 2:
            b.wait("gpsimd", "ST", ctx["resmm"][dt - 2])
        ctx["win_dg"][dt] = b.ldg("gpsimd", [(
            winb[dt % 2][:].rearrange("p (ki cc) -> p ki cc", ki=NKM),
            di["win_s"][dt * DM:(dt + 1) * DM, :]
            .rearrange("(ki p) cc -> p ki cc", p=128))])

    def emit_m1_dt(ctx, dt):
        c = ctx["c"]; tsl = ctx["tsl"]; ydp = ctx["ydp"]
        if SERIAL & 8:
            for _e in ENGS:
                for _s in ("SV", "SS", "ST"):
                    b.wait(_e, _s, b.v(_s))
        wslab = winb[dt % 2]
        b.wait_tok("tensor", ctx["win_dg"][dt])
        if dt == 0:
            b.wait_tok("tensor", ctx["xbf_tok"])
            b.wait_tok("tensor", wxdt_tok)
        if dt >= 1:
            b.wait("tensor", "SS", ctx["evu"][dt - 1])
            b.wait("tensor", "SS", ctx["res_s"][dt - 1])
            b.wait("tensor", "SV", ctx["conv"][dt - 1])
        elif ydp is not None:
            b.wait("tensor", "SV", ydp[8])
            b.wait("tensor", "SV", ydp[9])
        for ki in range(NKM):
            wv = wslab[:, ki * 256:ki * 256 + 128]
            last = ki == NKM - 1
            if last:
                ctx["umm"][dt] = b.emit("tensor", lambda e, wv=wv, ki=ki:
                                        e.matmul(
                    pbank[0][:], wv, xbf[ki][:], start=(ki == 0), stop=True),
                    inc="ST")
            else:
                b.emit("tensor", lambda e, wv=wv, ki=ki: e.matmul(
                    pbank[0][:], wv, xbf[ki][:], start=(ki == 0), stop=False))
        for ki in range(NKM):
            wv = wslab[:, ki * 256 + 128:ki * 256 + 256]
            last = ki == NKM - 1
            if last:
                ctx["resmm"][dt] = b.emit("tensor", lambda e, wv=wv, ki=ki:
                                          e.matmul(
                    pbank[1][:], wv, xbf[ki][:], start=(ki == 0), stop=True),
                    inc="ST")
            else:
                b.emit("tensor", lambda e, wv=wv, ki=ki: e.matmul(
                    pbank[1][:], wv, xbf[ki][:], start=(ki == 0), stop=False))
        load_win(ctx, dt + 2)
        b.wait("scalar", "ST", ctx["umm"][dt])
        if dt == 0 and c == 0:
            b.wait("scalar", "SV", b.v("SV"))
        ctx["evu"][dt] = b.emit("scalar", lambda e: e.activation(
            ubuf[:, 2:2 + TC], pbank[0][:], AF.Copy), inc="SS")
        if "u0" in dbg:
            b.wait("sync", "SS", ctx["evu"][dt])
            dver = b.emit("sync", lambda e, dt=dt, tsl=tsl: e.dma_start(
                out=dbg["u0"][dt * 128:(dt + 1) * 128, tsl],
                in_=ubuf[:, 2:2 + TC]), inc="DO", amt=16)
            b.wait("scalar", "DO", dver)
        b.wait("vector", "SS", ctx["evu"][dt])
        cpin = ucarry[:, dt * 2:dt * 2 + 2]
        cp1 = b.emit("vector", lambda e, cpin=cpin: e.tensor_copy(
            ubuf[:, 0:2], cpin), inc="SV")
        b.wait("vector", "SV", cp1)
        b.emit("vector", lambda e, cpin=cpin: e.tensor_copy(
            cpin, pbank[0][:, TC - 2:TC]))
        w0 = wconv[:, dt * 3:dt * 3 + 1]
        w1c = wconv[:, dt * 3 + 1:dt * 3 + 2]
        w2c = wconv[:, dt * 3 + 2:dt * 3 + 3]
        b.emit("vector", lambda e, w0=w0: e.tensor_scalar(
            convA[:], ubuf[:, 0:TC], w0, None, OP.mult))
        b.emit("vector", lambda e, w1c=w1c: e.scalar_tensor_tensor(
            convB[:], ubuf[:, 1:1 + TC], w1c, convA[:], OP.mult, OP.add))
        conv_v = b.emit("vector", lambda e, w2c=w2c: e.scalar_tensor_tensor(
            convA[:], ubuf[:, 2:2 + TC], w2c, convB[:], OP.mult, OP.add),
            inc="SV")
        ctx["conv"][dt] = conv_v
        b.wait("scalar", "SV", conv_v)
        sud = sudu[dt]; sub = subf[dt]; srb = sres[dt]
        b.emit("scalar", lambda e, sud=sud: e.activation(
            sud[:], convA[:], AF.Silu))
        su_s = b.emit("scalar", lambda e, sub=sub: e.activation(
            sub[:], convA[:], AF.Silu), inc="SS")
        b.wait("scalar", "ST", ctx["resmm"][dt])
        ctx["res_s"][dt] = b.emit("scalar", lambda e, srb=srb: e.activation(
            srb[:], pbank[1][:], AF.Silu), inc="SS")
        ctx["su_s"][dt] = su_s
        if not ctx.get("defer_xdbl"):
            emit_xdbl(ctx, dt)

    def emit_xdbl(ctx, dt):
        ydp = ctx["ydp"]
        b.wait("tensor", "SS", ctx["su_s"][dt])
        if dt == 0 and ydp is not None:
            b.wait("tensor", "SV", ydp[10])
        wxs = wx[:, dt * 96:(dt + 1) * 96]
        sub = subf[dt]
        ctx["xdbl_done"] = b.emit("tensor", lambda e, wxs=wxs, sub=sub, dt=dt:
                                  e.matmul(pbank[2][0:96, :], wxs, sub[:],
                                           start=(dt == 0),
                                           stop=(dt == NDT - 1)),
                                  inc="ST")

    def emit_m2(ctx, wout7):
        c = ctx["c"]; tsl = ctx["tsl"]; ydp = ctx["ydp"]
        b.wait("scalar", "ST", ctx["xdbl_done"])
        ctx["dtbf_s"] = b.emit("scalar", lambda e: e.activation(
            dtbf[:], pbank[2][0:64, :], AF.Copy), inc="SS")
        ctx["bc_s"] = b.emit("scalar", lambda e: e.activation(
            bcsb[64:96, :], pbank[2][64:96, :], AF.Copy), inc="SS")
        b.wait("sync", "SS", ctx["bc_s"])
        ctx["bc_tok"] = b.ldg("sync", [(bc_scr[c * 32:(c + 1) * 32, :],
                                        bcsb[64:96, :])])
        if wout7 is not None:
            b.wait("vector", "ST", wout7)
        for dt in range(NDT):
            b.wait("tensor", "SS", ctx["dtbf_s"])
            if dt >= 1:
                b.wait("tensor", "SS", ctx["sp"][dt - 1])
            elif ydp is not None:
                b.wait("tensor", "SV", ydp[11])
            wds = wdt[:, dt * 128:(dt + 1) * 128]
            dmm = b.emit("tensor", lambda e, wds=wds: e.matmul(
                pbank[3][:], wds, dtbf[:], start=True, stop=True), inc="ST")
            b.wait("scalar", "ST", dmm)
            if dt >= 1:
                b.wait("scalar", "SV", ctx["du_done"][dt - 1])
            bcol = bdt[:, dt:dt + 1]
            b.emit("scalar", lambda e, bcol=bcol: e.activation(
                convA[:], pbank[3][:], AF.Abs, bias=bcol))
            b.emit("scalar", lambda e: e.activation(
                convA[:], convA[:], AF.Exp, scale=-1.0))
            b.emit("scalar", lambda e: e.activation(
                convA[:], convA[:], AF.Ln, bias=1.0))
            ctx["sp"][dt] = b.emit("scalar", lambda e, bcol=bcol: e.activation(
                convB[:], pbank[3][:], AF.Relu, bias=bcol), inc="SS")
            b.wait("vector", "SS", ctx["sp"][dt])
            dl_ = delta[dt]; sud = sudu[dt]
            b.emit("vector", lambda e, dl_=dl_: e.tensor_tensor(
                dl_[:], convA[:], convB[:], OP.add))
            ctx["du_done"][dt] = b.emit("vector", lambda e, dl_=dl_, sud=sud:
                                        e.tensor_tensor(sud[:], dl_[:], sud[:],
                                                        OP.mult), inc="SV")
        if "su" in dbg:
            for dt in range(NDT):
                b.wait("gpsimd", "SS", ctx["res_s"][dt])
                b.emit("gpsimd", lambda e, dt=dt, tsl=tsl: e.dma_start(
                    out=dbg["su"][dt * 128:(dt + 1) * 128, tsl],
                    in_=subf[dt][:]), inc="DO2", amt=16)
        if "xdbl" in dbg:
            b.wait("gpsimd", "SS", ctx["bc_s"])
            b.emit("gpsimd", lambda e, tsl=tsl: e.dma_start(
                out=dbg["xdbl"][0:64, tsl], in_=dtbf[:]), inc="DO2", amt=16)
            b.wait("sync", "SS", ctx["bc_s"])
            b.emit("sync", lambda e, tsl=tsl: e.dma_start(
                out=dbg["xdbl"][64:96, tsl], in_=bcsb[64:96, :]), inc="DO", amt=16)
        if "delta" in dbg:
            for dt in range(NDT):
                b.wait("sync", "SV", ctx["du_done"][dt])
                b.emit("sync", lambda e, dt=dt, tsl=tsl: e.dma_start(
                    out=dbg["delta"][dt * 128:(dt + 1) * 128, tsl],
                    in_=delta[dt][:]), inc="DO", amt=16)
        if "du" in dbg:
            for dt in range(NDT):
                b.wait("sync", "SV", ctx["du_done"][dt])
                b.emit("sync", lambda e, dt=dt, tsl=tsl: e.dma_start(
                    out=dbg["du"][dt * 128:(dt + 1) * 128, tsl],
                    in_=sudu[dt][:]), inc="DO", amt=16)

    def emit_scan(ctx, prev_sc):
        c = ctx["c"]; tsl = ctx["tsl"]
        wout_dg = [None] * NKM
        wout_nop = [None] * NKM

        def load_wout(dm):
            if dm >= NKM:
                return
            if dm >= 2:
                b.wait("gpsimd", "ST", wout_nop[dm - 2])
            elif prev_sc is not None:
                b.wait("gpsimd", "ST", prev_sc["wout_nop"][6 + dm])
            wout_dg[dm] = b.ldg("gpsimd", [(
                woutb[dm % 2][:].rearrange("p (ki cc) -> p ki cc", ki=NDT),
                di["wout_s"][dm * DI:(dm + 1) * DI, :]
                .rearrange("(ki p) cc -> p ki cc", p=128))])
        load_wout(0)
        load_wout(1)
        flush = ctx.get("flush")
        bankfree = ctx.get("bankfree")
        if flush is None:
            b.wait("tensor", "SS", b.v("SS"))
            b.wait("tensor", "SV", b.v("SV"))
            b.wait("scalar", "SV", b.v("SV"))
            b.wait("gpsimd", "SV", b.v("SV"))
        scan_ms = {}
        scan_pe = {}
        repl_ms = {}
        nlast_ms = {}
        y3_done = [None] * NDT
        for half in range(2):
            for n in range(NS):
                hn = half * NS + n
                par = hn % 2
                b.wait_tok("sync", ctx["bc_tok"])
                if hn >= 2:
                    b.wait("sync", "SV", nlast_ms[hn - 2])
                elif flush is not None:
                    b.wait_tok("sync", flush[4 + par])
                    b.wait_tok("sync", flush[6 + par])
                repl_ms[hn] = b.ldg("sync", [
                    (Bsl[par][:],
                     bass.AP(bc_scr.tensor, (c * 32 + n) * TC, [[0, 128], [1, TC]])),
                    (Csl[par][:],
                     bass.AP(bc_scr.tensor, (c * 32 + 16 + n) * TC,
                             [[0, 128], [1, TC]]))])
                for k8 in range(8):
                    dt = half * 8 + k8
                    it = hn * 8 + k8
                    apar = it % 2
                    ipar = it % 2
                    b.wait("scalar", "SV", ctx["du_done"][dt])
                    if it >= 2:
                        b.wait("scalar", "SV", scan_ms.get(it - 2, 0))
                    elif flush is not None:
                        b.wait_tok("scalar", flush[it])
                    acol = aneg[:, dt * NS + n:dt * NS + n + 1]
                    dl_ = delta[dt]
                    da_s = b.emit("scalar", lambda e, apar=apar, acol=acol,
                                  dl_=dl_: e.activation(
                        dA2[apar][:], dl_[:], AF.Exp, scale=acol), inc="SS")
                    sud = sudu[dt]
                    b.wait_tok("vector", repl_ms[hn])
                    if it < 2 and flush is not None:
                        b.wait_tok("vector", flush[2 + it])
                    b.emit("vector", lambda e, sud=sud, par=par, ipar=ipar:
                           e.tensor_tensor(dbu2[ipar][:], sud[:],
                                           Bsl[par][:], OP.mult))
                    b.wait("vector", "SS", da_s)
                    if it >= 2:
                        b.wait("vector", "ST", scan_pe.get(it - 2, 0))
                    hinit = hstate[:, dt * NS + n:dt * NS + n + 1]
                    b.emit("vector", lambda e, apar=apar, hinit=hinit, ipar=ipar:
                           e.tensor_tensor_scan(hsl[:], dA2[apar][:],
                                                dbu2[ipar][:], hinit,
                                                OP.mult, OP.add))
                    chv = b.emit("vector", lambda e, par=par, ipar=ipar:
                                 e.tensor_tensor(chsl[ipar][:], hsl[:],
                                                 Csl[par][:], OP.mult), inc="SV")
                    sv1 = b.emit("vector", lambda e, hinit=hinit: e.tensor_copy(
                        hinit, hsl[:, TC - 1:TC]), inc="SV")
                    scan_ms[it] = sv1
                    if k8 == 7:
                        nlast_ms[hn] = sv1
                    b.wait("tensor", "SV", chv)
                    if n == 0 and half == 0 and bankfree is not None:
                        for sem_, val_ in bankfree[k8]:
                            b.wait("tensor", sem_, val_)
                    pe_ = b.emit("tensor", lambda e, k8=k8, ipar=ipar, n=n:
                                 e.matmul(pbank[k8][:], i128bf[:],
                                          chsl[ipar][:], start=(n == 0),
                                          stop=(n == NS - 1)), inc="ST")
                    scan_pe[it] = pe_
            for k8 in range(8):
                dt = half * 8 + k8
                dcol = dpar[:, dt:dt + 1]
                sub = subf[dt]; srb = sres[dt]; dl3 = y3b[dt]
                b.wait("vector", "ST", scan_pe[(half * NS + NS - 1) * 8 + k8])
                b.emit("vector", lambda e, sub=sub, dcol=dcol, k8=k8:
                       e.scalar_tensor_tensor(y2buf[:], sub[:], dcol,
                                              pbank[k8][:], OP.mult, OP.add))
                if "y2" in dbg:
                    yq = b.emit("vector", lambda e: e.memset(
                        onesr[0:1, 0:1], 1.0), inc="SV")
                    b.wait("sync", "SV", yq)
                    dq = b.ldg("sync", [(dbg["y2"][dt * 128:(dt + 1) * 128, tsl],
                                         y2buf[:])])
                    b.wait_tok("vector", dq)
                yev = b.emit("vector", lambda e, dl3=dl3, srb=srb:
                             e.tensor_tensor(dl3, y2buf[:], srb[:],
                                             OP.mult), inc="SV")
                y3_done[dt] = yev
        if "y3" in dbg:
            for dt in range(NDT):
                b.wait("gpsimd", "SV", y3_done[dt])
                b.emit("gpsimd", lambda e, dt=dt, tsl=tsl: e.dma_start(
                    out=dbg["y3"][dt * 128:(dt + 1) * 128, tsl],
                    in_=y3b[dt]), inc="DO2", amt=16)
        return dict(y3_done=y3_done, nlast=nlast_ms[2 * NS - 1],
                    wout_dg=wout_dg, wout_nop=wout_nop, load_wout=load_wout)

    def emit_m5(ctx, sc, nctx):
        c = ctx["c"]; tsl = ctx["tsl"]
        y3_done = sc["y3_done"]
        wout_dg = sc["wout_dg"]; wout_nop = sc["wout_nop"]
        b.wait("sync", "SV", sc["nlast"])
        xch_tok = b.ldg("sync", [
            (xch[k][:], di["xT"][k * 128:(k + 1) * 128, tsl])
            for k in range(NKM)])
        s_done = [None] * NKM
        b.wait("tensor", "SV", y3_done[NDT - 1])
        for dm in range(NKM):
            if nctx is not None and INTERLEAVE in (1, 2, 3):
                emit_m1_dt(nctx, 2 * dm)
                emit_m1_dt(nctx, 2 * dm + 1)
            b.wait_tok("tensor", wout_dg[dm])
            if DBGPOOL:
                for s_ in b._pool["gpsimd"]:
                    b.wait("tensor", s_, b.v(s_))
            if dm >= 2:
                b.wait("tensor", "SV", s_done[dm - 2])
            bank = pbank[4 + dm % 2]
            wslab = woutb[dm % 2]
            b.wait("tensor", "SV", y3_done[8 + 4 + dm % 2])
            for ki in range(NDT):
                wv = wslab[:, ki * 128:(ki + 1) * 128]
                dl3 = y3b[ki]
                last = ki == NDT - 1
                mm = b.emit("tensor", lambda e, wv=wv, dl3=dl3, ki=ki,
                            bank=bank: e.matmul(bank[:], wv, dl3,
                                                start=(ki == 0),
                                                stop=(ki == NDT - 1)),
                            inc="ST" if last else None)
                if last:
                    wout_nop[dm] = mm
            sc["load_wout"](dm + 2)
            b.wait("vector", "ST", wout_nop[dm])
            b.wait_tok("vector", xch_tok)
            if "sx" in dbg:
                b.wait_tok("sync", xch_tok)
                dsx = b.emit("sync", lambda e, dm=dm, tsl=tsl: e.dma_start(
                    out=dbg["sx"][dm * 128:(dm + 1) * 128, tsl],
                    in_=xch[dm][:]), inc="DO", amt=16)
                b.wait("vector", "DO", dsx)
            xc = xch[dm]
            s_done[dm] = b.emit("vector", lambda e, xc=xc, bank=bank:
                                e.tensor_tensor(xc[:], xc[:], bank[:], OP.add),
                                inc="SV")
        if "s0" in dbg:
            for dm in range(NKM):
                b.wait("sync", "SV", s_done[dm])
                b.emit("sync", lambda e, dm=dm, tsl=tsl: e.dma_start(
                    out=dbg["s0"][dm * 128:(dm + 1) * 128, tsl],
                    in_=xch[dm][:]), inc="DO", amt=16)
        mu_ms = None
        for dm in range(NKM):
            b.wait("tensor", "SV", s_done[dm])
            if dm == 0:
                b.wait("tensor", "SV", y3_done[14])
            xc = xch[dm]
            mu_ms = b.emit("tensor", lambda e, xc=xc, dm=dm: e.matmul(
                pbank[6][0:1, :], onesc[:], xc[:],
                start=(dm == 0), stop=(dm == NKM - 1)),
                inc="ST" if dm == NKM - 1 else None)
        sq_mm = [None] * NKM
        for dm in range(NKM):
            b.wait("scalar", "SV", s_done[dm])
            if dm >= 1:
                b.wait("scalar", "ST", sq_mm[dm - 1])
            xc = xch[dm]
            v = b.emit("scalar", lambda e, xc=xc: e.activation(
                sqs[:], xc[:], AF.Square), inc="SS")
            b.wait("tensor", "SS", v)
            if dm == 0:
                b.wait("tensor", "SV", y3_done[15])
            sq_mm[dm] = b.emit("tensor", lambda e, dm=dm: e.matmul(
                pbank[7][0:1, :], onescb[:], sqs[:],
                start=(dm == 0), stop=(dm == NKM - 1)), inc="ST")
        b.wait("scalar", "ST", mu_ms)
        b.emit("scalar", lambda e: e.activation(
            rowm[:], pbank[6][0:1, :], AF.Copy, scale=1.0 / DM))
        b.emit("scalar", lambda e: e.activation(rowv[:], rowm[:], AF.Square))
        b.wait("scalar", "ST", sq_mm[NKM - 1])
        var_s = b.emit("scalar", lambda e: e.activation(
            rowr[:], pbank[7][0:1, :], AF.Copy, scale=1.0 / DM), inc="SS")
        b.wait("vector", "SS", var_s)
        var_v = b.emit("vector", lambda e: e.tensor_tensor(
            rowv[:], rowr[:], rowv[:], OP.subtract), inc="SV")
        b.wait("scalar", "SV", var_v)
        b.emit("scalar", lambda e, bv=b.const_col(1e-5): e.activation(
            rowv[:], rowv[:], AF.Ln, bias=bv))
        rst_s = b.emit("scalar", lambda e: e.activation(
            rowv[:], rowv[:], AF.Exp, scale=-0.5), inc="SS")
        b.wait("tensor", "SS", rst_s)
        b.emit("tensor", lambda e: e.matmul(
            pbank[4][:], onesr[:], rowm[:], start=True, stop=True))
        bc_mm2 = b.emit("tensor", lambda e: e.matmul(
            pbank[5][:], onesr[:], rowv[:], start=True, stop=True), inc="ST")
        flush = [None] * NKM
        nv_last = None
        for dm in range(NKM):
            b.wait("vector", "ST", bc_mm2)
            xc = xch[dm]
            b.emit("vector", lambda e, xc=xc: e.tensor_tensor(
                xc[:], xc[:], pbank[4][:], OP.subtract))
            nv = b.emit("vector", lambda e, xc=xc: e.tensor_tensor(
                xc[:], xc[:], pbank[5][:], OP.mult), inc="SV")
            nv_last = nv
            cv = b.emit("vector", lambda e, xc=xc: e.tensor_copy(
                x1bf[:], xc[:]), inc="SV")
            b.wait("sync", "SV", cv)
            flush[dm] = b.ldg("sync", [
                (x1_scr[dm * 128:(dm + 1) * 128, tsl], xch[dm][:]),
                (x2_scr[dm * 128:(dm + 1) * 128,
                        1 + c * TC:1 + (c + 1) * TC], xch[dm][:]),
                (x1bf_scr[dm * 128:(dm + 1) * 128, 1 + c * TC:1 + (c + 1) * TC],
                 x1bf[:])])
            if "x1" in dbg:
                b.emit("sync", lambda e, dm=dm, tsl=tsl: e.dma_start(
                    out=dbg["x1"][dm * 128:(dm + 1) * 128, tsl],
                    in_=xch[dm][:]), inc="DO", amt=16)
            b.wait_tok("vector", flush[dm])
        if nctx is not None and INTERLEAVE and nctx["defer_xdbl"]:
            for dt2 in range(NDT):
                emit_xdbl(nctx, dt2)
        if nctx is not None and INTERLEAVE == 4:
            for dt2 in range(NDT):
                emit_m1_dt(nctx, dt2)
        if nctx is not None and INTERLEAVE >= 2:
            emit_m2(nctx, wout_nop[NKM - 1])
        if nctx is not None:
            nctx["flush"] = flush
            nctx["bankfree"] = {
                0: [("SV", nctx["conv"][NDT - 1]), ("SS", nctx["evu"][NDT - 1])],
                1: [("SS", nctx["res_s"][NDT - 1])],
                2: [("SS", nctx["bc_s"])],
                3: [("SS", nctx["sp"][NDT - 1])],
                4: [("SV", nv_last)], 5: [("SV", nv_last)],
                6: [("SS", var_s)], 7: [("SS", var_s)]}
        return flush[NKM - 1]

    ctx = new_m1_ctx(0, None)
    emit_xbf_load(ctx)
    load_win(ctx, 0)
    load_win(ctx, 1)
    for dt in range(NDT):
        emit_m1_dt(ctx, dt)
    emit_m2(ctx, None)
    prev_sc = None
    for c in range(NCH):
        sc = emit_scan(ctx, prev_sc)
        prev_sc = sc
        if c + 1 < NCH:
            nctx = new_m1_ctx(c + 1, sc["y3_done"])
            nctx["defer_xdbl"] = (INTERLEAVE == 3)
            b.wait("gpsimd", "ST", ctx["resmm"][NDT - 1])
            emit_xbf_load(nctx)
            load_win(nctx, 0)
            load_win(nctx, 1)
        else:
            nctx = None
        emit_m5(ctx, sc, nctx)
        if nctx is not None:
            if INTERLEAVE == 0:
                for dt in range(NDT):
                    emit_m1_dt(nctx, dt)
            if INTERLEAVE <= 1:
                emit_m2(nctx, sc["wout_nop"][NKM - 1])
            ctx = nctx

    mamba_ds = b.v("DS")

    def bar_all(dma=True):
        cur = {s: b.v(s) for s in ("SV", "SS", "ST", "SG")}
        if dma:
            b.wait("sync", "DS", b.v("DS"))
            b.wait("sync", "DO", b.v("DO"))
            for s in b._pool["sync"]:
                b.wait("sync", s, b.v(s))
            mks = b.emit("sync", lambda e: e.nop(), inc="MKS")
            for s in b._pool["gpsimd"]:
                b.wait("gpsimd", s, b.v(s))
            b.wait("gpsimd", "DO2", b.v("DO2"))
            mkg = b.emit("gpsimd", lambda e: e.nop(), inc="MKG")
            for eng in ENGS:
                b.wait(eng, "MKS", mks)
                b.wait(eng, "MKG", mkg)
        for eng in ENGS:
            for s, v in cur.items():
                b.wait(eng, s, v)

    b.free("aneg", "wx", "wdt")
    b.free(*[f"xbf{k}" for k in range(NKM)], "winb0", "winb1", "woutb0",
           "woutb1", "ubuf", "convA", "convB",
           *[f"sudu{i}" for i in range(NDT)], *[f"delta{i}" for i in range(NDT)],
           *[f"subf{i}" for i in range(NDT)], *[f"sres{i}" for i in range(NDT)],
           "y2buf", "dtbf", "bcsb", "dA0", "dA1",
           "dbu0", "dbu1", "hsl", "chsl0", "chsl1", "Bsl0", "Bsl1", "Csl0", "Csl1")
    bar_all()

    x2c = [A(f"x2c{dm}", 1 + TC) for dm in range(NKM)]
    onesP = b.sb("onesP", [128, 128])
    zcol = b.sb("zcol", [128, 1])

    b.emit("vector", lambda e: e.memset(onesP[:], 1.0), inc="SV")
    b.emit("vector", lambda e: e.memset(zcol[:], 0.0))
    mz = b.emit("vector", lambda e: e.memset(x1bf[:, 0:1], 0.0), inc="SV")
    b.wait("sync", "SV", mz)
    for dm in range(NKM):
        b.emit("sync", lambda e, dm=dm: e.dma_start(
            out=x1bf_scr[dm * 128:(dm + 1) * 128, 0:1], in_=x1bf[:, 0:1],
            allow_slow_non_contiguous=True), inc="DS", amt=16)
        b.emit("sync", lambda e, dm=dm: e.dma_start(
            out=x2bf_scr[dm * 128:(dm + 1) * 128, 0:1], in_=x1bf[:, 0:1],
            allow_slow_non_contiguous=True), inc="DS", amt=16)
        b.emit("sync", lambda e, dm=dm: e.dma_start(
            out=x2_scr[dm * 128:(dm + 1) * 128, 0:1], in_=zcol[:],
            allow_slow_non_contiguous=True), inc="DS", amt=16)
    b.wait("sync", "DS", mamba_ds)

    lnst = {"ss": 0, "sv": 0}

    def ln_stats_norm(s_views, out_emit, tag):
        b.wait("tensor", "SS", lnst["ss"])
        b.wait("tensor", "SV", lnst["sv"])
        mu_ms = None
        for dm in range(NKM):
            sv_ = s_views[dm]
            mu_ms = b.emit("tensor", lambda e, sv_=sv_, dm=dm: e.matmul(
                pbank[5][0:1, :], onesc[:], sv_,
                start=(dm == 0), stop=(dm == NKM - 1)),
                inc="ST" if dm == NKM - 1 else None)
        sq_mm = None
        for dm in range(NKM):
            sv_ = s_views[dm]
            if dm >= 1:
                b.wait("scalar", "ST", sq_mm)
            v = b.emit("scalar", lambda e, sv_=sv_: e.activation(
                sqs[:], sv_, AF.Square), inc="SS")
            b.wait("tensor", "SS", v)
            sq_mm = b.emit("tensor", lambda e, dm=dm: e.matmul(
                pbank[6][0:1, :], onescb[:], sqs[:],
                start=(dm == 0), stop=(dm == NKM - 1)), inc="ST")
        b.wait("scalar", "ST", mu_ms)
        b.emit("scalar", lambda e: e.activation(
            rowm[:], pbank[5][0:1, :], AF.Copy, scale=1.0 / DM))
        b.emit("scalar", lambda e: e.activation(rowv[:], rowm[:], AF.Square))
        b.wait("scalar", "ST", sq_mm)
        var_s = b.emit("scalar", lambda e: e.activation(
            rowr[:], pbank[6][0:1, :], AF.Copy, scale=1.0 / DM), inc="SS")
        b.wait("vector", "SS", var_s)
        var_v = b.emit("vector", lambda e: e.tensor_tensor(
            rowv[:], rowr[:], rowv[:], OP.subtract), inc="SV")
        b.wait("scalar", "SV", var_v)
        b.emit("scalar", lambda e, bv=b.const_col(1e-5): e.activation(
            rowv[:], rowv[:], AF.Ln, bias=bv))
        rst_s = b.emit("scalar", lambda e: e.activation(
            rowv[:], rowv[:], AF.Exp, scale=-0.5), inc="SS")
        b.wait("tensor", "SS", rst_s)
        b.emit("tensor", lambda e: e.matmul(
            pbank[7][:], onesr[:], rowm[:], start=True, stop=True))
        bc_mm2 = b.emit("tensor", lambda e: e.matmul(
            pbank[2][:], onesr[:], rowv[:], start=True, stop=True), inc="ST")
        for dm in range(NKM):
            b.wait("vector", "ST", bc_mm2)
            out_emit(dm, pbank[7][:], pbank[2][:])
        lnst["ss"] = b.v("SS")
        lnst["sv"] = b.v("SV")

    def ln_pipeline(groups, serial=False):
        n = len(groups)
        mu_l = [None] * n
        sq_l = [None] * n
        var_l = [None] * n
        rst_l = [None] * n
        bc2_l = [None] * n
        nv_l = [None] * n

        def stats(i):
            svs, _, prew = groups[i]
            if callable(prew):
                prew()
            else:
                for eng, sem, val in prew:
                    b.wait(eng, sem, val)
            if i >= 1:
                b.wait("tensor", "SS", var_l[i - 1])
            else:
                b.wait("tensor", "SS", lnst["ss"])
                b.wait("tensor", "SV", lnst["sv"])
            mu = None
            for dm in range(NKM):
                sv_ = svs[dm]
                mu = b.emit("tensor", lambda e, sv_=sv_, dm=dm: e.matmul(
                    pbank[5][0:1, :], onesc[:], sv_,
                    start=(dm == 0), stop=(dm == NKM - 1)),
                    inc="ST" if dm == NKM - 1 else None)
            mu_l[i] = mu
            sq = None
            for dm in range(NKM):
                sv_ = svs[dm]
                if dm >= 1:
                    b.wait("scalar", "ST", sq)
                elif i >= 1:
                    b.wait("scalar", "ST", sq_l[i - 1])
                v = b.emit("scalar", lambda e, sv_=sv_: e.activation(
                    sqs[:], sv_, AF.Square), inc="SS")
                b.wait("tensor", "SS", v)
                sq = b.emit("tensor", lambda e, dm=dm: e.matmul(
                    pbank[6][0:1, :], onescb[:], sqs[:],
                    start=(dm == 0), stop=(dm == NKM - 1)), inc="ST")
            sq_l[i] = sq

        def chain(i):
            b.wait("scalar", "ST", mu_l[i])
            if i >= 1:
                b.wait("scalar", "ST", bc2_l[i - 1])
            b.emit("scalar", lambda e: e.activation(
                rowm[:], pbank[5][0:1, :], AF.Copy, scale=1.0 / DM))
            b.emit("scalar", lambda e: e.activation(rowv[:], rowm[:], AF.Square))
            b.wait("scalar", "ST", sq_l[i])
            var_l[i] = b.emit("scalar", lambda e: e.activation(
                rowr[:], pbank[6][0:1, :], AF.Copy, scale=1.0 / DM), inc="SS")
            b.wait("vector", "SS", var_l[i])
            vv = b.emit("vector", lambda e: e.tensor_tensor(
                rowv[:], rowr[:], rowv[:], OP.subtract), inc="SV")
            b.wait("scalar", "SV", vv)
            b.emit("scalar", lambda e, bv=b.const_col(1e-5): e.activation(
                rowv[:], rowv[:], AF.Ln, bias=bv))
            rst_l[i] = b.emit("scalar", lambda e: e.activation(
                rowv[:], rowv[:], AF.Exp, scale=-0.5), inc="SS")

        def bcast_norm(i):
            _, out_emit, _ = groups[i]
            b.wait("tensor", "SS", rst_l[i])
            if i >= 1:
                b.wait("tensor", "SV", nv_l[i - 1])
            b.emit("tensor", lambda e: e.matmul(
                pbank[7][:], onesr[:], rowm[:], start=True, stop=True))
            bc2_l[i] = b.emit("tensor", lambda e: e.matmul(
                pbank[2][:], onesr[:], rowv[:], start=True, stop=True), inc="ST")
            for dm in range(NKM):
                b.wait("vector", "ST", bc2_l[i])
                out_emit(dm, pbank[7][:], pbank[2][:])
            nv_l[i] = b.v("SV")

        if serial:
            for i in range(n):
                stats(i)
                chain(i)
                bcast_norm(i)
        else:
            for i in range(n):
                stats(i)
                if i >= 1:
                    bcast_norm(i - 1)
                chain(i)
            bcast_norm(n - 1)
        lnst["ss"] = b.v("SS")
        lnst["sv"] = b.v("SV")

    def emit_ffn_layer(l):
        lw = b.v("DW")
        wqb = A("wqb", 8 * 512, BF16)
        wkb = A("wkb", 8 * 512, BF16)
        wvb = A("wvb", 8 * 512, BF16)
        tbA = A("tbA", L); tbB = A("tbB", L)
        qt = [A(f"qt{i}", L, BF16) for i in range(4)]
        kt = [A(f"kt{i}", L, BF16) for i in range(4)]
        vt = [A(f"vt{i}", 8 * 65, BF16) for i in range(16)]
        xin = [A(f"xin{k}", 1 + TC, BF16) for k in range(NKM)]
        qtmp2 = [A("qtmp0", TC), A("qtmp1", TC)]
        sqsB = A("sqsB", TC, BF16)
        row2B = A("row2B", TC, rows=2)
        rq1 = A("rq1", TC); rq2 = A("rq2", TC)
        bar_all()
        qkv_tok = b.ldg("gpsimd", [
            (dst.rearrange("p (ki cc) -> p ki cc", ki=NKM),
             di[src_].rearrange("(ki p) cc -> p ki cc", p=128))
            for dst, src_ in ((wqb, f"wq_{l}"), (wkb, f"wk_{l}"), (wvb, f"wv_{l}"))])
        for dm in range(16):
            b.emit("vector", lambda e, dm=dm: e.memset(
                vt[dm][:].rearrange("p (h c) -> p h c", h=H)[:, :, 64:65], 1.0))
        vones_v = b.emit("vector", lambda e: e.memset(onesP[0:1, 0:1], 1.0), inc="SV")

        xsrc = x1bf_scr if l == 0 else x2bf_scr

        def qk_build(which, wslab, tgt, tb_src_a, tb_src_b, sc2, sc2r):
            b.wait("sync", "SV", b.v("SV"))
            b.wait("sync", "SS", b.v("SS"))
            tb_tok = b.ldg("sync", [(tbA[:], di[tb_src_a]),
                                    (tbB[:], di[tb_src_b])])
            b.wait_tok("scalar", tb_tok)
            b.emit("scalar", lambda e: e.activation(
                tbA[:], tbA[:], AF.Copy, scale=di_sc[sc2]))
            fold_s = b.emit("scalar", lambda e: e.activation(
                tbB[:], tbB[:], AF.Copy, scale=di_sc[sc2r]), inc="SS")
            items = [(fc, fh) for fc in range(NCH) for fh in range(4)]
            xin_toks = {}
            base = qkst["base"]
            mlast_d = qkst["mlast"]
            ev_d = qkst["ev"]
            sq_d = qkst["sq"]
            rbs_d = qkst["rbs"]
            ssmm_d = qkst["ssmm"]
            rbmm_d = qkst["rbmm"]
            qk3_d = qkst["qk3"]
            sqb = [sqs, sqsB]
            r2b = [row2, row2B]

            def emit_qkv(gi):
                i = gi - base
                fc, fh = items[i]
                ip = fh % 2
                bank = pbank[ip]
                if fh == 0:
                    b.wait("sync", "ST", b.v("ST"))
                    b.wait("sync", "DS", b.v("DS"))
                    xin_toks[fc] = b.ldg("sync", [
                        (xin[k][:],
                         xsrc[k * 128:(k + 1) * 128, fc * TC:fc * TC + 1 + TC])
                        for k in range(NKM)])
                b.wait_tok("tensor", qkv_tok)
                b.wait_tok("tensor", xin_toks[fc])
                g = (qk_ev.get((which, fc, fh - 2))
                     or qk_ev.get((which, fc - 1, fh + 2))
                     or (qk_ev.get(("q", NCH - 1, fh + 2), 0)
                         if which == "k" else 0))
                b.wait("tensor", "SV", g)
                if SERIAL & 1:
                    for _e in ENGS:
                        for _s in ("SV", "SS", "ST"):
                            b.wait(_e, _s, b.v(_s))
                mlast = None
                for ki in range(NKM):
                    wv_ = wslab[:, ki * 512 + fh * 128:ki * 512 + fh * 128 + 128]
                    rhs = xin[ki][:, 1:1 + TC] if ki < 4 else xin[ki][:, 0:TC]
                    mlast = b.emit("tensor", lambda e, wv_=wv_, rhs=rhs, ki=ki,
                                   bank=bank: e.matmul(
                        bank[:], wv_, rhs, start=(ki == 0), stop=(ki == NKM - 1)),
                        inc="ST" if ki == NKM - 1 else None)
                mlast_d[gi] = mlast
                b.wait("scalar", "ST", mlast)
                b.wait("scalar", "SV", qk3_d.get(gi - 2, 0))
                qt_ = qtmp2[ip]
                ev_d[gi] = b.emit("scalar", lambda e, qt_=qt_, bank=bank:
                                  e.activation(qt_[:], bank[:], AF.Copy),
                                  inc="SS")
                if l == DUMP_LAYER and which == "q" and "qraw" in dbg:
                    b.wait("sync", "SS", ev_d[gi])
                    dq = b.emit("sync", lambda e, fh=fh, fc=fc, qt_=qt_:
                                e.dma_start(
                        out=dbg["qraw"][fh * 128:(fh + 1) * 128,
                                        fc * TC:(fc + 1) * TC],
                        in_=qt_[:]), inc="DO", amt=16)
                    b.wait("scalar", "DO", dq)
                    b.wait("vector", "DO", dq)
                if gi >= 2:
                    b.wait("scalar", "ST", ssmm_d.get(gi - 2, 0))
                sq_d[gi] = b.emit("scalar", lambda e, qt_=qt_, ip=ip:
                                  e.activation(sqb[ip][:], qt_[:], AF.Square),
                                  inc="SS")

            def emit_chain(gi):
                i = gi - base
                fc, fh = items[i]
                ip = fh % 2
                t0, t1 = fc * TC, (fc + 1) * TC
                qt_ = qtmp2[ip]
                rotb = pbank[2 + 3 * ip]
                ssb = pbank[3 + 3 * ip]
                rbb = pbank[4 + 3 * ip]
                rw = r2b[ip]
                b.wait("tensor", "SS", ev_d[gi])
                b.wait("tensor", "SV", qk3_d.get(gi - 2, 0))
                b.emit("tensor", lambda e, rotb=rotb, qt_=qt_: e.matmul(
                    rotb[:], r2f[:], qt_[:], start=True, stop=True), inc="ST")
                b.wait("tensor", "SS", sq_d[gi])
                if gi >= 2:
                    b.wait("tensor", "SS", rbs_d.get(gi - 2, 0))
                ssmm_d[gi] = b.emit("tensor", lambda e, ssb=ssb, ip=ip: e.matmul(
                    ssb[0:2, :], bl2[:], sqb[ip][:], start=True, stop=True),
                    inc="ST")
                b.wait("scalar", "ST", ssmm_d[gi])
                if gi >= 2:
                    b.wait("scalar", "ST", rbmm_d.get(gi - 2, 0))
                b.emit("scalar", lambda e, rw=rw, ssb=ssb,
                       bv=b.const_col(1e-24): e.activation(
                    rw[:], ssb[0:2, :], AF.Ln, bias=bv))
                rbs_d[gi] = b.emit("scalar", lambda e, rw=rw: e.activation(
                    rw[:], rw[:], AF.Exp, scale=-0.5), inc="SS")
                b.wait("tensor", "SS", rbs_d[gi])
                rbmm_d[gi] = b.emit("tensor", lambda e, rbb=rbb, rw=rw: e.matmul(
                    rbb[:], headsel[:], rw[:], start=True, stop=True), inc="ST")
                b.wait("vector", "SS", fold_s)
                b.wait("vector", "ST", rbmm_d[gi])
                tga = tgt[fh]
                b.emit("vector", lambda e, qt_=qt_, t0=t0, t1=t1: e.tensor_tensor(
                    rq1[:], qt_[:], tbA[:, t0:t1], OP.mult))
                b.emit("vector", lambda e, rotb=rotb, t0=t0, t1=t1:
                       e.tensor_tensor(
                    rq2[:], rotb[:], tbB[:, t0:t1], OP.mult))
                b.emit("vector", lambda e: e.tensor_tensor(
                    rq1[:], rq1[:], rq2[:], OP.add))
                qk_ev[(which, fc, fh)] = b.emit(
                    "vector", lambda e, tga=tga, rbb=rbb, t0=t0, t1=t1:
                    e.tensor_tensor(
                        tga[:, t0:t1], rq1[:], rbb[:], OP.mult), inc="SV")
                qk3_d[gi] = qk_ev[(which, fc, fh)]

            for gi in range(base, base + len(items)):
                emit_qkv(gi)
                if gi > base:
                    emit_chain(gi - 1)
            emit_chain(base + len(items) - 1)
            qkst["base"] = base + len(items)

        qk_ev = {}
        qkst = {"base": 0, "mlast": {}, "ev": {}, "sq": {}, "rbs": {},
                "ssmm": {}, "rbmm": {}, "qk3": {}}
        qscol = b.sb(f"qscol{l}", [128, 4])
        sc_tok = b.ldg("sync", [
            (qscol[:, 0:1], di[f"qs2_{l}"]), (qscol[:, 1:2], di[f"qs2r_{l}"]),
            (qscol[:, 2:3], di[f"ks2_{l}"]), (qscol[:, 3:4], di[f"ks2r_{l}"])])
        di_sc = {f"qs2_{l}": qscol[:, 0:1], f"qs2r_{l}": qscol[:, 1:2],
                 f"ks2_{l}": qscol[:, 2:3], f"ks2r_{l}": qscol[:, 3:4]}
        b.wait_tok("scalar", sc_tok)
        qk_build("q", wqb, qt, "tab_csq", "tab_ssq", f"qs2_{l}", f"qs2r_{l}")
        if l == DUMP_LAYER and "qt" in dbg:
            for i4 in range(4):
                b.wait("gpsimd", "SV", b.v("SV"))
                b.emit("gpsimd", lambda e, i4=i4: e.dma_start(
                    out=dbg["qt"][i4 * 128:(i4 + 1) * 128, :], in_=qt[i4][:]),
                    inc="DO2", amt=16)
        qk_build("k", wkb, kt, "tab_csk", "tab_ssk", f"ks2_{l}", f"ks2r_{l}")
        if l == DUMP_LAYER and "kt" in dbg:
            for i4 in range(4):
                b.wait("gpsimd", "SV", b.v("SV"))
                b.emit("gpsimd", lambda e, i4=i4: e.dma_start(
                    out=dbg["kt"][i4 * 128:(i4 + 1) * 128, :], in_=kt[i4][:]),
                    inc="DO2", amt=16)
        v_ev = {}
        for fc in range(NCH):
            b.wait("sync", "ST", b.v("ST"))
            b.wait("sync", "DS", b.v("DS"))
            xin_tok = b.ldg("sync", [
                (xin[k][:],
                 xsrc[k * 128:(k + 1) * 128, fc * TC:fc * TC + 1 + TC])
                for k in range(NKM)])
            for jt in range(4):
                tix = fc * 4 + jt
                bank = pbank[tix % 2]
                b.wait_tok("tensor", xin_tok)
                b.wait("tensor", "SS", v_ev.get(tix - 2, 0))
                if tix < 2:
                    b.wait("tensor", "SV", qk_ev.get(("k", NCH - 1, 2 + tix), 0))
                mlast = None
                for ki in range(NKM):
                    lhs = (xin[ki][:, 1 + jt * 128:1 + (jt + 1) * 128] if ki < 4
                           else xin[ki][:, jt * 128:(jt + 1) * 128])
                    rhs = wvb[:, ki * 512:(ki + 1) * 512]
                    mlast = b.emit("tensor", lambda e, lhs=lhs, rhs=rhs, ki=ki,
                                   bank=bank: e.matmul(
                        bank[:], lhs, rhs, start=(ki == 0), stop=(ki == NKM - 1)),
                        inc="ST" if ki == NKM - 1 else None)
                b.wait("scalar", "ST", mlast)
                b.wait("scalar", "SV", vones_v)
                vtile = vt[tix]
                v_ev[tix] = b.emit("scalar", lambda e, vtile=vtile, bank=bank:
                                   e.activation(
                    vtile[:].rearrange("p (h c) -> p h c", h=H)[:, :, 0:64],
                    bank[:].rearrange("p (h c) -> p h c", h=H), AF.Copy),
                    inc="SS")
        bar_all(dma=False)
        b.free("wqb", "wkb", "wvb", "tbA", "tbB",
               *[f"xin{k}" for k in range(NKM)], "qtmp0", "qtmp1",
               "sqsB", "row2B", "rq1", "rq2")

        ptsl = [A(f"ptsl{j}", 1536, BF16) for j in range(16)]
        for j in range(16):
            b.emit("vector", lambda e, j=j: e.memset(ptsl[j][:], 0.0))
        b.emit("vector", lambda e: e.memset(onesP[0:1, 0:1], 1.0), inc="SV")
        otb = [A(f"ot{h}", L, BF16, rows=64) for h in range(H)]
        pij = [A(f"pij{p}", 640, BF16) for p in range(2)]
        oscr = A("oscr", TC)
        den = A("den", TC)
        nmAB = b.sb(f"nmAB{l}", [128, 4])
        bar_all(dma=False)
        exp_ms = {}
        red_ms = {}
        m1_ms = {}
        tr_ms = {}
        ev_ms = {}
        avmm_ms = {}
        avml_ms = {}

        def geo(it):
            I0 = it * 128
            jlo = max(0, I0 - 512)
            return I0, jlo, I0 + 128 - jlo

        def sbanks(it):
            return (pbank[0], pbank[1]) if it % 2 == 0 else (pbank[2], pbank[3])

        def emit_sim(h, it):
            I0, jlo, w = geo(it)
            par = it % 2
            bl_, bh_ = sbanks(it)
            qsl = qt[h // 2]
            ksl = kt[h // 2]
            fhrow = (h % 2) * 64
            if it >= 2:
                b.wait("tensor", "SS", exp_ms[(h, it - 2)])
            elif h > 0:
                b.wait("tensor", "SS", exp_ms[(h - 1, 14 + par)])
            lhsT = qsl[fhrow:fhrow + 64, I0:I0 + 128]
            m1 = b.emit("tensor", lambda e, lhsT=lhsT, ksl=ksl, jlo=jlo, w=w,
                        fhrow=fhrow, bl_=bl_: e.matmul(
                bl_[:, 0:min(w, 512)], lhsT,
                ksl[fhrow:fhrow + 64, jlo:jlo + min(w, 512)],
                start=True, stop=True), inc="ST")
            if w > 512:
                m1 = b.emit("tensor", lambda e, lhsT=lhsT, ksl=ksl, jlo=jlo, w=w,
                            fhrow=fhrow, bh_=bh_: e.matmul(
                    bh_[:, 0:w - 512], lhsT,
                    ksl[fhrow:fhrow + 64, jlo + 512:jlo + w],
                    start=True, stop=True), inc="ST")
            m1_ms[(h, it)] = m1

        def emit_rowmax(h, it):
            I0, jlo, w = geo(it)
            par = it % 2
            bl_, bh_ = sbanks(it)
            nmA = nmAB[:, 2 * par:2 * par + 1]
            nmB = nmAB[:, 2 * par + 1:2 * par + 2]
            b.wait("vector", "ST", m1_ms[(h, it)])
            if I0 >= 512:
                b.emit("vector", lambda e, bl_=bl_: e.tensor_tensor(
                    bl_[:, 0:128], bl_[:, 0:128], mfij[:], OP.add))
            if w > 512:
                b.emit("vector", lambda e, bh_=bh_, w=w: e.tensor_tensor(
                    bh_[:, w - 512 - 128:w - 512], bh_[:, w - 512 - 128:w - 512],
                    mnij[:], OP.add))
                b.emit("vector", lambda e, bl_=bl_, nmA=nmA: e.tensor_reduce(
                    nmA, bl_[:, 0:512], AX.X, OP.max, negate=True))
                b.emit("vector", lambda e, bh_=bh_, w=w, nmB=nmB: e.tensor_reduce(
                    nmB, bh_[:, 0:w - 512], AX.X, OP.max, negate=True))
                red = b.emit("vector", lambda e, nmA=nmA, nmB=nmB: e.tensor_tensor(
                    nmA, nmA, nmB, OP.min), inc="SV")
            else:
                b.emit("vector", lambda e, bl_=bl_, w=w: e.tensor_tensor(
                    bl_[:, w - 128:w], bl_[:, w - 128:w], mnij[:], OP.add))
                red = b.emit("vector", lambda e, bl_=bl_, w=w, nmA=nmA:
                             e.tensor_reduce(
                    nmA, bl_[:, 0:w], AX.X, OP.max, negate=True), inc="SV")
            red_ms[(h, it)] = red

        def emit_exp(h, it):
            I0, jlo, w = geo(it)
            par = it % 2
            bl_, bh_ = sbanks(it)
            nmA = nmAB[:, 2 * par:2 * par + 1]
            pj = pij[par]
            b.wait("scalar", "SV", red_ms[(h, it)])
            if it >= 2:
                b.wait("scalar", "ST", tr_ms[(h, it - 2)])
            elif h > 0:
                b.wait("scalar", "ST", tr_ms[(h - 1, 14 + par)])
            if w > 512:
                b.emit("scalar", lambda e, pj=pj, bh_=bh_, w=w, nmA=nmA:
                       e.activation(pj[:, 512:w], bh_[:, 0:w - 512], AF.Exp,
                                    bias=nmA), inc="SS")
            ex = b.emit("scalar", lambda e, pj=pj, bl_=bl_, w=w, nmA=nmA:
                        e.activation(pj[:, 0:min(w, 512)], bl_[:, 0:min(w, 512)],
                                     AF.Exp, bias=nmA), inc="SS")
            exp_ms[(h, it)] = ex

        def emit_trans(h, it):
            I0, jlo, w = geo(it)
            par = it % 2
            tb = pbank[4] if par == 0 else pbank[6]
            pj = pij[par]
            b.wait("tensor", "SS", exp_ms[(h, it)])
            if it >= 2:
                b.wait("tensor", "SS", ev_ms[(h, it - 2)])
            elif h > 0:
                b.wait("tensor", "SS", ev_ms[(h - 1, 14 + par)])
            ntile = w // 128
            tr = None
            for k in range(ntile):
                dst = (tb[:, k * 128:(k + 1) * 128] if k < 4
                       else pbank[5][:, par * 128:par * 128 + 128])
                tr = b.emit("tensor", lambda e, dst=dst, pj=pj, k=k: e.matmul(
                    dst, pj[:, k * 128:(k + 1) * 128], i128bf[:],
                    start=True, stop=True),
                    inc="ST" if k == ntile - 1 else None)
            tr_ms[(h, it)] = tr

        def emit_evac(h, it):
            I0, jlo, w = geo(it)
            par = it % 2
            tb = pbank[4] if par == 0 else pbank[6]
            b.wait("scalar", "ST", tr_ms[(h, it)])
            if h > 0 and it in (0, 4, 8):
                b.wait("scalar", "ST", avmm_ms[(h - 1, 1 + it // 4)])
            ntile = w // 128
            ev = None
            for k in range(ntile):
                J = jlo // 128 + k
                src = (tb[:, k * 128:(k + 1) * 128] if k < 4
                       else pbank[5][:, par * 128:par * 128 + 128])
                c = 512 + I0 - J * 128
                ptile = ptsl[J]
                ev = b.emit("scalar", lambda e, ptile=ptile, src=src, c=c:
                            e.activation(ptile[:, c:c + 128], src, AF.Copy),
                            inc="SS" if k == ntile - 1 else None)
            ev_ms[(h, it)] = ev

        avev_ms = {}

        def emit_av_mm(h, ib, wtok):
            IB0 = ib * 512
            Jlo = max(0, (IB0 - 512) // 128)
            Jhi = (IB0 + 511) // 128
            b7 = pbank[7]
            prev = avml_ms.get((h, ib - 1), avml_ms.get((h - 1, 3), 0))
            b.wait("tensor", "SS", wtok)
            b.wait("tensor", "SV", prev)
            segs2 = list(range(Jlo, Jhi + 1))
            mlast = None
            for idx, J in enumerate(segs2):
                J0 = J * 128
                vtile = vt[J]
                ptile = ptsl[J]
                off = 512 + IB0 - J0
                mlast = b.emit("tensor", lambda e, vtile=vtile, ptile=ptile,
                               h=h, off=off, b7=b7, fj=(idx == 0),
                               lj=(idx == len(segs2) - 1): e.matmul(
                    b7[0:65, :], vtile[:, h * 65:h * 65 + 65],
                    ptile[:, off:off + 512], start=fj, stop=lj),
                    inc="ST" if idx == len(segs2) - 1 else None)
            avmm_ms[(h, ib)] = mlast
            b.wait("scalar", "ST", mlast)
            b.wait("scalar", "SV", prev)
            b.emit("scalar", lambda e, b7=b7: e.activation(
                den[64:65, :], b7[64:65, :], AF.Ln))
            b.emit("scalar", lambda e: e.activation(
                oscr[64:65, :], den[64:65, :], AF.Exp, scale=-1.0), inc="SS")
            avev_ms[(h, ib)] = b.emit("scalar", lambda e, b7=b7: e.activation(
                oscr[0:64, :], b7[0:64, :], AF.Copy), inc="SS")

        def emit_av_dn(h, ib):
            IB0 = ib * 512
            b7 = pbank[7]
            b.wait("tensor", "SS", avev_ms[(h, ib)])
            dn_mm = b.emit("tensor", lambda e, b7=b7: e.matmul(
                b7[0:64, :], onesP[64:65, 0:64], oscr[64:65, :],
                start=True, stop=True), inc="ST")
            b.wait("vector", "ST", dn_mm)
            oth = otb[h]
            avml_ms[(h, ib)] = b.emit("vector", lambda e, oth=oth, IB0=IB0,
                                      b7=b7: e.tensor_tensor(
                oth[:, IB0:IB0 + 512], oscr[0:64, :], b7[0:64, :], OP.mult),
                inc="SV")

        for h in range(H):
            if SERIAL & 2:
                for _e in ENGS:
                    for _s in ("SV", "SS", "ST"):
                        b.wait(_e, _s, b.v(_s))
            for it in range(17):
                if it < 16:
                    emit_sim(h, it)
                    emit_rowmax(h, it)
                if it >= 1:
                    emit_trans(h, it - 1)
                    emit_evac(h, it - 1)
                if it < 16:
                    emit_exp(h, it)
                if it == 1 and h > 0:
                    emit_av_mm(h - 1, 2, ev_ms[(h, 0)])
                elif it == 2 and h > 0:
                    emit_av_dn(h - 1, 2)
                    emit_av_mm(h - 1, 3, ev_ms[(h, 1)])
                elif it == 3 and h > 0:
                    emit_av_dn(h - 1, 3)
                elif it == 10:
                    emit_av_mm(h, 0, ev_ms[(h, 8)])
                elif it == 11:
                    emit_av_dn(h, 0)
                elif it == 14:
                    emit_av_mm(h, 1, ev_ms[(h, 12)])
                elif it == 15:
                    emit_av_dn(h, 1)
        d1 = b.emit("scalar", lambda e: e.activation(
            onesP[0:1, 0:1], onesP[0:1, 0:1], AF.Copy), inc="SS")
        emit_av_mm(H - 1, 2, d1)
        emit_av_dn(H - 1, 2)
        d2 = b.emit("scalar", lambda e: e.activation(
            onesP[0:1, 0:1], onesP[0:1, 0:1], AF.Copy), inc="SS")
        emit_av_mm(H - 1, 3, d2)
        emit_av_dn(H - 1, 3)
        if l == DUMP_LAYER and "ot" in dbg:
            for h2 in range(H):
                b.wait("gpsimd", "SV", b.v("SV"))
                b.emit("gpsimd", lambda e, h2=h2: e.dma_start(
                    out=dbg["ot"][h2 * 64:(h2 + 1) * 64, :], in_=otb[h2][0:64, :]),
                    inc="DO2", amt=16)
        bar_all(dma=False)
        b.free(*[f"ptsl{j}" for j in range(16)], "oscr", "den", "pij0", "pij1",
               *[f"qt{i}" for i in range(4)], *[f"kt{i}" for i in range(4)],
               *[f"vt{i}" for i in range(16)])

        woB = A("woB", H * DM, BF16, rows=64)
        hlnb = [A(f"hlnb{dm}", L, BF16) for dm in range(NKM)]
        fs1 = A("fs1", TC)
        x2full = [A(f"x2f{dm}", 1 + L) for dm in range(NKM)]
        wo_tok = b.ldg("gpsimd", [(
            woB.rearrange("p (hh cc) -> p hh cc", hh=H),
            di[f"wo_s_{l}"].rearrange("p (hh cc) -> p hh cc", hh=H))])
        lng = b.sb(f"lng{l}", [128, NKM]); lnb = b.sb(f"lnb{l}", [128, NKM])
        lnp_tok = b.ldg("sync", [(lng[:], di[f"lng_c_{l}"]),
                                 (lnb[:], di[f"lnb_c_{l}"])])
        bar_all(dma=False)
        b.wait("sync", "DS", b.v("DS"))
        x2f_tok = b.ldg("sync", [(
            x2full[dm][:], x2_scr[dm * 128:(dm + 1) * 128, 0:1 + L])
            for dm in range(NKM)])
        res_ms = {}
        for dm in range(NKM):
            bb = (dm % 2) * 4
            b.wait_tok("tensor", wo_tok)
            for fc in range(NCH):
                b.wait("tensor", "SV", res_ms.get((dm - 2, fc), 0))
            mlast = [None] * NCH
            for h in range(H):
                oth = otb[h]
                wv_ = woB[0:64, h * DM + dm * 128:h * DM + (dm + 1) * 128]
                for fc in range(NCH):
                    mlast[fc] = b.emit("tensor", lambda e, wv_=wv_, oth=oth,
                                       fc=fc, bb=bb, h=h: e.matmul(
                        pbank[bb + fc][:], wv_, oth[:, fc * TC:(fc + 1) * TC],
                        start=(h == 0), stop=(h == H - 1)),
                        inc="ST" if h == H - 1 else None)
            xv = x2full[dm]
            for fc in range(NCH):
                b.wait("vector", "ST", mlast[fc])
                if dm == 0 and fc == 0:
                    b.wait_tok("vector", x2f_tok)
                res_ms[(dm, fc)] = b.emit("vector", lambda e, xv=xv, fc=fc,
                                          bb=bb: e.tensor_tensor(
                    xv[:, 1 + fc * TC:1 + (fc + 1) * TC],
                    xv[:, 1 + fc * TC:1 + (fc + 1) * TC],
                    pbank[bb + fc][:], OP.add), inc="SV")
        groups = []
        for fc in range(NCH):
            t0 = 1 + fc * TC
            prew = []
            for dm in range(NKM):
                prew.append(("tensor", "SV", res_ms[(dm, fc)]))
                prew.append(("scalar", "SV", res_ms[(dm, fc)]))
            svs = [x2full[dm][:, t0:t0 + TC] if dm < 4
                   else x2full[dm][:, t0 - 1:t0 - 1 + TC]
                   for dm in range(NKM)]

            def emit_hln(dm, mu_b, rs_b, svs=svs, fc=fc):
                b.emit("vector", lambda e, dm=dm: e.tensor_tensor(
                    fs1[:], svs[dm], mu_b, OP.subtract))
                nv = b.emit("vector", lambda e, dm=dm: e.tensor_tensor(
                    fs1[:], fs1[:], rs_b, OP.mult), inc="SV")
                b.wait_tok("vector", lnp_tok)
                hv = b.emit("vector", lambda e, dm=dm, fc=fc: e.tensor_scalar(
                    hlnb[dm][:, fc * TC:(fc + 1) * TC], fs1[:],
                    lng[:, dm:dm + 1], lnb[:, dm:dm + 1], OP.mult, OP.add),
                    inc="SV")
            groups.append((svs, emit_hln, prew))
        ln_pipeline(groups)
        b.wait("sync", "SV", b.v("SV"))
        x2st_tok = b.ldg("sync", [(
            x2_scr[dm * 128:(dm + 1) * 128, 1:1 + L], x2full[dm][:, 1:1 + L])
            for dm in range(NKM)])
        b.wait_tok("sync", x2st_tok)
        bar_all(dma=False)
        b.free(*[f"ot{h}" for h in range(H)], "woB", "fs1",
               *[f"x2f{dm}" for dm in range(NKM)])

        w1b = [A(f"w1b{p}", 8 * 256, BF16) for p in range(2)]
        hmid = [A(f"hmid{j}", L, BF16) for j in range(NPAIR)]
        gact = [A(f"gact{p}", TC) for p in range(2)]
        b1c = b.sb(f"b1c{l}s", [128, 2 * NPAIR])
        b1_tok = b.ldg("sync", [(b1c[:], di[f"b1c_{l}"])])
        b.wait_tok("scalar", b1_tok)
        bar_all(dma=False)
        w1_dg = [None] * NPAIR
        w1_last = [None] * NPAIR

        def load_w1(j):
            if j >= NPAIR:
                return
            if j >= 2:
                b.wait("gpsimd", "ST", w1_last[j - 2])
            w1_dg[j] = b.ldg("gpsimd", [(
                w1b[j % 2].rearrange("p (ki cc) -> p ki cc", ki=NKM),
                di[f"w1_s_{l}"][j * DM:(j + 1) * DM, :]
                .rearrange("(ki p) cc -> p ki cc", p=128))])
        load_w1(0)
        load_w1(1)
        gg_ms = {}
        gs_ms = {}
        for j in range(NPAIR):
            wslab = w1b[j % 2]
            b.wait_tok("tensor", w1_dg[j])
            ml_a = [None] * NCH
            ml_g = [None] * NCH
            for fc in range(NCH):
                b.wait("tensor", "SV", gg_ms.get((j - 1, fc), 0))
                b.wait("tensor", "SS", gs_ms.get((j - 1, fc), 0))
                for ki in range(NKM):
                    wa = wslab[:, ki * 256:ki * 256 + 128]
                    hk = hlnb[ki]
                    ml_a[fc] = b.emit("tensor", lambda e, wa=wa, hk=hk, fc=fc,
                                      ki=ki: e.matmul(
                        pbank[fc][:], wa, hk[:, fc * TC:(fc + 1) * TC],
                        start=(ki == 0), stop=(ki == NKM - 1)),
                        inc="ST" if ki == NKM - 1 else None)
                for ki in range(NKM):
                    wg = wslab[:, ki * 256 + 128:ki * 256 + 256]
                    hk = hlnb[ki]
                    ml_g[fc] = b.emit("tensor", lambda e, wg=wg, hk=hk, fc=fc,
                                      ki=ki: e.matmul(
                        pbank[4 + fc][:], wg, hk[:, fc * TC:(fc + 1) * TC],
                        start=(ki == 0), stop=(ki == NKM - 1)),
                        inc="ST" if ki == NKM - 1 else None)
                b.wait("scalar", "ST", ml_g[fc])
                if fc >= 2 or j > 0:
                    b.wait("scalar", "SV",
                           gg_ms.get((j, fc - 2), gg_ms.get((j - 1, NCH - 2 + fc), 0)))
                gc = gact[fc % 2]
                g_s = b.emit("scalar", lambda e, j=j, fc=fc, gc=gc: e.activation(
                    gc[:], pbank[4 + fc][:], AF.Gelu,
                    bias=b1c[:, NPAIR + j:NPAIR + j + 1]), inc="SS")
                gs_ms[(j, fc)] = g_s
                b.wait("vector", "ST", ml_a[fc])
                b.wait("vector", "SS", g_s)
                hm = hmid[j]
                gg_ms[(j, fc)] = b.emit("vector", lambda e, hm=hm, j=j, fc=fc,
                                        gc=gc: e.scalar_tensor_tensor(
                    hm[:, fc * TC:(fc + 1) * TC], pbank[fc][:], b1c[:, j:j + 1],
                    gc[:], OP.add, OP.mult), inc="SV")
            w1_last[j] = ml_g[NCH - 1]
            load_w1(j + 2)
        if l == DUMP_LAYER and "hmid" in dbg:
            for j2 in range(NPAIR):
                b.wait("gpsimd", "SV", b.v("SV"))
                b.emit("gpsimd", lambda e, j2=j2: e.dma_start(
                    out=dbg["hmid"][j2 * 128:(j2 + 1) * 128, :], in_=hmid[j2][:]),
                    inc="DO2", amt=16)
        bar_all(dma=False)
        b.free("w1b0", "w1b1", *[f"hlnb{dm}" for dm in range(NKM)],
               "gact0", "gact1")
        w2bA = A("w2bA", 11 * DM, BF16)
        w2bB = A("w2bB", 11 * DM, BF16)
        w2_tok = b.ldg("gpsimd", [
            (w2bA.rearrange("p (ki cc) -> p ki cc", ki=11),
             di[f"w2_s_{l}"][0:11 * 128, :].rearrange("(ki p) cc -> p ki cc", p=128)),
            (w2bB.rearrange("p (ki cc) -> p ki cc", ki=11),
             di[f"w2_s_{l}"][11 * 128:, :].rearrange("(ki p) cc -> p ki cc",
                                                     p=128))])
        w2b_tok = w2_tok

        x2row = [A(f"x2row{p}", L) for p in range(2)]
        x2bfrow = A("x2bfrow", L, BF16)
        b2c = b.sb(f"b2c{l}s", [128, NKM])
        b2_tok = b.ldg("sync", [(b2c[:], di[f"b2c_{l}"])])
        bar_all(dma=False)
        b.wait_tok("vector", b2_tok)
        b.wait("sync", "DS", b.v("DS"))
        r2_ms = {}
        x2r_tok = [None] * NKM
        st_toks = {}
        x2bf_cv = {}
        x2r_tok[0] = b.ldg("sync", [(
            x2row[0][:], x2_scr[0:128, 1:1 + L])])
        if NKM > 1:
            x2r_tok[1] = b.ldg("sync", [(
                x2row[1][:], x2_scr[128:256, 1:1 + L])])
        for dm in range(NKM):
            bb = (dm % 2) * 4
            b.wait_tok("tensor", w2_tok)
            b.wait_tok("tensor", w2b_tok)
            for fc in range(NCH):
                b.wait("tensor", "SV", r2_ms.get((dm - 2, fc), 0))
            mlast = [None] * NCH
            for ki in range(NPAIR):
                wsl2 = w2bA if ki < 11 else w2bB
                kio = ki if ki < 11 else ki - 11
                wv_ = wsl2[:, kio * DM + dm * 128:kio * DM + (dm + 1) * 128]
                hm = hmid[ki]
                for fc in range(NCH):
                    mlast[fc] = b.emit("tensor", lambda e, wv_=wv_, hm=hm, fc=fc,
                                       ki=ki, bb=bb: e.matmul(
                        pbank[bb + fc][:], wv_, hm[:, fc * TC:(fc + 1) * TC],
                        start=(ki == 0), stop=(ki == NPAIR - 1)),
                        inc="ST" if ki == NPAIR - 1 else None)
            xr = x2row[dm % 2]
            for fc in range(NCH):
                b.wait("vector", "ST", mlast[fc])
                if fc == 0:
                    b.wait_tok("vector", x2r_tok[dm])
                r2_ms[(dm, fc)] = b.emit("vector", lambda e, xr=xr, dm=dm,
                                         fc=fc, bb=bb: e.scalar_tensor_tensor(
                    xr[:, fc * TC:(fc + 1) * TC], pbank[bb + fc][:],
                    b2c[:, dm:dm + 1], xr[:, fc * TC:(fc + 1) * TC],
                    OP.add, OP.add), inc="SV")
            b.wait("sync", "SV", r2_ms[(dm, NCH - 1)])
            st_tok = b.ldg("sync", [(
                x2_scr[dm * 128:(dm + 1) * 128, 1:1 + L], xr[:])])
            st_toks[dm] = st_tok
            if l == 0:
                if dm >= 1:
                    b.wait("vector", "SV", x2bf_cv.get(dm - 1, 0))
                cv = b.emit("vector", lambda e, xr=xr: e.tensor_copy(
                    x2bfrow[:], xr[:]), inc="SV")
                x2bf_cv[dm] = cv
                b.wait("sync", "SV", cv)
                sd = b.ldg("sync", [(
                    x2bf_scr[dm * 128:(dm + 1) * 128, 1:1 + L], x2bfrow[:])])
                b.wait_tok("vector", sd)
            if dm + 2 < NKM:
                b.wait_tok("sync", st_tok)
                if l == 0:
                    b.wait("sync", "SV", cv)
                x2r_tok[dm + 2] = b.ldg("sync", [(
                    x2row[(dm + 2) % 2][:],
                    x2_scr[(dm + 2) * 128:(dm + 3) * 128, 1:1 + L])])
        bar_all()
        b.free("w2bA", "w2bB", "x2row0", "x2row1", "x2bfrow",
               *[f"hmid{j}" for j in range(NPAIR)])
        bar_all()

    for l in range(2):
        emit_ffn_layer(l)

    fin = [[A(f"fin{p}_{dm}", TC) for dm in range(NKM)] for p in range(2)]
    x2d = [[A(f"x2d{p}_{dm}", TC) for dm in range(NKM)] for p in range(2)]
    bar_all()
    b.wait("sync", "DO", b.v("DO"))
    fin_toks = [None] * NCH
    dv_toks = [[] for _ in range(NCH)]

    def fin_load(fc):
        par = fc % 2
        if fc >= 2:
            for tok in dv_toks[fc - 2]:
                b.wait_tok("sync", tok)
        fin_toks[fc] = b.ldg("sync", [
            p_ for dm in range(NKM) for p_ in (
                (fin[par][dm][:],
                 x1_scr[dm * 128:(dm + 1) * 128, fc * TC:(fc + 1) * TC]),
                (x2d[par][dm][:],
                 x2_scr[dm * 128:(dm + 1) * 128,
                        1 + fc * TC:1 + (fc + 1) * TC]))])
    fin_load(0)
    fin_load(1)
    groups = []
    for fc in range(NCH):
        par = fc % 2
        svs = [x2d[par][dm][:] for dm in range(NKM)]

        def pre_emit(fc=fc, par=par):
            for dm in range(NKM):
                b.wait_tok("vector", fin_toks[fc])
                xv = x2d[par][dm]
                am = b.emit("vector", lambda e, xv=xv, par=par, dm=dm:
                            e.tensor_tensor(
                    xv[:], xv[:], fin[par][dm][:], OP.add), inc="SV")
                b.wait("tensor", "SV", am)
                b.wait("scalar", "SV", am)

        def emit_out(dm, mu_b, rs_b, svs=svs, fc=fc, par=par):
            b.emit("vector", lambda e, dm=dm: e.tensor_tensor(
                fin[par][dm][:], svs[dm], mu_b, OP.subtract))
            nv = b.emit("vector", lambda e, dm=dm: e.tensor_tensor(
                fin[par][dm][:], fin[par][dm][:], rs_b, OP.mult), inc="SV")
            b.wait("sync", "SV", nv)
            dv = b.ldg("sync", [(
                out[dm * 128:(dm + 1) * 128, fc * TC:(fc + 1) * TC],
                fin[par][dm][:])])
            dv_toks[fc].append(dv)
            b.wait_tok("vector", dv)
            if dm == NKM - 1 and fc + 2 < NCH:
                fin_load(fc + 2)
        groups.append((svs, emit_out, pre_emit))
    ln_pipeline(groups, serial=LNSER)
    for s in b._pool["sync"]:
        b.wait("sync", s, b.v(s))
    b.wait("sync", "DS", b.v("DS"))
    b.wait("sync", "DO", b.v("DO"))

    b.finish()
    cvals_np = np.zeros((128, 8), np.float32)
    for val, i in getattr(b, "cmap", {}).items():
        cvals_np[:, i] = val
    return nc, cvals_np


def kernel(**inputs):
    g, xts = host_prep(inputs)
    nc, cvals_np = build_program(DUMPS)
    in_maps = []
    for c in range(8):
        m = dict(g)
        m["xT"] = xts[c]
        m["cvals"] = cvals_np
        in_maps.append(m)
    res = run_bass_kernel_spmd(nc, in_maps, core_ids=list(range(8)))
    outs = [np.asarray(r["out"]).T for r in res.results]
    return np.stack(outs).astype(np.float32)
